# version 12 (final)
# baseline (speedup 1.0000x reference)
import contextlib
import numpy as np
import ml_dtypes
import concourse.bass as bass
import concourse.mybir as mybir
from concourse.bass_utils import run_bass_kernel_spmd

F32 = mybir.dt.float32
BF16 = mybir.dt.bfloat16
AF = mybir.ActivationFunctionType
ALU = mybir.AluOpType
AX = mybir.AxisListType

B = 2
D = 2048
NEG = -1e30
EPS = 1e-6
LNEPS = 1e-5


class Prog:
    ENGINES = ("t", "v", "s", "g", "y")

    def __init__(self, nc):
        self.nc = nc
        self.ops = {e: [] for e in self.ENGINES}
        self.sems = {}
        self.cnt = {}
        self.hist = {}

    def sem(self, name):
        h = self.nc.semaphore(name).__enter__()
        self.sems[name] = h
        self.cnt[name] = 0

    SEQCST = True
    CONS_FULL = ("T", "V", "S", "LD", "ST", "G", "CC")
    CONS_LITE = ("LD", "ST", "G", "CC")
    CONS = CONS_FULL

    def __init2__(self):
        pass

    def op(self, eng, fn, waits=(), inc=None, inc_n=1, nodrain=False,
           drain_before=False):
        w = dict()
        for sname, val in waits:
            w[sname] = max(w.get(sname, 0), val)
        if self.SEQCST:
            for sname in self.CONS:
                w[sname] = max(w.get(sname, 0), self.cnt[sname])
        hist = self.hist.setdefault(eng, {})
        waits = []
        for k, v in w.items():
            if v > hist.get(k, 0):
                waits.append((k, v))
                hist[k] = v
        if inc is not None:
            self.cnt[inc] += inc_n
            if (eng, inc) in (("t", "T"), ("v", "V"), ("s", "S")):
                h2 = self.hist.setdefault(eng, {})
                h2[inc] = max(h2.get(inc, 0), self.cnt[inc])
        self.ops[eng].append((fn, tuple(waits), inc, inc_n, nodrain,
                              drain_before))

    def now(self, sem):
        return self.cnt[sem]

    def replay(self):
        nc = self.nc
        attr = {"t": "tensor", "v": "vector", "s": "scalar", "g": "gpsimd", "y": "sync"}
        with nc.Block() as blk:
            for ename in self.ENGINES:
                ops_local = self.ops[ename]

                def mk(ops_local, ename=ename):
                    def body(eng):
                        first = True
                        for fn, waits, inc, inc_n, nodrain, drain_before in ops_local:
                            if drain_before and not first:
                                eng.drain()
                            if ename == "v" and not first and not nodrain:
                                eng.drain()
                            first = False
                            for (sname, val) in waits:
                                if val > 0:
                                    eng.wait_ge(self.sems[sname], val)
                            inst = fn()
                            if inc is not None:
                                inst.then_inc(self.sems[inc], inc_n)
                    return body

                getattr(blk, attr[ename])(mk(ops_local))


def _consts():
    ident = np.eye(128, dtype=np.float32)
    ii = np.arange(128)[:, None]
    tt = np.arange(128)[None, :]
    triu = (ii <= tt).astype(np.float32)
    maskMA = np.concatenate([
        np.where(ii < tt, 0.0, NEG), np.where(ii <= tt, 0.0, NEG)],
        axis=1).astype(np.float32)
    maskT = np.where(tt < ii, 0.0, NEG).astype(np.float32)
    eye4 = np.zeros((128, 16), np.float32)
    for h in range(4):
        eye4[:, 4 * h + h] = 1.0
    return dict(c_ident=ident,
                c_ones_row=np.ones((1, 128), np.float32),
                c_ones_col=np.ones((128, 1), np.float32),
                c_triu=triu, c_maskMA=maskMA, c_maskT=maskT, c_eye4=eye4)


def make_in_maps(inputs, S_):
    NQ = S_ // 4
    RT = min(2048, NQ)
    hs = np.asarray(inputs["L0_attn_hidden_states"])[:, :S_, :]
    Wqkv = np.asarray(inputs["L0_attn_W_qkv"])
    Wz = np.asarray(inputs["L0_attn_W_z"])
    Wb = np.asarray(inputs["L0_attn_W_b"])
    Wa = np.asarray(inputs["L0_attn_W_a"])
    convw = np.asarray(inputs["L0_attn_conv_w"])[:, 0, :]
    A_log = np.asarray(inputs["L0_attn_A_log"])
    dtb = np.asarray(inputs["L0_attn_dt_bias"])
    norm_w = np.asarray(inputs["L0_attn_norm_w"])
    W_out = np.asarray(inputs["L0_attn_W_out"])
    sw_w = np.asarray(inputs["L0_ffn_switch_w"])
    sw_b = np.asarray(inputs["L0_ffn_switch_b"])
    w1 = np.asarray(inputs["L0_ffn_w1"]); b1 = np.asarray(inputs["L0_ffn_b1"])
    w2 = np.asarray(inputs["L0_ffn_w2"]); b2 = np.asarray(inputs["L0_ffn_b2"])
    w3 = np.asarray(inputs["L0_ffn_w3"]); b3 = np.asarray(inputs["L0_ffn_b3"])
    consts = _consts()

    maps = []
    for core in range(8):
        b = core // 4
        j = core % 4
        heads = list(range(4 * j, 4 * j + 4))
        wqkv_sl = np.concatenate(
            [Wqkv[:, 128 * h:128 * (h + 1)] for h in heads]
            + [Wqkv[:, 2048 + 128 * h:2048 + 128 * (h + 1)] for h in heads]
            + [Wqkv[:, 4096 + 128 * h:4096 + 128 * (h + 1)] for h in heads]
            + [Wz[:, 128 * h:128 * (h + 1)] for h in heads], axis=1)
        convc = np.concatenate(
            [convw[128 * h:128 * (h + 1), :] for h in heads]
            + [convw[2048 + 128 * h:2048 + 128 * (h + 1), :] for h in heads]
            + [convw[4096 + 128 * h:4096 + 128 * (h + 1), :] for h in heads], axis=1)
        e0, e1 = 2 * j, 2 * j + 1
        sw0 = sw_w[e0, :2048 * RT].reshape(RT, 2048).T
        sw1 = sw_w[e1, :2048 * RT].reshape(RT, 2048).T
        switch_fm = np.concatenate([sw0, sw1], axis=1)
        m0 = np.zeros((1, 8), np.float32); m0[0, e0] = 1.0
        m1 = np.zeros((1, 8), np.float32); m1[0, e1] = 1.0
        m = dict(
            h_own=hs[b, j * NQ:(j + 1) * NQ, :],
            wqkv_sl=wqkv_sl, wb_sl=Wb[:, heads], wa_sl=Wa[:, heads], convc=convc,
            alog_c=A_log[heads].reshape(4, 1), dtb_c=dtb[heads].reshape(4, 1),
            normw_c=norm_w.reshape(128, 1),
            W_out=W_out, w3T=w3.T, b3_c=b3.reshape(16, 128).T,
            w1T=np.transpose(w1, (2, 0, 1)).reshape(128, 1024),
            w2T=np.transpose(w2, (2, 0, 1)).reshape(128, 1024),
            b1T=b1.T, b2T=b2.T,
            swb_row=sw_b.reshape(1, 8), switch_fm=switch_fm,
            emask0=m0, emask1=m1,
            offtok=np.array([[j * NQ]], np.int32),
            **consts,
        )
        mm = {}
        for k, vv in m.items():
            vv = np.ascontiguousarray(vv)
            if k != "offtok":
                vv = vv.astype(np.float32)
            mm[k] = vv
        maps.append(mm)
    return maps


DEBUG = False
RELAX = 1
SEQDRV = False


def build(S_):
    NQ = S_ // 4
    NW = S_ // 512
    NWQ = NW // 4
    NW3 = NQ // 512
    RT = min(2048, NQ)
    RTW = RT // 512

    nc = bass.Bass("TRN2", target_bir_lowering=False)

    def din(name, shape, dt=F32):
        return nc.dram_tensor(name, list(shape), dt, kind="ExternalInput").ap()

    h_own = din("h_own", (NQ, 2048))
    wqkv_in = din("wqkv_sl", (2048, 2048))
    wb_in = din("wb_sl", (2048, 4))
    wa_in = din("wa_sl", (2048, 4))
    convc_in = din("convc", (128, 48))
    alog_in = din("alog_c", (4, 1))
    dtb_in = din("dtb_c", (4, 1))
    normw_in = din("normw_c", (128, 1))
    Wout_in = din("W_out", (2048, 2048))
    w3T_in = din("w3T", (2048, 2048))
    b3_in = din("b3_c", (128, 16))
    w1T_in = din("w1T", (128, 1024))
    w2T_in = din("w2T", (128, 1024))
    b1T_in = din("b1T", (128, 8))
    b2T_in = din("b2T", (128, 8))
    swb_in = din("swb_row", (1, 8))
    switch_in = din("switch_fm", (2048, 2 * RT))
    emask0_in = din("emask0", (1, 8))
    emask1_in = din("emask1", (1, 8))
    offtok_in = nc.dram_tensor("offtok", [1, 1], mybir.dt.int32, kind="ExternalInput").ap()
    ident_in = din("c_ident", (128, 128))
    onesr_in = din("c_ones_row", (1, 128))
    onesc_in = din("c_ones_col", (128, 1))
    triu_in = din("c_triu", (128, 128))
    maskMA_in = din("c_maskMA", (128, 256))
    maskT_in = din("c_maskT", (128, 128))
    eye4_in = din("c_eye4", (128, 16))

    out_t = nc.dram_tensor("out", [NQ, 2048], F32, kind="ExternalOutput").ap()

    ag1_in = nc.dram_tensor("ag1_in", [2048, NQ], BF16, kind="Internal").ap()
    ag1_out = nc.dram_tensor("ag1_out", [4 * 2048, NQ], BF16, kind="Internal").ap()
    ag2_in = nc.dram_tensor("ag2_in", [512, S_], BF16, kind="Internal").ap()
    ag2_out = nc.dram_tensor("ag2_out", [4 * 512, S_], BF16, kind="Internal").ap()
    agr_in = nc.dram_tensor("agr_in", [2048, RT], BF16, kind="Internal").ap()
    agr_out = nc.dram_tensor("agr_out", [4 * 2048, RT], BF16, kind="Internal").ap()
    lgt_in = nc.dram_tensor("lgt_in", [1, 8], F32, kind="Internal").ap()
    lgt_out = nc.dram_tensor("lgt_out", [1, 8], F32, kind="Internal").ap()
    hbf = nc.dram_tensor("hbf", [NQ, 2048], BF16, kind="Internal").ap()
    x1_sp = nc.dram_tensor("x1_sp", [2048, NQ], F32, kind="Internal").ap()
    ln2_sp = nc.dram_tensor("ln2_sp", [2048, NQ], BF16, kind="Internal").ap()

    es = contextlib.ExitStack()

    def sb(name, shape, dt=BF16):
        return es.enter_context(nc.sbuf_tensor("sb_" + name, list(shape), dt))

    ident_bf = sb("ident_bf", (128, 128))
    ident_f = sb("ident_f", (128, 128), F32)
    onesr_f = sb("onesr_f", (1, 128), F32)
    onesc_bf = sb("onesc_bf", (128, 1))
    onesc_f = sb("onesc_f", (128, 1), F32)
    triu_f = sb("triu_f", (128, 128), F32)
    maskMA_f = sb("maskMA_f", (128, 256), F32)
    maskT_f = sb("maskT_f", (128, 128), F32)
    eye4_bf = sb("eye4_bf", (128, 16))
    normw_c = sb("normw_c", (128, 1), F32)
    alog_c = sb("alog_c", (4, 1), F32)
    negexpA = sb("negexpA", (4, 1), F32)
    dtb_c = sb("dtb_c", (4, 1), F32)
    Wqkv_sb = sb("WBIG", (128, 16 * 2048))
    Wb_sb = sb("Wb_sb", (128, 16 * 4))
    Wa_sb = sb("Wa_sb", (128, 16 * 4))
    convc_sb = sb("convc_sb", (128, 48), F32)
    slot1 = sb("slot1", (128, 16384))
    slot2 = sb("slot2", (128, 16384))
    slot3 = sb("slot3", (128, 14848))
    slot4 = sb("slot4", (128, 8192))
    s1f = slot1[:, :].bitcast(F32)
    s2f = slot2[:, :].bitcast(F32)
    s3f = slot3[:, :].bitcast(F32)
    p0_in = s1f[:, 0:2048]
    p0_sq = s1f[:, 2048:4096]
    p0_bf = slot2[:, 0:2048]
    p0_hb = slot2[:, 2048:4096]
    p0_tr = slot2[:, 4096:6144]
    p0_st = sb("p0_st", (128, 6), F32)
    p0_m = sb("p0_m", (128, 1), F32)
    p0_r = sb("p0_r", (128, 1), F32)
    lw = [slot1[:, 0:8192], slot1[:, 8192:16384]]
    qk_buf = slot2[:, 0:4096]
    v_buf = slot2[:, 4096:6144]
    z_buf = slot2[:, 6144:8192]
    coreo = slot2[:, 8192:10240]
    mixed_sb = slot2[:, 10240:10756]
    tail_sb = slot2[:, 10756:10792]
    tanh_sb = slot2[:, 10792:11304]
    sq_tmp = slot2[:, 11304:11816]
    acc_f = s2f[:, 5908:6420]
    s4f = slot4[:, :].bitcast(F32)
    gpre_r = s4f[0:4, 0:512]
    bpre_r = s4f[0:4, 512:1024]
    ssq_r = s4f[0:4, 1024:1536]
    ssk_r = s4f[0:4, 1536:2048]
    g_r = s4f[0:4, 2048:2560]
    c1_r = s4f[0:4, 2560:3072]
    c2_r = s4f[0:4, 3072:3584]
    c3_r = s4f[0:4, 3584:4096]
    beta_r = s2f[0:4, 6420:6932]
    tmp_r = s2f[0:4, 6932:7444]
    tmp2_r = s2f[0:4, 7444:7956]
    mexp_sb = s3f[:, 0:256]
    EM_sb = slot3[:, 512:768]
    texp_sb = s3f[:, 512:640]
    Tpre_sb = slot3[:, 1280:1408]
    MA_2 = [slot3[:, 768:1024], slot2[:, 15400:15656]]
    T_2 = [slot3[:, 1408:1536], slot2[:, 15656:15784]]
    Qg_2 = [slot3[:, 1536:1664], slot2[:, 15784:15912]]
    Rbuf_2 = [slot3[:, 1664:1920], slot2[:, 10240:10496]]
    KTd_2 = [slot3[:, 1920:2048], slot2[:, 10496:10624]]
    Tp_2 = [slot3[:, 2048:3328], slot2[:, 13864:15144]]
    TTp_2 = [slot3[:, 3328:3456], slot2[:, 15144:15272]]
    ImM_2 = [slot3[:, 3456:3584], slot2[:, 15272:15400]]
    Ybuf = [slot3[:, 3584:3840], slot3[:, 3840:4096]]
    U_sb = slot3[:, 4096:4352]
    Isc_sb = slot3[:, 4352:4480]
    PhiT_sb = slot3[:, 4480:4608]
    QtT_sb = slot3[:, 4608:4736]
    S0_sb = slot3[:, 4736:5248]
    Ot_sb = slot3[:, 5248:5376]
    Osq_sb = slot3[:, 5376:5504]
    gat_sb = slot3[:, 5504:5632]
    cols_sb = s3f[:, 2816:2836]
    gcols_sb = s3f[:, 2836:2840]
    gendr_sb = s3f[0:1, 2868:2872]
    npsic = s3f[:, 2844:2848]
    phic = s3f[:, 2848:2852]
    expphic = s3f[:, 2852:2856]
    kdc = s3f[:, 2856:2860]
    expgend = s3f[:, 2860:2864]
    betac = s3f[:, 2864:2868]
    rows4 = s3f[0:1, 2880:4928]
    r4v = rows4.rearrange("p (a b t) -> p a b t", a=4, b=4)
    growb = s3f[0:1, 4928:5440]
    crowb = s3f[0:1, 5440:6976]
    msr_sb = s3f[0:1, 6976:7104]
    rstd_sb = s3f[0:1, 7104:7232]
    x1_w = s1f[:, :]
    core_w = slot2[:, 0:8192]
    w3in_w = slot2[:, 0:8192]
    hbf_w = slot2[:, 8192:16384]
    outb = s2f[:, 4096:6144]
    sw_sb = [slot2[:, 8192:10240], slot2[:, 10240:12288]]
    rtl_sb = slot2[:, 12288:14336]
    prod_sb = slot2[:, 14336:14848]
    attn_sb = slot3[:, 0:8192]
    asq_sb = slot3[:, 8192:8704]
    stat_bf = slot3[:, 8704:9728]
    ln2_w = slot4[:, 0:8192]
    murow = sb("murow", (1, 512), F32)
    varow = sb("varow", (1, 512), F32)
    r1row = sb("r1row", (1, 512), F32)
    tmpsr = sb("tmpsr", (1, 512), F32)
    w1T_sb = sb("w1T_sb", (128, 1024))
    w2T_sb = sb("w2T_sb", (128, 1024))
    b1T_sb = sb("b1T_sb", (128, 8), F32)
    b2T_sb = sb("b2T_sb", (128, 8), F32)
    b3_sb = sb("b3_sb", (128, 16), F32)
    swb_sb = sb("swb_sb", (1, 8), F32)
    emask_sb = sb("emask_sb", (1, 16), F32)
    lgrow = sb("lgrow", (1, 8), F32)
    lgsc = sb("lgsc", (1, 4), F32)
    prob_sb = sb("prob_sb", (1, 8), F32)
    probc_sb = sb("probc_sb", (8, 1), F32)
    cmp_sb = sb("cmp_sb", (8, 8), F32)
    selr_sb = sb("selr_sb", (1, 16), F32)
    selbc = sb("selbc", (128, 16), F32)
    gcol_sb = sb("gcol_sb", (128, 2), F32)
    w1sel = sb("w1sel", (128, 256))
    w2sel = sb("w2sel", (128, 256))
    b1sel = sb("b1sel", (128, 2), F32)
    b2sel = sb("b2sel", (128, 2), F32)
    b2c_sb = sb("b2c_sb", (128, 1), F32)
    h1_sb = sb("h1_sb", (128, 512))
    oacc = sb("oacc", (128, 512), F32)
    x2_sb = sb("x2_sb", (128, 512), F32)
    offr_sb = es.enter_context(nc.sbuf_tensor("sb_offr", [1, 1], mybir.dt.int32))
    dbgk = sb("dbgk", (1, 32), F32)
    scr_row = sb("scr_row", (1, 512), F32)
    h1b_sb = sb("h1b", (128, 512))

    pA = es.enter_context(nc.psum_tensor("pA", [128, 512], F32))
    pG_full = es.enter_context(nc.psum_tensor("pG", [128, 512], F32))
    pG = pG_full[:, 0:256]
    pBC = es.enter_context(nc.psum_tensor("pBC", [128, 512], F32))
    pSQ_full = es.enter_context(nc.psum_tensor("pSQ", [128, 512], F32))
    pSQ = pSQ_full[:, 0:256]
    pY_full = es.enter_context(nc.psum_tensor("pY", [128, 512], F32))
    pY = pY_full[:, 0:256]
    pOS_full = es.enter_context(nc.psum_tensor("pOS", [128, 512], F32))
    pOS = pOS_full[:, 0:256]
    pMISC = es.enter_context(nc.psum_tensor("pMISC", [128, 512], F32))
    pTR_full = es.enter_context(nc.psum_tensor("pTR", [128, 1024], BF16))
    pTR = pTR_full[:, 0:512]

    P = Prog(nc)
    for sname in ("LD", "ST", "T", "V", "S", "G", "GC", "CC"):
        P.sem(sname)

    def t(fn, waits=()):
        P.op("t", fn, waits, "T")

    def v(fn, waits=(), nodrain=False):
        P.op("v", fn, waits, "V", nodrain=nodrain)

    def s(fn, waits=(), drain=False):
        P.op("s", fn, waits, "S", drain_before=drain)

    def gd(fn, waits=()):
        P.op("g", fn, waits, "G", 16)

    def gc(fn, waits=()):
        P.op("g", fn, waits, "GC", 1)

    def cc(fn, waits=()):
        P.op("g", fn, waits, "CC", 1)

    def yd(fn, waits=()):
        P.op("y", fn, waits, "LD", 16)

    def ys(fn, waits=()):
        P.op("y", fn, waits, "ST", 16)

    def yw(fn, waits=()):
        P.op("y", fn, waits, None)

    NCV, NCS, NCT, NCG, NCY = nc.vector, nc.scalar, nc.tensor, nc.gpsimd, nc.sync

    gc(lambda: NCG.memset(S0_sb[:, :], 0.0))
    gc(lambda: NCG.memset(tail_sb[:, :], 0.0))
    gd(lambda: NCG.dma_start(ident_bf[:, :], ident_in))
    gd(lambda: NCG.dma_start(ident_f[:, :], ident_in))
    gd(lambda: NCG.dma_start(onesr_f[:, :], onesr_in))
    gd(lambda: NCG.dma_start(onesc_bf[:, :], onesc_in))
    gd(lambda: NCG.dma_start(onesc_f[:, :], onesc_in))
    gd(lambda: NCG.dma_start(triu_f[:, :], triu_in))
    gd(lambda: NCG.dma_start(maskMA_f[:, :], maskMA_in))
    gd(lambda: NCG.dma_start(maskT_f[:, :], maskT_in))
    gd(lambda: NCG.dma_start(eye4_bf[:, :], eye4_in))
    gd(lambda: NCG.dma_start(normw_c[:, :], normw_in))
    gd(lambda: NCG.dma_start(alog_c[:, :], alog_in))
    gd(lambda: NCG.dma_start(dtb_c[:, :], dtb_in))
    gd(lambda: NCG.dma_start(convc_sb[:, :], convc_in))
    gd(lambda: NCG.dma_start(Wqkv_sb[:, :].rearrange("p (k c) -> p k c", k=16),
                             wqkv_in.rearrange("(k p) c -> p k c", p=128)))
    gd(lambda: NCG.dma_start(Wb_sb[:, :].rearrange("p (k c) -> p k c", k=16),
                             wb_in.rearrange("(k p) c -> p k c", p=128)))
    gd(lambda: NCG.dma_start(Wa_sb[:, :].rearrange("p (k c) -> p k c", k=16),
                             wa_in.rearrange("(k p) c -> p k c", p=128)))
    G_PRE = P.now("G")
    s(lambda: NCS.activation(negexpA[:, :], alog_c[:, :], AF.Exp), waits=[("G", G_PRE), ("GC", 2)])
    SN = P.now("S")
    v(lambda: NCV.tensor_scalar(negexpA[:, :], negexpA[:, :], -1.0, None, ALU.mult),
      waits=[("S", SN)])
    V_PRE = P.now("V")

    NT0 = NQ // 128
    p0_tr_done = 0
    p0_st_done = 0
    for it in range(NT0):
        ldn = P.now("LD")
        prevV = P.now("V")
        prevS = P.now("S")
        prevST = P.now("ST")
        yd(lambda it=it: NCY.dma_start(p0_in[:, :], h_own[128 * it:128 * (it + 1), :]),
           waits=[("V", prevV), ("S", prevS), ("ST", prevST)])
        LD_T = ldn + 16
        v(lambda: NCV.tensor_reduce(p0_st[:, 0:1], p0_in[:, :], AX.X, ALU.add),
          waits=[("LD", LD_T)])
        s(lambda: NCS.activation(p0_sq[:, :], p0_in[:, :], AF.Square),
          waits=[("LD", LD_T)])
        SN = P.now("S")
        v(lambda: NCV.tensor_reduce(p0_st[:, 1:2], p0_sq[:, :], AX.X, ALU.add),
          waits=[("S", SN)])
        v(lambda: NCV.tensor_scalar(p0_m[:, :], p0_st[:, 0:1], 1.0 / 2048, None, ALU.mult))
        v(lambda: NCV.scalar_tensor_tensor(p0_st[:, 2:3], p0_m[:, :], -1.0, p0_m[:, :],
                                           ALU.mult, ALU.mult))
        v(lambda: NCV.scalar_tensor_tensor(p0_st[:, 3:4], p0_st[:, 1:2], 1.0 / 2048,
                                           p0_st[:, 2:3], ALU.mult, ALU.add))
        v(lambda: NCV.tensor_scalar(p0_st[:, 3:4], p0_st[:, 3:4], LNEPS, None, ALU.add))
        VN = P.now("V")
        s(lambda: NCS.activation(p0_st[:, 4:5], p0_st[:, 3:4], AF.Ln),
          waits=[("V", VN)])
        s(lambda: NCS.activation(p0_r[:, :], p0_st[:, 4:5], AF.Exp, scale=-0.5),
          drain=True)
        SN = P.now("S")
        v(lambda: NCV.scalar_tensor_tensor(p0_m[:, :], p0_m[:, :], -1.0, p0_r[:, :],
                                           ALU.mult, ALU.mult), waits=[("S", SN)])
        v(lambda: NCV.tensor_scalar(p0_bf[:, :], p0_in[:, :], p0_r[:, :], p0_m[:, :],
                                    ALU.mult, ALU.add), waits=[("T", p0_tr_done)])
        v(lambda: NCV.tensor_copy(p0_hb[:, :], p0_in[:, :]))
        VN = P.now("V")
        ys(lambda it=it: NCY.dma_start(hbf[128 * it:128 * (it + 1), :], p0_hb[:, :]),
           waits=[("V", VN)])
        if DEBUG and it == 0:
            dbg_p0bf = nc.dram_tensor("dbg_p0bf", [128, 2048], BF16,
                                      kind="ExternalOutput").ap()
            ys(lambda: NCY.dma_start(dbg_p0bf, p0_bf[:, :]), waits=[("V", VN)])
            dbg_st = nc.dram_tensor("dbg_st", [128, 8], F32,
                                    kind="ExternalOutput").ap()
            ys(lambda: NCY.dma_start(dbg_st[:, 0:6], p0_st[:, :]), waits=[("V", VN)])
            ys(lambda: NCY.dma_start(dbg_st[:, 6:7], p0_m[:, :], allow_slow_non_contiguous=True), waits=[("V", VN)])
            ys(lambda: NCY.dma_start(dbg_st[:, 7:8], p0_r[:, :], allow_slow_non_contiguous=True), waits=[("V", VN)])
        for dd in range(16):
            t(lambda dd=dd: NCT.transpose(pTR[:, 128 * (dd % 4):128 * (dd % 4) + 128],
                                          p0_bf[:, 128 * dd:128 * (dd + 1)],
                                          ident_bf[:, :]),
              waits=[("V", VN), ("G", G_PRE), ("GC", 2)] if dd == 0 else
                    ([("V", P.now("V"))] if dd % 4 == 0 else []))
            if dd % 4 == 3:
                TN = P.now("T")
                v(lambda dd=dd: NCV.tensor_copy(p0_tr[:, 128 * (dd - 3):128 * (dd + 1)],
                                                pTR[:, 0:512]),
                  waits=[("T", TN), ("ST", p0_st_done)] if dd == 3 else [("T", TN)])
        VN = P.now("V")
        ys(lambda it=it: NCY.dma_start(
            ag1_in.rearrange("(k p) t -> p k t", p=128)[:, :, 128 * it:128 * (it + 1)],
            p0_tr[:, :].rearrange("p (k t) -> p k t", k=16)),
           waits=[("V", VN)])
        if DEBUG and it == 0:
            dbg_p0tr = nc.dram_tensor("dbg_p0tr", [128, 2048], BF16,
                                      kind="ExternalOutput").ap()
            ys(lambda: NCY.dma_start(dbg_p0tr, p0_tr[:, :]), waits=[("V", VN)])
        p0_tr_done = P.now("T")
        p0_st_done = P.now("ST")

    STN = P.now("ST")
    cc(lambda: NCG.collective_compute(
        "AllGather", ALU.bypass, replica_groups=[[0, 1, 2, 3], [4, 5, 6, 7]],
        ins=[ag1_in.opt()], outs=[ag1_out.opt()]),
       waits=[("ST", STN)])
    CC_AG1 = P.now("CC")

    win_mm_done = {}

    def gen_part1(w, ch, ci, hh):
        sl = hh % 2
        MA_sb, T_sb, Qg_sb, Rbuf, KTd_sb = MA_2[sl], T_2[sl], Qg_2[sl], Rbuf_2[sl], KTd_2[sl]
        Tp_sb, TTp_sb, ImM_sb = Tp_2[sl], TTp_2[sl], ImM_2[sl]
        koff = hh * 1024 + ci * 128
        qoff = hh * 1024 + 512 + ci * 128
        SN0 = P.now("S")
        VN0 = P.now("V")
        t(lambda koff=koff: NCT.matmul(
            pG[:, 0:256], qk_buf[:, koff:koff + 128],
            bass.AP(slot2, koff, [[16384, 128], [512, 2], [1, 128]]),
            start=True, stop=True), waits=[("S", SN0), ("V", VN0)])
        t(lambda hh=hh: NCT.matmul(
            pBC[:, 0:256], onesr_f[:, :], r4v[:, 0:2, hh, :],
            start=True, stop=True))
        TN = P.now("T")
        yield
        v(lambda: NCV.tensor_tensor(mexp_sb[:, :], pBC[:, 0:256], maskMA_f[:, :],
                                    ALU.add), waits=[("T", TN), ("S", P.now("S"))],
          nodrain=True)
        VN = P.now("V")
        s(lambda hh=hh: NCS.activation(EM_sb[:, :], mexp_sb[:, :], AF.Exp,
                                       bias=npsic[:, hh:hh + 1]), waits=[("V", VN)])
        SN = P.now("S")
        yield
        v(lambda MA_sb=MA_sb: NCV.tensor_tensor(MA_sb[:, :], EM_sb[:, :], pG[:, 0:256],
                                                ALU.mult),
          waits=[("S", SN)], nodrain=True)
        V_MA = P.now("V")
        t(lambda hh=hh: NCT.matmul(
            pBC[:, 256:512], onesr_f[:, :], r4v[:, 2:4, hh, :],
            start=True, stop=True), waits=[("V", V_MA)])
        TN = P.now("T")
        yield
        v(lambda: NCV.tensor_tensor(texp_sb[:, :], pBC[:, 256:384], maskT_f[:, :],
                                    ALU.add), waits=[("T", TN), ("S", P.now("S"))],
          nodrain=True)
        VN = P.now("V")
        s(lambda hh=hh: NCS.activation(Tpre_sb[:, :], texp_sb[:, :], AF.Exp,
                                       bias=phic[:, hh:hh + 1]), waits=[("V", VN)])
        SN = P.now("S")
        yield
        v(lambda T_sb=T_sb: NCV.tensor_tensor(T_sb[:, :], Tpre_sb[:, :], pG[:, 0:128],
                                              ALU.mult), waits=[("S", SN)], nodrain=True)
        v(lambda qoff=qoff, Qg_sb=Qg_sb: NCV.tensor_tensor(
            Qg_sb[:, :], qk_buf[:, qoff:qoff + 128], pBC[:, 384:512], ALU.mult),
          nodrain=True)
        V_TQ = P.now("V")
        t(lambda koff=koff: NCT.transpose(pTR[:, 0:128], qk_buf[:, koff:koff + 128],
                                          ident_bf[:, :]), waits=[("V", V_TQ)])
        t(lambda hh=hh, ci=ci: NCT.transpose(
            pTR[:, 128:256], v_buf[:, hh * 512 + ci * 128:hh * 512 + ci * 128 + 128],
            ident_bf[:, :]))
        TN = P.now("T")
        yield
        v(lambda hh=hh, Rbuf=Rbuf: NCV.tensor_scalar(
            Rbuf[:, 0:128], pTR[:, 128:256], betac[:, hh:hh + 1], None, ALU.mult),
          waits=[("T", TN)], nodrain=True)
        v(lambda hh=hh, Rbuf=Rbuf: NCV.tensor_scalar(
            Rbuf[:, 128:256], pTR[:, 0:128], expphic[:, hh:hh + 1], None, ALU.mult),
          nodrain=True)
        v(lambda hh=hh, KTd_sb=KTd_sb: NCV.tensor_scalar(
            KTd_sb[:, :], pTR[:, 0:128], kdc[:, hh:hh + 1], None, ALU.mult),
          nodrain=True)
        yield
        sq_src = [(T_sb, 0, MA_sb, 0)]
        for lvl in range(5):
            pTt, pTo, pTTt, pTTo = sq_src[-1]
            t(lambda pTt=pTt, pTo=pTo, pTTt=pTTt, pTTo=pTTo: NCT.matmul(
                pSQ[:, 0:128], pTTt[:, pTTo:pTTo + 128], pTt[:, pTo:pTo + 128],
                start=True, stop=True),
              waits=[("V", P.now("V"))])
            t(lambda pTt=pTt, pTo=pTo, pTTt=pTTt, pTTo=pTTo: NCT.matmul(
                pSQ[:, 128:256], pTt[:, pTo:pTo + 128], pTTt[:, pTTo:pTTo + 128],
                start=True, stop=True))
            TN = P.now("T")
            v(lambda lvl=lvl, Tp_sb=Tp_sb: NCV.tensor_copy(
                Tp_sb[:, 256 * lvl:256 * (lvl + 1)], pSQ[:, 0:256]),
              waits=[("T", TN)], nodrain=True)
            sq_src.append((Tp_sb, 256 * lvl, Tp_sb, 256 * lvl + 128))
            yield
        VN = P.now("V")
        t(lambda Tp_sb=Tp_sb: NCT.matmul(pSQ[:, 128:256], Tp_sb[:, 1024:1152],
                                         Tp_sb[:, 1152:1280],
                                         start=True, stop=True), waits=[("V", VN)])
        TN = P.now("T")
        v(lambda TTp_sb=TTp_sb: NCV.tensor_copy(TTp_sb[:, 0:128], pSQ[:, 128:256]),
          waits=[("T", TN)], nodrain=True)
        v(lambda ImM_sb=ImM_sb, MA_sb=MA_sb: NCV.tensor_tensor(
            ImM_sb[:, :], ident_bf[:, :], MA_sb[:, 0:128], ALU.subtract),
          nodrain=True)

    def gen_part2(w, ch, ci, hh):
        sl = hh % 2
        MA_sb, Qg_sb, Rbuf, KTd_sb = MA_2[sl], Qg_2[sl], Rbuf_2[sl], KTd_2[sl]
        Tp_sb, TTp_sb, ImM_sb = Tp_2[sl], TTp_2[sl], ImM_2[sl]
        ycur = Rbuf
        for li, lvl in enumerate((5, 4, 3, 2, 1, 0)):
            stap = (TTp_sb[:, 0:128] if lvl == 5
                    else Tp_sb[:, 256 * lvl + 128:256 * lvl + 256])
            t(lambda stap=stap, ycur=ycur: NCT.matmul(
                pY[:, :], stap, ycur[:, 0:256],
                start=True, stop=False),
              waits=[("V", P.now("V"))])
            t(lambda ycur=ycur: NCT.matmul(pY[:, :], ident_bf[:, :], ycur[:, 0:256],
                                           start=False, stop=True))
            TN = P.now("T")
            ynext = Ybuf[li % 2]
            v(lambda ynext=ynext: NCV.tensor_copy(ynext[:, :], pY[:, :]),
              waits=[("T", TN)], nodrain=True)
            ycur = ynext
            yield
        VN = P.now("V")
        t(lambda ycur=ycur, ImM_sb=ImM_sb: NCT.matmul(
            pY[:, :], ImM_sb[:, :], ycur[:, 0:256],
            start=True, stop=True), waits=[("V", VN)])
        TN = P.now("T")
        v(lambda: NCV.tensor_copy(U_sb[:, :], pY[:, :]), waits=[("T", TN)],
          nodrain=True)
        V_U = P.now("V")
        yield
        t(lambda KTd_sb=KTd_sb: NCT.matmul(pY[:, 0:128], U_sb[:, 128:256], KTd_sb[:, :],
                                           start=True, stop=True), waits=[("V", V_U)])
        t(lambda MA_sb=MA_sb: NCT.matmul(pY[:, 128:256], U_sb[:, 128:256],
                                         MA_sb[:, 128:256], start=True, stop=True))
        TN = P.now("T")
        v(lambda hh=hh: NCV.tensor_scalar(Isc_sb[:, :], ident_bf[:, :],
                                          expgend[:, hh:hh + 1], None, ALU.mult),
          nodrain=True)
        v(lambda: NCV.tensor_tensor(PhiT_sb[:, :], Isc_sb[:, :], pY[:, 0:128],
                                    ALU.subtract), waits=[("T", TN)])
        v(lambda Qg_sb=Qg_sb: NCV.tensor_tensor(QtT_sb[:, :], Qg_sb[:, :],
                                                pY[:, 128:256], ALU.subtract),
          nodrain=True)
        V_C = P.now("V")
        yield
        t(lambda MA_sb=MA_sb: NCT.matmul(pOS[:, 0:128], U_sb[:, 0:128],
                                         MA_sb[:, 128:256], start=True, stop=False),
          waits=[("V", V_C)])
        t(lambda hh=hh: NCT.matmul(pOS[:, 0:128], S0_sb[:, 128 * hh:128 * (hh + 1)],
                                   QtT_sb[:, :], start=False, stop=True))
        t(lambda hh=hh: NCT.matmul(pOS[:, 128:256], PhiT_sb[:, :],
                                   S0_sb[:, 128 * hh:128 * (hh + 1)],
                                   start=True, stop=False))
        t(lambda KTd_sb=KTd_sb: NCT.matmul(pOS[:, 128:256], KTd_sb[:, :],
                                           U_sb[:, 0:128], start=False, stop=True))
        TN = P.now("T")
        yield
        v(lambda: NCV.tensor_copy(Ot_sb[:, :], pOS[:, 0:128]), waits=[("T", TN)],
          nodrain=True)
        v(lambda hh=hh: NCV.tensor_copy(S0_sb[:, 128 * hh:128 * (hh + 1)],
                                        pOS[:, 128:256]), nodrain=True)
        v(lambda: NCV.tensor_tensor(Osq_sb[:, :], Ot_sb[:, :], Ot_sb[:, :], ALU.mult))
        VN = P.now("V")
        yield
        t(lambda: NCT.matmul(pMISC[0:1, 256:384], onesc_bf[:, :], Osq_sb[:, :],
                             start=True, stop=True), waits=[("V", VN)])
        TN = P.now("T")
        v(lambda: NCV.tensor_scalar(msr_sb[:, :], pMISC[0:1, 256:384], 1.0 / 128,
                                    EPS, ALU.mult, ALU.add),
          waits=[("T", TN), ("S", P.now("S"))], nodrain=True)
        VNx = P.now("V")
        s(lambda: NCS.activation(msr_sb[:, :], msr_sb[:, :], AF.Ln),
          waits=[("V", VNx)])
        s(lambda: NCS.activation(scr_row[:, 0:128], msr_sb[:, :], AF.Exp, scale=-0.5),
          waits=[("V", P.now("V"))], drain=True)
        SNx = P.now("S")
        yield
        v(lambda: NCV.tensor_copy(rstd_sb[:, :], scr_row[:, 0:128]), waits=[("S", SNx)])
        VNr = P.now("V")
        t(lambda: NCT.matmul(pMISC[:, 384:512], onesr_f[:, :], rstd_sb[:, :],
                             start=True, stop=True), waits=[("V", VNr)])
        TN = P.now("T")
        yield
        v(lambda: NCV.tensor_tensor(gat_sb[:, :], Ot_sb[:, :], pMISC[:, 384:512],
                                    ALU.mult), waits=[("T", TN)], nodrain=True)
        v(lambda hh=hh, ci=ci: NCV.scalar_tensor_tensor(
            coreo[:, hh * 512 + ci * 128:hh * 512 + ci * 128 + 128],
            gat_sb[:, :], normw_c[:, :],
            z_buf[:, hh * 512 + ci * 128:hh * 512 + ci * 128 + 128],
            ALU.mult, ALU.mult))

    def _drive(gens):
        gens = [g for g in gens if g is not None]
        while gens:
            alive = []
            for g in gens:
                try:
                    next(g)
                    alive.append(g)
                except StopIteration:
                    pass
            gens = alive

    def emit_chunk(w, ch, ci):
        csl = slice(128 * ci, 128 * (ci + 1))
        gsl = slice(128 * ci, 128 * (ci + 1))
        VN = P.now("V")
        SN = P.now("S")
        for idx, rsrc in enumerate((g_r, c1_r, c2_r, beta_r, c3_r)):
            t(lambda idx=idx, rsrc=rsrc, gsl=gsl: NCT.transpose(
                pMISC[:, 4 * idx:4 * idx + 4], rsrc[0:4, gsl],
                ident_f[0:4, 0:4]),
              waits=[("V", VN), ("S", SN)] if idx == 0 else [])
        TN = P.now("T")
        v(lambda: NCV.tensor_copy(cols_sb[:, :], pMISC[:, 0:20]), waits=[("T", TN)], nodrain=True)
        VN = P.now("V")
        t(lambda: NCT.matmul(pMISC[:, 32:36], triu_f[:, :], cols_sb[:, 0:4],
                             start=True, stop=True), waits=[("V", VN)])
        t(lambda: NCT.matmul(pMISC[0:1, 36:40], onesc_f[:, :], cols_sb[:, 0:4],
                             start=True, stop=True))
        TN = P.now("T")
        v(lambda: NCV.tensor_copy(gcols_sb[:, :], pMISC[:, 32:36]), waits=[("T", TN)], nodrain=True)
        v(lambda: NCV.tensor_copy(gendr_sb[:, :], pMISC[0:1, 36:40]), nodrain=True)
        VN = P.now("V")
        for hh in range(4):
            t(lambda hh=hh: NCT.matmul(pBC[0:1, 128 * hh:128 * (hh + 1)],
                                       cols_sb[:, hh:hh + 1], triu_f[:, :],
                                       start=True, stop=True),
              waits=[("V", VN)] if hh == 0 else [])
        TN = P.now("T")
        v(lambda: NCV.tensor_copy(growb[:, :], pBC[0:1, 0:512]), waits=[("T", TN)], nodrain=True)
        for k_i, coff in ((0, 4), (1, 8), (2, 16)):
            VN = P.now("V")
            for hh in range(4):
                t(lambda coff=coff, hh=hh: NCT.matmul(
                    pBC[0:1, 128 * hh:128 * (hh + 1)],
                    cols_sb[:, coff + hh:coff + hh + 1], ident_f[:, :],
                    start=True, stop=True),
                  waits=[("V", VN)] if hh == 0 else [])
            TN = P.now("T")
            v(lambda k_i=k_i: NCV.tensor_copy(crowb[:, 512 * k_i:512 * (k_i + 1)],
                                              pBC[0:1, 0:512]), waits=[("T", TN)], nodrain=True)
        v(lambda: NCV.tensor_tensor(rows4[:, 0:512], growb[:, :], crowb[:, 0:512],
                                    ALU.add), waits=[("T", P.now("T")), ("S", P.now("S"))])
        v(lambda: NCV.tensor_tensor(rows4[:, 512:1024], growb[:, :],
                                    crowb[:, 1024:1536], ALU.add), nodrain=True)
        v(lambda: NCV.tensor_tensor(rows4[:, 1024:1536], crowb[:, 512:1024],
                                    growb[:, :], ALU.subtract), nodrain=True)
        VN = P.now("V")
        s(lambda: NCS.activation(scr_row[:, :], rows4[:, 512:1024], AF.Exp),
          waits=[("V", VN)])
        SNx = P.now("S")
        v(lambda: NCV.tensor_copy(rows4[:, 1536:2048], scr_row[:, :]),
          waits=[("S", SNx)])
        v(lambda: NCV.tensor_tensor(npsic[:, :], cols_sb[:, 8:12], gcols_sb[:, :],
                                    ALU.subtract), waits=[("S", P.now("S"))], nodrain=True)
        v(lambda: NCV.tensor_tensor(phic[:, :], gcols_sb[:, :], cols_sb[:, 4:8],
                                    ALU.add), nodrain=True)
        v(lambda: NCV.tensor_copy(betac[:, :], cols_sb[:, 12:16]), nodrain=True)
        VN = P.now("V")
        s(lambda: NCS.activation(expphic[:, :], phic[:, :], AF.Exp), waits=[("V", VN)])
        t(lambda: NCT.matmul(pMISC[:, 40:44], onesr_f[:, :], gendr_sb[:, :],
                             start=True, stop=True), waits=[("V", VN)])
        TN = P.now("T")
        v(lambda: NCV.tensor_tensor(kdc[:, :], pMISC[:, 40:44], npsic[:, :], ALU.add),
          waits=[("T", TN)])
        VN = P.now("V")
        s(lambda: NCS.activation(kdc[:, :], kdc[:, :], AF.Exp), waits=[("V", VN)])
        s(lambda: NCS.activation(expgend[:, :], pMISC[:, 40:44], AF.Exp))
        p1 = gen_part1(w, ch, ci, 0)
        for _ in p1:
            pass
        for hh in range(4):
            if SEQDRV:
                nxt = gen_part1(w, ch, ci, hh + 1) if hh < 3 else None
                for _ in gen_part2(w, ch, ci, hh):
                    pass
                if nxt is not None:
                    for _ in nxt:
                        pass
            else:
                nxt = gen_part1(w, ch, ci, hh + 1) if hh < 3 else None
                _drive([gen_part2(w, ch, ci, hh), nxt])

    def emit_window(w):
        r = w // NWQ
        tloc = 512 * (w % NWQ)
        buf = lw[w % 2]
        ldn = P.now("LD")
        lwait = [("CC", CC_AG1)]
        if w >= 2:
            lwait.append(("T", win_mm_done[w - 2]))
        yd(lambda r=r, tloc=tloc, buf=buf: NCY.dma_start(
            buf[:, :].rearrange("p (k t) -> p k t", k=16),
            bass.AP(ag1_out.tensor, 2048 * r * NQ + tloc, [[NQ, 128], [128 * NQ, 16], [1, 512]])),
           waits=lwait)
        LD_T = ldn + 16

        pA_free = 0
        for ci16 in range(16):
            for kk in range(16):
                wl = []
                if kk == 0:
                    wl = [("LD", LD_T), ("V", pA_free), ("G", G_PRE), ("GC", 2)]
                t(lambda ci16=ci16, kk=kk, buf=buf: NCT.matmul(
                    pA[:, :], Wqkv_sb[:, kk * 2048 + ci16 * 128:kk * 2048 + ci16 * 128 + 128],
                    buf[:, kk * 512:(kk + 1) * 512],
                    start=(kk == 0), stop=(kk == 15)), waits=wl)
            TN = P.now("T")
            hh = ci16 % 4
            if ci16 < 12:
                v(lambda: NCV.tensor_copy(mixed_sb[:, 3:515], pA[:, :]),
                  waits=[("T", TN)], nodrain=True)
                v(lambda ci16=ci16: NCV.tensor_copy(mixed_sb[:, 0:3],
                                                    tail_sb[:, 3 * ci16:3 * ci16 + 3]), nodrain=True)
                v(lambda ci16=ci16: NCV.tensor_copy(tail_sb[:, 3 * ci16:3 * ci16 + 3],
                                                    mixed_sb[:, 512:515]))
                v(lambda ci16=ci16: NCV.tensor_scalar(
                    acc_f[:, :], mixed_sb[:, 0:512],
                    convc_sb[:, 4 * ci16:4 * ci16 + 1], None, ALU.mult), nodrain=True)
                v(lambda ci16=ci16: NCV.tensor_scalar(
                    sq_tmp[:, :], mixed_sb[:, 1:513],
                    convc_sb[:, 4 * ci16 + 1:4 * ci16 + 2], None, ALU.mult),
                  nodrain=True)
                v(lambda ci16=ci16: NCV.scalar_tensor_tensor(
                    acc_f[:, :], mixed_sb[:, 2:514],
                    convc_sb[:, 4 * ci16 + 2:4 * ci16 + 3],
                    acc_f[:, :], ALU.mult, ALU.add))
                v(lambda ci16=ci16: NCV.scalar_tensor_tensor(
                    sq_tmp[:, :], mixed_sb[:, 3:515],
                    convc_sb[:, 4 * ci16 + 3:4 * ci16 + 4],
                    sq_tmp[:, :], ALU.mult, ALU.add), nodrain=True)
                v(lambda: NCV.tensor_tensor(acc_f[:, :], acc_f[:, :], sq_tmp[:, :],
                                            ALU.add))
                VN = P.now("V")
                s(lambda: NCS.activation(tanh_sb[:, :], acc_f[:, :], AF.Tanh,
                                         scale=0.5), waits=[("V", VN)])
                SN = P.now("S")
                v(lambda: NCV.tensor_scalar(tanh_sb[:, :], tanh_sb[:, :], 0.5, 0.5,
                                            ALU.mult, ALU.add), waits=[("S", SN)])
                if ci16 < 4:
                    dst, off = qk_buf, hh * 1024 + 512
                elif ci16 < 8:
                    dst, off = qk_buf, hh * 1024
                else:
                    dst, off = v_buf, hh * 512
                v(lambda dst=dst, off=off: NCV.tensor_tensor(
                    dst[:, off:off + 512], acc_f[:, :], tanh_sb[:, :], ALU.mult))
                pA_free = P.now("V")
                if ci16 < 8:
                    v(lambda dst=dst, off=off: NCV.tensor_tensor(
                        sq_tmp[:, :], dst[:, off:off + 512], dst[:, off:off + 512],
                        ALU.mult), waits=[("T", P.now("T"))])
                    VN = P.now("V")
                    t(lambda hh=hh: NCT.matmul(
                        pMISC[0:4, 0:512], eye4_bf[:, 4 * hh:4 * hh + 4], sq_tmp[:, :],
                        start=(hh == 0), stop=(hh == 3)), waits=[("V", VN)])
                    if hh == 3:
                        TN2 = P.now("T")
                        dstr = ssq_r if ci16 < 4 else ssk_r
                        v(lambda dstr=dstr: NCV.tensor_copy(
                            dstr[0:4, 0:512], pMISC[0:4, 0:512]),
                          waits=[("T", TN2)], nodrain=True)
            else:
                s(lambda: NCS.activation(tanh_sb[:, :], pA[:, :], AF.Tanh, scale=0.5),
                  waits=[("T", TN), ("V", P.now("V"))])
                SN = P.now("S")
                v(lambda: NCV.tensor_scalar(tanh_sb[:, :], tanh_sb[:, :], 0.5, 0.5,
                                            ALU.mult, ALU.add), waits=[("S", SN)])
                v(lambda hh=hh: NCV.tensor_tensor(z_buf[:, hh * 512:(hh + 1) * 512],
                                                  pA[:, :], tanh_sb[:, :], ALU.mult))
                pA_free = P.now("V")

        for Wsb, dstr in ((Wb_sb, bpre_r), (Wa_sb, gpre_r)):
            VN = P.now("V")
            for kk in range(16):
                t(lambda Wsb=Wsb, kk=kk, buf=buf: NCT.matmul(
                    pMISC[0:4, 0:512], Wsb[:, kk * 4:(kk + 1) * 4],
                    buf[:, kk * 512:(kk + 1) * 512],
                    start=(kk == 0), stop=(kk == 15)),
                  waits=[("V", VN)] if kk == 0 else [])
            TN = P.now("T")
            v(lambda dstr=dstr: NCV.tensor_copy(dstr[0:4, 0:512],
                                                     pMISC[0:4, 0:512]),
              waits=[("T", TN)], nodrain=True)
        win_mm_done[w] = P.now("T")

        wsl = slice(0, 512)
        VN = P.now("V")
        s(lambda wsl=wsl: NCS.activation(tmp_r[0:4, :], bpre_r[0:4, wsl], AF.Tanh,
                                         scale=0.5),
          waits=[("V", VN), ("T", P.now("T"))])
        SN = P.now("S")
        v(lambda wsl=wsl: NCV.tensor_scalar(beta_r[0:4, wsl], tmp_r[0:4, :], 0.5, 0.5,
                                            ALU.mult, ALU.add), waits=[("S", SN)])
        VN = P.now("V")
        s(lambda wsl=wsl: NCS.activation(tmp_r[0:4, :], beta_r[0:4, wsl], AF.Ln),
          waits=[("V", VN)])
        v(lambda wsl=wsl: NCV.tensor_scalar(tmp2_r[0:4, :], ssk_r[0:4, wsl], EPS,
                                            None, ALU.add))
        VNx = P.now("V")
        s(lambda: NCS.activation(tmp2_r[0:4, :], tmp2_r[0:4, :], AF.Ln),
          waits=[("V", VNx)])
        SN = P.now("S")
        v(lambda wsl=wsl: NCV.tensor_scalar(c2_r[0:4, wsl], tmp2_r[0:4, :], -0.5, None,
                                            ALU.mult), waits=[("S", SN)])
        v(lambda wsl=wsl: NCV.scalar_tensor_tensor(c1_r[0:4, wsl], tmp2_r[0:4, :],
                                                   -0.5, tmp_r[0:4, :], ALU.mult,
                                                   ALU.add))
        VN = P.now("V")
        v(lambda wsl=wsl: NCV.tensor_scalar(tmp2_r[0:4, :], ssq_r[0:4, wsl], EPS,
                                            None, ALU.add), waits=[("V", VN)])
        VNx = P.now("V")
        s(lambda: NCS.activation(tmp2_r[0:4, :], tmp2_r[0:4, :], AF.Ln),
          waits=[("V", VNx)])
        SN = P.now("S")
        v(lambda wsl=wsl: NCV.tensor_scalar(c3_r[0:4, wsl], tmp2_r[0:4, :], -0.5,
                                            -0.5 * float(np.log(128.0)), ALU.mult,
                                            ALU.add), waits=[("S", SN)])
        v(lambda wsl=wsl: NCV.tensor_scalar(tmp2_r[0:4, :], gpre_r[0:4, wsl],
                                            dtb_c[0:4, :], None, ALU.add))
        VN = P.now("V")
        s(lambda: NCS.activation(tmp_r[0:4, :], tmp2_r[0:4, :], AF.Abs),
          waits=[("V", VN)])
        s(lambda: NCS.activation(tmp_r[0:4, :], tmp_r[0:4, :], AF.Exp, scale=-1.0),
          drain=True)
        SNx = P.now("S")
        v(lambda: NCV.tensor_scalar(tmp_r[0:4, :], tmp_r[0:4, :], 1.0, None, ALU.add),
          waits=[("S", SNx)])
        VNx = P.now("V")
        s(lambda: NCS.activation(tmp_r[0:4, :], tmp_r[0:4, :], AF.Ln),
          waits=[("V", VNx)])
        SN = P.now("S")
        v(lambda: NCV.tensor_scalar(tmp2_r[0:4, :], tmp2_r[0:4, :], 0.0, None,
                                    ALU.max), waits=[("S", SN)])
        v(lambda: NCV.tensor_tensor(tmp_r[0:4, :], tmp_r[0:4, :], tmp2_r[0:4, :],
                                    ALU.add))
        v(lambda wsl=wsl: NCV.tensor_scalar(g_r[0:4, wsl], tmp_r[0:4, :],
                                            negexpA[0:4, :], None, ALU.mult),
          waits=[("V", V_PRE)])

        for ci in range(4):
            emit_chunk(w, 4 * w + ci, ci)

        VN = P.now("V")
        ys(lambda w=w: NCY.dma_start(
            ag2_in.rearrange("(k p) t -> p k t", p=128)[:, :, 512 * w:512 * (w + 1)],
            coreo[:, :].rearrange("p (k t) -> p k t", k=4)), waits=[("V", VN)])
        STN = P.now("ST")
        P.op("v", lambda: NCV.engine_nop(), [("ST", STN)], None)

    for w in range(NW):
        emit_window(w)

    STN = P.now("ST")
    cc(lambda: NCG.collective_compute(
        "AllGather", ALU.bypass, replica_groups=[[0, 1, 2, 3], [4, 5, 6, 7]],
        ins=[ag2_in.opt()], outs=[ag2_out.opt()]), waits=[("ST", STN)])
    CC_AG2 = P.now("CC")
    if RELAX < 2:
        P.CONS = P.CONS_FULL

    T_P2 = P.now("T")
    gd(lambda: NCG.dma_start(Wqkv_sb[:, :].rearrange("p (k c) -> p k c", k=16),
                             Wout_in.rearrange("(k p) c -> p k c", p=128)),
       waits=[("T", T_P2)])
    G_WOUT = P.now("G")
    gd(lambda: NCG.dma_start(w1T_sb[:, :], w1T_in))
    gd(lambda: NCG.dma_start(w2T_sb[:, :], w2T_in))
    gd(lambda: NCG.dma_start(b1T_sb[:, :], b1T_in))
    gd(lambda: NCG.dma_start(b2T_sb[:, :], b2T_in))
    gd(lambda: NCG.dma_start(swb_sb[:, :], swb_in))
    gd(lambda: NCG.dma_start(emask_sb[:, 0:8], emask0_in))
    gd(lambda: NCG.dma_start(emask_sb[:, 8:16], emask1_in))
    gd(lambda: NCG.dma_start(b3_sb[:, :], b3_in))
    G_W3X = P.now("G")
    ys(lambda: NCY.dma_start(offr_sb[:, :], offtok_in))
    OFF_LD = P.now("LD") + P.now("ST")
    ST_OFF = P.now("ST")
    hold = {}

    def _load_off():
        reg = NCY.alloc_register("offreg")
        NCY.reg_load(reg, offr_sb[0:1, 0:1])
        hold["off"] = NCY.snap(reg)
        return NCY.nop()
    P.op("y", _load_off, [("ST", ST_OFF)], None)

    def ln_pair_stats(src_getter, n_cc, dst_ps1, dst_ps2, first_waits):
        pass

    def emit_p3a_window(wi):
        ldn = P.now("LD")
        wl = [("CC", CC_AG2)]
        if wi > 0:
            wl.append(("ST", hold["p3a_st_%d" % (wi - 1)]))

        def ld_core(wi=wi):
            off = hold["off"]
            src = bass.AP(ag2_out, 0, [[S_, 128], [128 * S_, 16], [1, 512]],
                          ) if False else None
            view = ag2_out.rearrange("(k p) t -> p k t", p=128)
            return NCY.dma_start(core_w[:, :].rearrange("p (k t) -> p k t", k=16),
                                 view[:, :, bass.ds(off + 512 * wi, 512)])
        yd(ld_core, waits=wl)
        for dd in range(16):
            yd(lambda dd=dd, wi=wi: NCY.dma_start(
                hbf_w[:, 512 * dd:512 * (dd + 1)],
                hbf[512 * wi:512 * (wi + 1), 128 * dd:128 * (dd + 1)],
                transpose=True))
        LD_T = ldn + 16 * 17

        for ci16 in range(16):
            for kk in range(16):
                wl = []
                if kk == 0:
                    wl = [("LD", LD_T), ("G", G_WOUT),
                          ("V", hold.get("p3a_pa", 0))]
                t(lambda ci16=ci16, kk=kk: NCT.matmul(
                    pA[:, :],
                    Wqkv_sb[:, kk * 2048 + ci16 * 128:kk * 2048 + ci16 * 128 + 128],
                    core_w[:, 512 * kk:512 * (kk + 1)],
                    start=(kk == 0), stop=(kk == 15)), waits=wl)
            TN = P.now("T")
            evw = [("T", TN)]
            if ci16 == 0 and wi > 0:
                evw.append(("ST", hold["p3a_st_%d" % (wi - 1)]))
            v(lambda ci16=ci16: NCV.tensor_copy(attn_sb[:, 512 * ci16:512 * (ci16 + 1)],
                                                pA[:, :]), waits=evw, nodrain=True)
            VN = P.now("V")
            hold["p3a_pa"] = VN
            v(lambda ci16=ci16: NCV.tensor_tensor(
                asq_sb[:, :], attn_sb[:, 512 * ci16:512 * (ci16 + 1)],
                attn_sb[:, 512 * ci16:512 * (ci16 + 1)], ALU.mult),
              waits=[("T", P.now("T"))])
            VN2 = P.now("V")
            t(lambda ci16=ci16: NCT.matmul(
                pMISC[0:1, 0:512], onesc_bf[:, :], attn_sb[:, 512 * ci16:512 * (ci16 + 1)],
                start=(ci16 == 0), stop=(ci16 == 15)), waits=[("V", VN2)])
            t(lambda ci16=ci16: NCT.matmul(
                pBC[0:1, 0:512], onesc_bf[:, :], asq_sb[:, :],
                start=(ci16 == 0), stop=(ci16 == 15)))
        TN = P.now("T")

        def stats_rows(TN):
            v(lambda: NCV.tensor_scalar(murow[:, :], pMISC[0:1, 0:512], 1.0 / 2048,
                                        None, ALU.mult), waits=[("T", TN)])
            v(lambda: NCV.tensor_scalar(tmpsr[:, :], pBC[0:1, 0:512], 1.0 / 2048,
                                        None, ALU.mult))
            v(lambda: NCV.scalar_tensor_tensor(varow[:, :], murow[:, :], -1.0,
                                               murow[:, :], ALU.mult, ALU.mult))
            v(lambda: NCV.tensor_tensor(varow[:, :], tmpsr[:, :], varow[:, :],
                                        ALU.add))
            v(lambda: NCV.tensor_scalar(varow[:, :], varow[:, :], LNEPS, None,
                                        ALU.add))
            VN = P.now("V")
            s(lambda: NCS.activation(varow[:, :], varow[:, :], AF.Ln),
              waits=[("V", VN)])
            s(lambda: NCS.activation(scr_row[:, :], varow[:, :], AF.Exp, scale=-0.5),
              waits=[("V", P.now("V"))], drain=True)
            SNx = P.now("S")
            v(lambda: NCV.tensor_copy(r1row[:, :], scr_row[:, :]), waits=[("S", SNx)])
            SN = P.now("S")
            t(lambda: NCT.matmul(pMISC[:, 0:512], onesr_f[:, :], murow[:, :],
                                 start=True, stop=True), waits=[("S", SN)])
            t(lambda: NCT.matmul(pBC[:, 0:512], onesr_f[:, :], r1row[:, :],
                                 start=True, stop=True))
            return P.now("T")

        TN = stats_rows(TN)
        for ci16 in range(16):
            v(lambda ci16=ci16: NCV.tensor_tensor(
                x1_w[:, 512 * ci16:512 * (ci16 + 1)],
                attn_sb[:, 512 * ci16:512 * (ci16 + 1)], pMISC[:, 0:512],
                ALU.subtract), waits=[("T", TN)] if ci16 == 0 else [])
            v(lambda ci16=ci16: NCV.tensor_tensor(
                x1_w[:, 512 * ci16:512 * (ci16 + 1)],
                x1_w[:, 512 * ci16:512 * (ci16 + 1)], pBC[:, 0:512], ALU.mult))
            v(lambda ci16=ci16: NCV.tensor_tensor(
                x1_w[:, 512 * ci16:512 * (ci16 + 1)],
                x1_w[:, 512 * ci16:512 * (ci16 + 1)],
                hbf_w[:, 512 * ci16:512 * (ci16 + 1)], ALU.add))
        V_X1 = P.now("V")
        for ci16 in range(16):
            v(lambda ci16=ci16: NCV.tensor_copy(stat_bf[:, 0:512],
                                                x1_w[:, 512 * ci16:512 * (ci16 + 1)]),
              waits=[("T", P.now("T"))], nodrain=True)
            VN = P.now("V")
            v(lambda: NCV.tensor_tensor(stat_bf[:, 512:1024], stat_bf[:, 0:512],
                                        stat_bf[:, 0:512], ALU.mult))
            VN2 = P.now("V")
            t(lambda ci16=ci16: NCT.matmul(
                pMISC[0:1, 0:512], onesc_bf[:, :], stat_bf[:, 0:512],
                start=(ci16 == 0), stop=(ci16 == 15)), waits=[("V", VN2)])
            t(lambda ci16=ci16: NCT.matmul(
                pBC[0:1, 0:512], onesc_bf[:, :], stat_bf[:, 512:1024],
                start=(ci16 == 0), stop=(ci16 == 15)))
        TN = P.now("T")
        TN = stats_rows(TN)
        for ci16 in range(16):
            v(lambda ci16=ci16: NCV.tensor_tensor(
                ln2_w[:, 512 * ci16:512 * (ci16 + 1)],
                x1_w[:, 512 * ci16:512 * (ci16 + 1)], pMISC[:, 0:512], ALU.subtract),
              waits=[("T", TN)] if ci16 == 0 else [])
            v(lambda ci16=ci16: NCV.tensor_tensor(
                ln2_w[:, 512 * ci16:512 * (ci16 + 1)],
                ln2_w[:, 512 * ci16:512 * (ci16 + 1)], pBC[:, 0:512], ALU.mult))
        V_LN2 = P.now("V")
        ys(lambda wi=wi: NCY.dma_start(
            bass.AP(x1_sp.tensor, 512 * wi, [[NQ, 128], [128 * NQ, 16], [1, 512]]),
            x1_w[:, :].rearrange("p (k t) -> p k t", k=16)), waits=[("V", V_X1)])
        ys(lambda wi=wi: NCY.dma_start(
            bass.AP(ln2_sp.tensor, 512 * wi, [[NQ, 128], [128 * NQ, 16], [1, 512]]),
            ln2_w[:, :].rearrange("p (k t) -> p k t", k=16)), waits=[("V", V_LN2)])
        hold["p3a_st_%d" % wi] = P.now("ST")

    for wi in range(NW3):
        emit_p3a_window(wi)

    STN = P.now("ST")
    gd(lambda: NCG.dma_start(agr_in, ln2_sp[:, 0:RT]), waits=[("ST", STN)])
    GN = P.now("G")
    cc(lambda: NCG.collective_compute(
        "AllGather", ALU.bypass, replica_groups=[[0, 1, 2, 3], [4, 5, 6, 7]],
        ins=[agr_in.opt()], outs=[agr_out.opt()]), waits=[("G", GN)])
    CC_AGR = P.now("CC")

    for e in range(2):
        for cc16 in range(16):
            ldn = P.now("LD")
            swwl = [("V", P.now("V"))] if (e, cc16) >= (0, 2) else []
            if e == 0 and cc16 < 2:
                swwl = [("CC", CC_AGR)]
            gd(lambda e=e, cc16=cc16: NCG.dma_start(
                sw_sb[cc16 % 2][:, 0:RT],
                switch_in[128 * cc16:128 * (cc16 + 1), RT * e:RT * (e + 1)]),
               waits=swwl)
            GN = P.now("G")
            yd(lambda cc16=cc16: NCY.dma_start(
                rtl_sb[:, 0:RT], agr_out[128 * cc16:128 * (cc16 + 1), 0:RT]),
               waits=[("CC", CC_AGR)])
            LDN = P.now("LD")
            for tt4 in range(RT // 512):
                v(lambda cc16=cc16, tt4=tt4: NCV.tensor_tensor(
                    prod_sb[:, :], rtl_sb[:, 512 * tt4:512 * (tt4 + 1)],
                    sw_sb[cc16 % 2][:, 512 * tt4:512 * (tt4 + 1)], ALU.mult),
                  waits=([("G", GN), ("LD", LDN)] if tt4 == 0 else [])
                  + [("T", P.now("T"))])
                VN = P.now("V")
                t(lambda e=e, cc16=cc16, tt4=tt4: NCT.matmul(
                    pMISC[0:1, 0:512] if e == 0 else pBC[0:1, 0:512],
                    onesc_bf[:, :], prod_sb[:, :],
                    start=(cc16 == 0 and tt4 == 0),
                    stop=(cc16 == 15 and tt4 == RT // 512 - 1)),
                  waits=[("V", VN)])
    TN = P.now("T")
    v(lambda: NCV.tensor_reduce(lgsc[:, 0:1], pMISC[0:1, 0:512], AX.X, ALU.add),
      waits=[("T", TN)])
    v(lambda: NCV.tensor_reduce(lgsc[:, 1:2], pBC[0:1, 0:512], AX.X, ALU.add))
    v(lambda: NCV.tensor_copy(dbgk[:, 0:2], lgsc[:, 0:2]))
    v(lambda: NCV.tensor_scalar(lgrow[:, :], emask_sb[:, 0:8], lgsc[:, 0:1], None,
                                ALU.mult))
    v(lambda: NCV.scalar_tensor_tensor(lgrow[:, :], emask_sb[:, 8:16], lgsc[:, 1:2],
                                       lgrow[:, :], ALU.mult, ALU.add))
    VN = P.now("V")
    gd(lambda: NCG.dma_start(lgt_in, lgrow[:, :]), waits=[("V", VN)])
    GN = P.now("G")
    cc(lambda: NCG.collective_compute(
        "AllReduce", ALU.add, replica_groups=[[0, 1, 2, 3], [4, 5, 6, 7]],
        ins=[lgt_in.opt()], outs=[lgt_out.opt()]), waits=[("G", GN)])
    CC_LGT = P.now("CC")
    gd(lambda: NCG.dma_start(lgrow[:, :], lgt_out), waits=[("CC", CC_LGT)])
    G_LGT = P.now("G")
    v(lambda: NCV.tensor_tensor(lgrow[:, :], lgrow[:, :], swb_sb[:, :], ALU.add),
      waits=[("G", G_LGT)])
    v(lambda: NCV.tensor_copy(dbgk[:, 8:16], lgrow[:, :]))
    v(lambda: NCV.tensor_reduce(lgsc[:, 0:1], lgrow[:, :], AX.X, ALU.max))
    v(lambda: NCV.tensor_scalar(prob_sb[:, :], lgrow[:, :], lgsc[:, 0:1], None,
                                ALU.subtract))
    VN = P.now("V")
    s(lambda: NCS.activation(prob_sb[:, :], prob_sb[:, :], AF.Exp), waits=[("V", VN)])
    SN = P.now("S")
    v(lambda: NCV.tensor_reduce(lgsc[:, 1:2], prob_sb[:, :], AX.X, ALU.add),
      waits=[("S", SN)])
    v(lambda: NCV.reciprocal(lgsc[:, 2:3], lgsc[:, 1:2]))
    v(lambda: NCV.tensor_scalar(prob_sb[:, :], prob_sb[:, :], lgsc[:, 2:3], None,
                                ALU.mult))
    v(lambda: NCV.tensor_copy(lgrow[:, :], prob_sb[:, :]))
    VN = P.now("V")
    t(lambda: NCT.transpose(pMISC[0:8, 16:17], lgrow[:, :], ident_f[0:1, 0:1]),
      waits=[("V", VN)])
    TN = P.now("T")
    v(lambda: NCV.tensor_copy(probc_sb[:, :], pMISC[0:8, 16:17]), waits=[("T", TN)])
    VN = P.now("V")
    t(lambda: NCT.matmul(pMISC[0:8, 20:28], onesr_f[:, 0:8], prob_sb[:, :],
                         start=True, stop=True), waits=[("V", VN)])
    TN = P.now("T")
    v(lambda: NCV.tensor_scalar(cmp_sb[:, :], pMISC[0:8, 20:28], probc_sb[:, :], None,
                                ALU.is_gt), waits=[("T", TN)])
    VN = P.now("V")
    t(lambda: NCT.matmul(pMISC[0:1, 28:36], onesc_f[0:8, :],
                         cmp_sb[:, :], start=True, stop=True), waits=[("V", VN)])
    TN = P.now("T")
    v(lambda: NCV.tensor_scalar(selr_sb[:, 0:8], pMISC[0:1, 28:36], 7.0, None,
                                ALU.is_equal), waits=[("T", TN)])
    v(lambda: NCV.tensor_scalar(selr_sb[:, 8:16], pMISC[0:1, 28:36], 6.0, None,
                                ALU.is_equal))
    v(lambda: NCV.tensor_tensor(lgrow[:, :], prob_sb[:, :], selr_sb[:, 0:8], ALU.mult),
      waits=[("T", P.now("T"))])
    v(lambda: NCV.tensor_reduce(lgsc[:, 0:1], lgrow[:, :], AX.X, ALU.add))
    v(lambda: NCV.tensor_tensor(lgrow[:, :], prob_sb[:, :], selr_sb[:, 8:16], ALU.mult))
    v(lambda: NCV.tensor_reduce(lgsc[:, 1:2], lgrow[:, :], AX.X, ALU.add))
    VN = P.now("V")
    t(lambda: NCT.matmul(pMISC[:, 36:52], onesr_f[:, :], selr_sb[:, :],
                         start=True, stop=True), waits=[("V", VN)])
    t(lambda: NCT.matmul(pMISC[:, 52:54], onesr_f[:, :], lgsc[:, 0:2],
                         start=True, stop=True))
    TN = P.now("T")
    v(lambda: NCV.tensor_copy(selbc[:, :], pMISC[:, 36:52]), waits=[("T", TN)])
    v(lambda: NCV.tensor_copy(gcol_sb[:, :], pMISC[:, 52:54]))
    for name, wsrc, dst in (("w1", w1T_sb, w1sel), ("w2", w2T_sb, w2sel)):
        for e2 in range(2):
            v(lambda wsrc=wsrc, dst=dst, e2=e2: NCV.tensor_scalar(
                dst[:, 128 * e2:128 * (e2 + 1)], wsrc[:, 0:128],
                selbc[:, 8 * e2:8 * e2 + 1], None, ALU.mult),
              waits=[("G", G_W3X)])
            for ee in range(1, 8):
                v(lambda wsrc=wsrc, dst=dst, e2=e2, ee=ee: NCV.scalar_tensor_tensor(
                    dst[:, 128 * e2:128 * (e2 + 1)], wsrc[:, 128 * ee:128 * (ee + 1)],
                    selbc[:, 8 * e2 + ee:8 * e2 + ee + 1],
                    dst[:, 128 * e2:128 * (e2 + 1)], ALU.mult, ALU.add))
    for bsrc, dst in ((b1T_sb, b1sel), (b2T_sb, b2sel)):
        for e2 in range(2):
            v(lambda bsrc=bsrc, dst=dst, e2=e2: NCV.tensor_scalar(
                dst[:, e2:e2 + 1], bsrc[:, 0:1], selbc[:, 8 * e2:8 * e2 + 1], None,
                ALU.mult))
            for ee in range(1, 8):
                v(lambda bsrc=bsrc, dst=dst, e2=e2, ee=ee: NCV.scalar_tensor_tensor(
                    dst[:, e2:e2 + 1], bsrc[:, ee:ee + 1],
                    selbc[:, 8 * e2 + ee:8 * e2 + ee + 1], dst[:, e2:e2 + 1],
                    ALU.mult, ALU.add))
    v(lambda: NCV.tensor_scalar(b2c_sb[:, :], b2sel[:, 0:1], gcol_sb[:, 0:1], None,
                                ALU.mult))
    v(lambda: NCV.scalar_tensor_tensor(b2c_sb[:, :], b2sel[:, 1:2], gcol_sb[:, 1:2],
                                       b2c_sb[:, :], ALU.mult, ALU.add))
    V_SEL = P.now("V")

    T_P3A = P.now("T")
    gd(lambda: NCG.dma_start(Wqkv_sb[:, :].rearrange("p (k c) -> p k c", k=16),
                             w3T_in.rearrange("(k p) c -> p k c", p=128)),
       waits=[("T", T_P3A)])
    G_W3 = P.now("G")

    def emit_p3b_window(wi):
        ldn = P.now("LD")
        wl = [("V", V_SEL)]
        if wi > 0:
            wl.append(("ST", hold["p3b_st_%d" % (wi - 1)]))
        yd(lambda wi=wi: NCY.dma_start(
            ln2_w[:, :].rearrange("p (k t) -> p k t", k=16),
            bass.AP(ln2_sp.tensor, 512 * wi, [[NQ, 128], [128 * NQ, 16], [1, 512]])),
           waits=wl)
        yd(lambda wi=wi: NCY.dma_start(
            x1_w[:, :].rearrange("p (k t) -> p k t", k=16),
            bass.AP(x1_sp.tensor, 512 * wi, [[NQ, 128], [128 * NQ, 16], [1, 512]])))
        LD_T = ldn + 32
        for ci16 in range(16):
            for e2 in range(2):
                wl = [("LD", LD_T)] if (ci16 == 0 and e2 == 0) else []
                t(lambda ci16=ci16, e2=e2: NCT.matmul(
                    pA[:, :], w1sel[:, 128 * e2:128 * (e2 + 1)],
                    ln2_w[:, 512 * ci16:512 * (ci16 + 1)], start=True, stop=True),
                  waits=wl + [("V", P.now("V")), ("S", P.now("S"))])
                TN = P.now("T")
                s(lambda e2=e2: NCS.activation(h1_sb[:, :], pA[:, :], AF.Gelu,
                                               bias=b1sel[:, e2:e2 + 1]),
                  waits=[("T", TN), ("V", P.now("V"))])
                SN = P.now("S")
                v(lambda: NCV.tensor_copy(h1b_sb[:, :], h1_sb[:, :]),
                  waits=[("S", SN), ("T", P.now("T"))], nodrain=True)
                VN3 = P.now("V")
                t(lambda e2=e2: NCT.matmul(pBC[:, 0:512], w2sel[:, 128 * e2:128 * (e2 + 1)],
                                           h1b_sb[:, :], start=True, stop=True),
                  waits=[("V", VN3)])
                TN = P.now("T")
                if e2 == 0:
                    v(lambda: NCV.tensor_scalar(oacc[:, :], pBC[:, 0:512],
                                                gcol_sb[:, 0:1], None, ALU.mult),
                      waits=[("T", TN)])
                else:
                    v(lambda: NCV.scalar_tensor_tensor(oacc[:, :], pBC[:, 0:512],
                                                       gcol_sb[:, 1:2], oacc[:, :],
                                                       ALU.mult, ALU.add),
                      waits=[("T", TN)])
            v(lambda ci16=ci16: NCV.tensor_scalar(oacc[:, :], oacc[:, :],
                                                  b2c_sb[:, :], None, ALU.add))
            v(lambda ci16=ci16: NCV.tensor_tensor(
                w3in_w[:, 512 * ci16:512 * (ci16 + 1)], oacc[:, :],
                ln2_w[:, 512 * ci16:512 * (ci16 + 1)], ALU.add))
        V_W3IN = P.now("V")
        if DEBUG and wi == 0:
            dbg_w3in = nc.dram_tensor("dbg_w3in", [128, 8192], BF16,
                                      kind="ExternalOutput").ap()
            ys(lambda: NCY.dma_start(dbg_w3in, w3in_w[:, :]), waits=[("V", V_W3IN)])
        for ci16 in range(16):
            for kk in range(16):
                wl = [("V", max(V_W3IN, P.now("V"))), ("G", G_W3),
                      ("S", P.now("S"))] if kk == 0 else []
                t(lambda ci16=ci16, kk=kk: NCT.matmul(
                    pA[:, :],
                    Wqkv_sb[:, kk * 2048 + ci16 * 128:kk * 2048 + ci16 * 128 + 128],
                    w3in_w[:, 512 * kk:512 * (kk + 1)],
                    start=(kk == 0), stop=(kk == 15)), waits=wl)
            TN = P.now("T")
            v(lambda ci16=ci16: NCV.tensor_scalar(
                attn_sb[:, 512 * ci16:512 * (ci16 + 1)], pA[:, :],
                b3_sb[:, ci16:ci16 + 1], None, ALU.add), waits=[("T", TN)])
            VN = P.now("V")
            v(lambda ci16=ci16: NCV.tensor_tensor(
                asq_sb[:, :], attn_sb[:, 512 * ci16:512 * (ci16 + 1)],
                attn_sb[:, 512 * ci16:512 * (ci16 + 1)], ALU.mult),
              waits=[("T", P.now("T"))])
            VN2 = P.now("V")
            t(lambda ci16=ci16: NCT.matmul(
                pMISC[0:1, 0:512], onesc_bf[:, :],
                attn_sb[:, 512 * ci16:512 * (ci16 + 1)],
                start=(ci16 == 0), stop=(ci16 == 15)), waits=[("V", VN2)])
            t(lambda ci16=ci16: NCT.matmul(
                pBC[0:1, 0:512], onesc_bf[:, :], asq_sb[:, :],
                start=(ci16 == 0), stop=(ci16 == 15)))
        TN = P.now("T")
        v(lambda: NCV.tensor_scalar(murow[:, :], pMISC[0:1, 0:512], 1.0 / 2048, None,
                                    ALU.mult), waits=[("T", TN)])
        v(lambda: NCV.tensor_scalar(tmpsr[:, :], pBC[0:1, 0:512], 1.0 / 2048, None,
                                    ALU.mult))
        v(lambda: NCV.scalar_tensor_tensor(varow[:, :], murow[:, :], -1.0,
                                           murow[:, :], ALU.mult, ALU.mult))
        v(lambda: NCV.tensor_tensor(varow[:, :], tmpsr[:, :], varow[:, :], ALU.add))
        v(lambda: NCV.tensor_scalar(varow[:, :], varow[:, :], LNEPS, None, ALU.add))
        if DEBUG and wi == 0:
            dbg_lnr = nc.dram_tensor("dbg_lnr", [1, 1536], F32,
                                     kind="ExternalOutput").ap()
            VNR = P.now("V")
            ys(lambda: NCY.dma_start(dbg_lnr[:, 0:512], murow[:, :]), waits=[("V", VNR)])
            ys(lambda: NCY.dma_start(dbg_lnr[:, 512:1024], tmpsr[:, :]), waits=[("V", VNR)])
            ys(lambda: NCY.dma_start(dbg_lnr[:, 1024:1536], varow[:, :]), waits=[("V", VNR)])
        VN = P.now("V")
        s(lambda: NCS.activation(varow[:, :], varow[:, :], AF.Ln),
          waits=[("V", VN)])
        s(lambda: NCS.activation(scr_row[:, :], varow[:, :], AF.Exp, scale=-0.5),
          waits=[("V", P.now("V"))], drain=True)
        SNx = P.now("S")
        v(lambda: NCV.tensor_copy(r1row[:, :], scr_row[:, :]), waits=[("S", SNx)])
        SN = P.now("S")
        t(lambda: NCT.matmul(pMISC[:, 0:512], onesr_f[:, :], murow[:, :],
                             start=True, stop=True), waits=[("S", SN)])
        t(lambda: NCT.matmul(pBC[:, 0:512], onesr_f[:, :], r1row[:, :],
                             start=True, stop=True))
        TN = P.now("T")
        if DEBUG and wi == 0:
            dbg_ffn = nc.dram_tensor("dbg_ffn", [128, 8192], BF16,
                                     kind="ExternalOutput").ap()
            VNF = P.now("V")
            ys(lambda: NCY.dma_start(dbg_ffn, attn_sb[:, :]), waits=[("V", VNF)])
        for ci16 in range(16):
            v(lambda ci16=ci16: NCV.tensor_tensor(
                x2_sb[:, :], attn_sb[:, 512 * ci16:512 * (ci16 + 1)], pMISC[:, 0:512],
                ALU.subtract), waits=[("T", TN)] if ci16 == 0 else [])
            v(lambda ci16=ci16: NCV.tensor_tensor(x2_sb[:, :], x2_sb[:, :],
                                                  pBC[:, 0:512], ALU.mult))
            v(lambda ci16=ci16: NCV.tensor_tensor(
                x1_w[:, 512 * ci16:512 * (ci16 + 1)], x2_sb[:, :],
                x1_w[:, 512 * ci16:512 * (ci16 + 1)], ALU.add))
        V_X2 = P.now("V")
        for tt4 in range(4):
            STW = hold.get("p3b_ost_%d_%d" % (wi, tt4 - 1), hold.get("p3b_ost_%d_3" % (wi - 1), 0))
            for ci16 in range(16):
                t(lambda ci16=ci16, tt4=tt4: NCT.transpose(
                    pY[:, 0:128], x1_w[:, 512 * ci16 + 128 * tt4:512 * ci16 + 128 * (tt4 + 1)],
                    ident_f[:, :]),
                  waits=[("V", V_X2 if tt4 == 0 else P.now("V"))] if ci16 == 0
                  else [("V", P.now("V"))])
                TN2 = P.now("T")
                v(lambda ci16=ci16: NCV.tensor_copy(
                    outb[:, 128 * ci16:128 * (ci16 + 1)], pY[:, 0:128]),
                  waits=[("T", TN2), ("ST", STW)] if ci16 == 0 else [("T", TN2)], nodrain=True)
            VN2 = P.now("V")
            ys(lambda wi=wi, tt4=tt4: NCY.dma_start(
                out_t[512 * wi + 128 * tt4:512 * wi + 128 * (tt4 + 1), :],
                outb[:, :]), waits=[("V", VN2)])
            hold["p3b_ost_%d_%d" % (wi, tt4)] = P.now("ST")
        hold["p3b_st_%d" % wi] = P.now("ST")

    for wi in range(NW3):
        emit_p3b_window(wi)

    if DEBUG:
        dbg_ag1 = nc.dram_tensor("dbg_ag1", [2048, NQ], BF16, kind="ExternalOutput").ap()
        dbg_ag2 = nc.dram_tensor("dbg_ag2", [512, S_], BF16, kind="ExternalOutput").ap()
        dbg_x1 = nc.dram_tensor("dbg_x1", [2048, NQ], F32, kind="ExternalOutput").ap()
        dbg_ln2 = nc.dram_tensor("dbg_ln2", [2048, NQ], BF16, kind="ExternalOutput").ap()
        dbg_sc = nc.dram_tensor("dbg_sc", [1, 64], F32, kind="ExternalOutput").ap()
        STN0 = P.now("ST")
        gd(lambda: NCG.dma_start(dbg_ag1, ag1_in), waits=[("ST", STN0), ("CC", P.now("CC"))])
        gd(lambda: NCG.dma_start(dbg_ag2, ag2_in))
        gd(lambda: NCG.dma_start(dbg_x1, x1_sp))
        gd(lambda: NCG.dma_start(dbg_ln2, ln2_sp))
        dbg_agr = nc.dram_tensor("dbg_agr", [2048, RT], BF16, kind="ExternalOutput").ap()
        gd(lambda: NCG.dma_start(dbg_agr, agr_out[0:2048, :]))
        VND = P.now("V")
        v(lambda: NCV.tensor_copy(murow[:, 0:8], prob_sb[:, :]))
        v(lambda: NCV.tensor_copy(murow[:, 8:24], selr_sb[:, :]))
        v(lambda: NCV.tensor_copy(murow[:, 24:28], lgsc[:, :]))
        v(lambda: NCV.tensor_copy(murow[:, 32:64], dbgk[:, :]))
        VND = P.now("V")
        gd(lambda: NCG.dma_start(dbg_sc, murow[:, 0:64]), waits=[("V", VND)])
        GDN = P.now("G")
        P.op("g", lambda: NCG.engine_nop(), [("G", GDN)], None)

    STN = P.now("ST")
    P.op("y", lambda: NCY.nop(), [("ST", STN)], None)

    P.replay()
    es.close()
    return nc


_BUILD_CACHE = {}


def kernel(**inputs):
    S_ = int(inputs["L0_attn_hidden_states"].shape[1])
    NQ = S_ // 4
    maps = make_in_maps(inputs, S_)
    if S_ not in _BUILD_CACHE:
        _BUILD_CACHE[S_] = build(S_)
    nc = _BUILD_CACHE[S_]
    res = run_bass_kernel_spmd(nc, maps, core_ids=list(range(8)))
    out = np.empty((B, S_, D), np.float32)
    for core in range(8):
        b = core // 4
        j = core % 4
        out[b, j * NQ:(j + 1) * NQ, :] = res.results[core]["out"]
    return out


# version 13
# speedup vs baseline: 1.0129x; 1.0129x over previous
import contextlib
import numpy as np
import ml_dtypes
import concourse.bass as bass
import concourse.mybir as mybir
from concourse.bass_utils import run_bass_kernel_spmd

F32 = mybir.dt.float32
BF16 = mybir.dt.bfloat16
AF = mybir.ActivationFunctionType
ALU = mybir.AluOpType
AX = mybir.AxisListType

B = 2
D = 2048
NEG = -1e30
EPS = 1e-6
LNEPS = 1e-5


class Prog:
    ENGINES = ("t", "v", "s", "g", "y")

    def __init__(self, nc):
        self.nc = nc
        self.ops = {e: [] for e in self.ENGINES}
        self.sems = {}
        self.cnt = {}
        self.hist = {}

    def sem(self, name):
        h = self.nc.semaphore(name).__enter__()
        self.sems[name] = h
        self.cnt[name] = 0

    SEQCST = True
    CONS_FULL = ("T", "V", "S", "LD", "ST", "G", "CC")
    CONS_LITE = ("LD", "ST", "G", "CC")
    CONS = CONS_FULL

    def __init2__(self):
        pass

    def op(self, eng, fn, waits=(), inc=None, inc_n=1, nodrain=False,
           drain_before=False):
        w = dict()
        for sname, val in waits:
            w[sname] = max(w.get(sname, 0), val)
        if self.SEQCST:
            for sname in self.CONS:
                w[sname] = max(w.get(sname, 0), self.cnt[sname])
        hist = self.hist.setdefault(eng, {})
        waits = []
        for k, v in w.items():
            if v > hist.get(k, 0):
                waits.append((k, v))
                hist[k] = v
        if inc is not None:
            self.cnt[inc] += inc_n
            if (eng, inc) in (("t", "T"), ("v", "V"), ("s", "S")):
                h2 = self.hist.setdefault(eng, {})
                h2[inc] = max(h2.get(inc, 0), self.cnt[inc])
        self.ops[eng].append((fn, tuple(waits), inc, inc_n, nodrain,
                              drain_before))

    def now(self, sem):
        return self.cnt[sem]

    def replay(self):
        nc = self.nc
        attr = {"t": "tensor", "v": "vector", "s": "scalar", "g": "gpsimd", "y": "sync"}
        with nc.Block() as blk:
            for ename in self.ENGINES:
                ops_local = self.ops[ename]

                def mk(ops_local, ename=ename):
                    def body(eng):
                        first = True
                        for fn, waits, inc, inc_n, nodrain, drain_before in ops_local:
                            if drain_before and not first:
                                eng.drain()
                            if ename == "v" and not first and not nodrain:
                                eng.drain()
                            first = False
                            for (sname, val) in waits:
                                if val > 0:
                                    eng.wait_ge(self.sems[sname], val)
                            inst = fn()
                            if inc is not None:
                                inst.then_inc(self.sems[inc], inc_n)
                    return body

                getattr(blk, attr[ename])(mk(ops_local))


def _consts():
    ident = np.eye(128, dtype=np.float32)
    ii = np.arange(128)[:, None]
    tt = np.arange(128)[None, :]
    triu = (ii <= tt).astype(np.float32)
    maskMA = np.concatenate([
        np.where(ii < tt, 0.0, NEG), np.where(ii <= tt, 0.0, NEG)],
        axis=1).astype(np.float32)
    maskT = np.where(tt < ii, 0.0, NEG).astype(np.float32)
    eye4 = np.zeros((128, 16), np.float32)
    for h in range(4):
        eye4[:, 4 * h + h] = 1.0
    return dict(c_ident=ident,
                c_ones_row=np.ones((1, 128), np.float32),
                c_ones_col=np.ones((128, 1), np.float32),
                c_triu=triu, c_maskMA=maskMA, c_maskT=maskT, c_eye4=eye4)


def make_in_maps(inputs, S_):
    NQ = S_ // 4
    RT = min(2048, NQ)
    hs = np.asarray(inputs["L0_attn_hidden_states"])[:, :S_, :]
    Wqkv = np.asarray(inputs["L0_attn_W_qkv"])
    Wz = np.asarray(inputs["L0_attn_W_z"])
    Wb = np.asarray(inputs["L0_attn_W_b"])
    Wa = np.asarray(inputs["L0_attn_W_a"])
    convw = np.asarray(inputs["L0_attn_conv_w"])[:, 0, :]
    A_log = np.asarray(inputs["L0_attn_A_log"])
    dtb = np.asarray(inputs["L0_attn_dt_bias"])
    norm_w = np.asarray(inputs["L0_attn_norm_w"])
    W_out = np.asarray(inputs["L0_attn_W_out"])
    sw_w = np.asarray(inputs["L0_ffn_switch_w"])
    sw_b = np.asarray(inputs["L0_ffn_switch_b"])
    w1 = np.asarray(inputs["L0_ffn_w1"]); b1 = np.asarray(inputs["L0_ffn_b1"])
    w2 = np.asarray(inputs["L0_ffn_w2"]); b2 = np.asarray(inputs["L0_ffn_b2"])
    w3 = np.asarray(inputs["L0_ffn_w3"]); b3 = np.asarray(inputs["L0_ffn_b3"])
    consts = _consts()

    maps = []
    for core in range(8):
        b = core // 4
        j = core % 4
        heads = list(range(4 * j, 4 * j + 4))
        wqkv_sl = np.concatenate(
            [Wqkv[:, 128 * h:128 * (h + 1)] for h in heads]
            + [Wqkv[:, 2048 + 128 * h:2048 + 128 * (h + 1)] for h in heads]
            + [Wqkv[:, 4096 + 128 * h:4096 + 128 * (h + 1)] for h in heads]
            + [Wz[:, 128 * h:128 * (h + 1)] for h in heads], axis=1)
        convc = np.concatenate(
            [convw[128 * h:128 * (h + 1), :] for h in heads]
            + [convw[2048 + 128 * h:2048 + 128 * (h + 1), :] for h in heads]
            + [convw[4096 + 128 * h:4096 + 128 * (h + 1), :] for h in heads], axis=1)
        e0, e1 = 2 * j, 2 * j + 1
        sw0 = sw_w[e0, :2048 * RT].reshape(RT, 2048).T
        sw1 = sw_w[e1, :2048 * RT].reshape(RT, 2048).T
        switch_fm = np.concatenate([sw0, sw1], axis=1)
        m0 = np.zeros((1, 8), np.float32); m0[0, e0] = 1.0
        m1 = np.zeros((1, 8), np.float32); m1[0, e1] = 1.0
        m = dict(
            h_own=hs[b, j * NQ:(j + 1) * NQ, :],
            wqkv_sl=wqkv_sl, wb_sl=Wb[:, heads], wa_sl=Wa[:, heads], convc=convc,
            alog_c=A_log[heads].reshape(4, 1), dtb_c=dtb[heads].reshape(4, 1),
            normw_c=norm_w.reshape(128, 1),
            W_out=W_out, w3T=w3.T, b3_c=b3.reshape(16, 128).T,
            w1T=np.transpose(w1, (2, 0, 1)).reshape(128, 1024),
            w2T=np.transpose(w2, (2, 0, 1)).reshape(128, 1024),
            b1T=b1.T, b2T=b2.T,
            swb_row=sw_b.reshape(1, 8), switch_fm=switch_fm,
            emask0=m0, emask1=m1,
            offtok=np.array([[j * NQ]], np.int32),
            **consts,
        )
        mm = {}
        for k, vv in m.items():
            vv = np.ascontiguousarray(vv)
            if k != "offtok":
                vv = vv.astype(np.float32)
            mm[k] = vv
        maps.append(mm)
    return maps


DEBUG = False
RELAX = 1
SEQDRV = False


def build(S_):
    NQ = S_ // 4
    NW = S_ // 512
    NWQ = NW // 4
    NW3 = NQ // 512
    RT = min(2048, NQ)
    RTW = RT // 512

    nc = bass.Bass("TRN2", target_bir_lowering=False)

    def din(name, shape, dt=F32):
        return nc.dram_tensor(name, list(shape), dt, kind="ExternalInput").ap()

    h_own = din("h_own", (NQ, 2048))
    wqkv_in = din("wqkv_sl", (2048, 2048))
    wb_in = din("wb_sl", (2048, 4))
    wa_in = din("wa_sl", (2048, 4))
    convc_in = din("convc", (128, 48))
    alog_in = din("alog_c", (4, 1))
    dtb_in = din("dtb_c", (4, 1))
    normw_in = din("normw_c", (128, 1))
    Wout_in = din("W_out", (2048, 2048))
    w3T_in = din("w3T", (2048, 2048))
    b3_in = din("b3_c", (128, 16))
    w1T_in = din("w1T", (128, 1024))
    w2T_in = din("w2T", (128, 1024))
    b1T_in = din("b1T", (128, 8))
    b2T_in = din("b2T", (128, 8))
    swb_in = din("swb_row", (1, 8))
    switch_in = din("switch_fm", (2048, 2 * RT))
    emask0_in = din("emask0", (1, 8))
    emask1_in = din("emask1", (1, 8))
    offtok_in = nc.dram_tensor("offtok", [1, 1], mybir.dt.int32, kind="ExternalInput").ap()
    ident_in = din("c_ident", (128, 128))
    onesr_in = din("c_ones_row", (1, 128))
    onesc_in = din("c_ones_col", (128, 1))
    triu_in = din("c_triu", (128, 128))
    maskMA_in = din("c_maskMA", (128, 256))
    maskT_in = din("c_maskT", (128, 128))
    eye4_in = din("c_eye4", (128, 16))

    out_t = nc.dram_tensor("out", [NQ, 2048], F32, kind="ExternalOutput").ap()

    ag1_in = nc.dram_tensor("ag1_in", [2048, NQ], BF16, kind="Internal").ap()
    ag1_out = nc.dram_tensor("ag1_out", [4 * 2048, NQ], BF16, kind="Internal").ap()
    ag2_in = nc.dram_tensor("ag2_in", [512, S_], BF16, kind="Internal").ap()
    ag2_out = nc.dram_tensor("ag2_out", [4 * 512, S_], BF16, kind="Internal").ap()
    agr_in = nc.dram_tensor("agr_in", [2048, RT], BF16, kind="Internal").ap()
    agr_out = nc.dram_tensor("agr_out", [4 * 2048, RT], BF16, kind="Internal").ap()
    lgt_in = nc.dram_tensor("lgt_in", [1, 8], F32, kind="Internal").ap()
    lgt_out = nc.dram_tensor("lgt_out", [1, 8], F32, kind="Internal").ap()
    hbf = nc.dram_tensor("hbf", [NQ, 2048], BF16, kind="Internal").ap()
    x1_sp = nc.dram_tensor("x1_sp", [2048, NQ], F32, kind="Internal").ap()
    ln2_sp = nc.dram_tensor("ln2_sp", [2048, NQ], BF16, kind="Internal").ap()

    es = contextlib.ExitStack()

    def sb(name, shape, dt=BF16):
        return es.enter_context(nc.sbuf_tensor("sb_" + name, list(shape), dt))

    ident_bf = sb("ident_bf", (128, 128))
    ident_f = sb("ident_f", (128, 128), F32)
    onesr_f = sb("onesr_f", (1, 128), F32)
    onesc_bf = sb("onesc_bf", (128, 1))
    onesc_f = sb("onesc_f", (128, 1), F32)
    triu_f = sb("triu_f", (128, 128), F32)
    maskMA_f = sb("maskMA_f", (128, 256), F32)
    maskT_f = sb("maskT_f", (128, 128), F32)
    eye4_bf = sb("eye4_bf", (128, 16))
    normw_c = sb("normw_c", (128, 1), F32)
    alog_c = sb("alog_c", (4, 1), F32)
    negexpA = sb("negexpA", (4, 1), F32)
    dtb_c = sb("dtb_c", (4, 1), F32)
    Wqkv_sb = sb("WBIG", (128, 16 * 2048))
    Wb_sb = sb("Wb_sb", (128, 16 * 4))
    Wa_sb = sb("Wa_sb", (128, 16 * 4))
    convc_sb = sb("convc_sb", (128, 48), F32)
    slot1 = sb("slot1", (128, 16384))
    slot2 = sb("slot2", (128, 16384))
    slot3 = sb("slot3", (128, 14848))
    slot4 = sb("slot4", (128, 8192))
    s1f = slot1[:, :].bitcast(F32)
    s2f = slot2[:, :].bitcast(F32)
    s3f = slot3[:, :].bitcast(F32)
    p0_in = s1f[:, 0:2048]
    p0_sq = s1f[:, 2048:4096]
    p0_bf = slot2[:, 0:2048]
    p0_hb = slot2[:, 2048:4096]
    p0_tr = slot2[:, 4096:6144]
    p0_st = sb("p0_st", (128, 6), F32)
    p0_m = sb("p0_m", (128, 1), F32)
    p0_r = sb("p0_r", (128, 1), F32)
    lw = [slot1[:, 0:8192], slot1[:, 8192:16384]]
    qk_buf = slot2[:, 0:4096]
    v_buf = slot2[:, 4096:6144]
    z_buf = slot2[:, 6144:8192]
    coreo = slot2[:, 8192:10240]
    mixed_sb = slot2[:, 10240:10756]
    tail_sb = slot2[:, 10756:10792]
    tanh_sb = slot2[:, 10792:11304]
    sq_tmp = slot2[:, 11304:11816]
    acc_f = s2f[:, 5908:6420]
    s4f = slot4[:, :].bitcast(F32)
    gpre_r = s4f[0:4, 0:512]
    bpre_r = s4f[0:4, 512:1024]
    ssq_r = s4f[0:4, 1024:1536]
    ssk_r = s4f[0:4, 1536:2048]
    g_r = s4f[0:4, 2048:2560]
    c1_r = s4f[0:4, 2560:3072]
    c2_r = s4f[0:4, 3072:3584]
    c3_r = s4f[0:4, 3584:4096]
    beta_r = s2f[0:4, 6420:6932]
    tmp_r = s2f[0:4, 6932:7444]
    tmp2_r = s2f[0:4, 7444:7956]
    mexp_sb = s3f[:, 0:256]
    EM_sb = slot3[:, 512:768]
    texp_sb = s3f[:, 512:640]
    Tpre_sb = slot3[:, 1280:1408]
    MA_2 = [slot3[:, 768:1024], slot2[:, 15400:15656]]
    T_2 = [slot3[:, 1408:1536], slot2[:, 15656:15784]]
    Qg_2 = [slot3[:, 1536:1664], slot2[:, 15784:15912]]
    Rbuf_2 = [slot3[:, 1664:1920], slot2[:, 10240:10496]]
    KTd_2 = [slot3[:, 1920:2048], slot2[:, 10496:10624]]
    Tp_2 = [slot3[:, 2048:3328], slot2[:, 13864:15144]]
    TTp_2 = [slot3[:, 3328:3456], slot2[:, 15144:15272]]
    ImM_2 = [slot3[:, 3456:3584], slot2[:, 15272:15400]]
    Ybuf = [slot3[:, 3584:3840], slot3[:, 3840:4096]]
    U_sb = slot3[:, 4096:4352]
    Isc_sb = slot3[:, 4352:4480]
    PhiT_sb = slot3[:, 4480:4608]
    QtT_sb = slot3[:, 4608:4736]
    S0_sb = slot3[:, 4736:5248]
    Ot_sb = slot3[:, 5248:5376]
    Osq_sb = slot3[:, 5376:5504]
    gat_sb = slot3[:, 5504:5632]
    cols_sb = s3f[:, 2816:2836]
    gcols_sb = s3f[:, 2836:2840]
    gendr_sb = s3f[0:1, 2868:2872]
    npsic = s3f[:, 2844:2848]
    phic = s3f[:, 2848:2852]
    expphic = s3f[:, 2852:2856]
    kdc = s3f[:, 2856:2860]
    expgend = s3f[:, 2860:2864]
    betac = s3f[:, 2864:2868]
    rows4 = s3f[0:1, 2880:4928]
    r4v = rows4.rearrange("p (a b t) -> p a b t", a=4, b=4)
    growb = s3f[0:1, 4928:5440]
    crowb = s3f[0:1, 5440:6976]
    msr_sb = s3f[0:1, 6976:7104]
    rstd_sb = s3f[0:1, 7104:7232]
    x1_w = s1f[:, :]
    core_w = slot2[:, 0:8192]
    w3in_w = slot2[:, 0:8192]
    hbf_w = slot2[:, 8192:16384]
    outb = s2f[:, 4096:6144]
    sw_sb = [slot2[:, 8192:10240], slot2[:, 10240:12288]]
    rtl_sb = slot2[:, 12288:14336]
    prod_sb = slot2[:, 14336:14848]
    attn_sb = slot3[:, 0:8192]
    asq_sb = slot3[:, 8192:8704]
    stat_bf = slot3[:, 8704:9728]
    ln2_w = slot4[:, 0:8192]
    murow = sb("murow", (1, 512), F32)
    varow = sb("varow", (1, 512), F32)
    r1row = sb("r1row", (1, 512), F32)
    tmpsr = sb("tmpsr", (1, 512), F32)
    w1T_sb = sb("w1T_sb", (128, 1024))
    w2T_sb = sb("w2T_sb", (128, 1024))
    b1T_sb = sb("b1T_sb", (128, 8), F32)
    b2T_sb = sb("b2T_sb", (128, 8), F32)
    b3_sb = sb("b3_sb", (128, 16), F32)
    swb_sb = sb("swb_sb", (1, 8), F32)
    emask_sb = sb("emask_sb", (1, 16), F32)
    lgrow = sb("lgrow", (1, 8), F32)
    lgsc = sb("lgsc", (1, 4), F32)
    prob_sb = sb("prob_sb", (1, 8), F32)
    probc_sb = sb("probc_sb", (8, 1), F32)
    cmp_sb = sb("cmp_sb", (8, 8), F32)
    selr_sb = sb("selr_sb", (1, 16), F32)
    selbc = sb("selbc", (128, 16), F32)
    gcol_sb = sb("gcol_sb", (128, 2), F32)
    w1sel = sb("w1sel", (128, 256))
    w2sel = sb("w2sel", (128, 256))
    b1sel = sb("b1sel", (128, 2), F32)
    b2sel = sb("b2sel", (128, 2), F32)
    b2c_sb = sb("b2c_sb", (128, 1), F32)
    h1_sb = sb("h1_sb", (128, 512))
    oacc = sb("oacc", (128, 512), F32)
    x2_sb = sb("x2_sb", (128, 512), F32)
    offr_sb = es.enter_context(nc.sbuf_tensor("sb_offr", [1, 1], mybir.dt.int32))
    dbgk = sb("dbgk", (1, 32), F32)
    scr_row = sb("scr_row", (1, 512), F32)
    h1b_sb = sb("h1b", (128, 512))

    pA = es.enter_context(nc.psum_tensor("pA", [128, 512], F32))
    pG_full = es.enter_context(nc.psum_tensor("pG", [128, 512], F32))
    pG = pG_full[:, 0:256]
    pBC = es.enter_context(nc.psum_tensor("pBC", [128, 512], F32))
    pSQ_full = es.enter_context(nc.psum_tensor("pSQ", [128, 512], F32))
    pSQ = pSQ_full[:, 0:256]
    pY_full = es.enter_context(nc.psum_tensor("pY", [128, 512], F32))
    pY = pY_full[:, 0:256]
    pOS_full = es.enter_context(nc.psum_tensor("pOS", [128, 512], F32))
    pOS = pOS_full[:, 0:256]
    pMISC = es.enter_context(nc.psum_tensor("pMISC", [128, 512], F32))
    pTR_full = es.enter_context(nc.psum_tensor("pTR", [128, 1024], BF16))
    pTR = pTR_full[:, 0:512]

    P = Prog(nc)
    for sname in ("LD", "ST", "T", "V", "S", "G", "GC", "CC"):
        P.sem(sname)

    def t(fn, waits=()):
        P.op("t", fn, waits, "T")

    def v(fn, waits=(), nodrain=False):
        P.op("v", fn, waits, "V", nodrain=nodrain)

    def s(fn, waits=(), drain=False):
        P.op("s", fn, waits, "S", drain_before=drain)

    def gd(fn, waits=()):
        P.op("g", fn, waits, "G", 16)

    def gc(fn, waits=()):
        P.op("g", fn, waits, "GC", 1)

    def cc(fn, waits=()):
        P.op("g", fn, waits, "CC", 1)

    def yd(fn, waits=()):
        P.op("y", fn, waits, "LD", 16)

    def ys(fn, waits=()):
        P.op("y", fn, waits, "ST", 16)

    def yw(fn, waits=()):
        P.op("y", fn, waits, None)

    NCV, NCS, NCT, NCG, NCY = nc.vector, nc.scalar, nc.tensor, nc.gpsimd, nc.sync

    gc(lambda: NCG.memset(S0_sb[:, :], 0.0))
    gc(lambda: NCG.memset(tail_sb[:, :], 0.0))
    gd(lambda: NCG.dma_start(ident_bf[:, :], ident_in))
    gd(lambda: NCG.dma_start(ident_f[:, :], ident_in))
    gd(lambda: NCG.dma_start(onesr_f[:, :], onesr_in))
    gd(lambda: NCG.dma_start(onesc_bf[:, :], onesc_in))
    gd(lambda: NCG.dma_start(onesc_f[:, :], onesc_in))
    gd(lambda: NCG.dma_start(triu_f[:, :], triu_in))
    gd(lambda: NCG.dma_start(maskMA_f[:, :], maskMA_in))
    gd(lambda: NCG.dma_start(maskT_f[:, :], maskT_in))
    gd(lambda: NCG.dma_start(eye4_bf[:, :], eye4_in))
    gd(lambda: NCG.dma_start(normw_c[:, :], normw_in))
    gd(lambda: NCG.dma_start(alog_c[:, :], alog_in))
    gd(lambda: NCG.dma_start(dtb_c[:, :], dtb_in))
    gd(lambda: NCG.dma_start(convc_sb[:, :], convc_in))
    gd(lambda: NCG.dma_start(Wqkv_sb[:, :].rearrange("p (k c) -> p k c", k=16),
                             wqkv_in.rearrange("(k p) c -> p k c", p=128)))
    gd(lambda: NCG.dma_start(Wb_sb[:, :].rearrange("p (k c) -> p k c", k=16),
                             wb_in.rearrange("(k p) c -> p k c", p=128)))
    gd(lambda: NCG.dma_start(Wa_sb[:, :].rearrange("p (k c) -> p k c", k=16),
                             wa_in.rearrange("(k p) c -> p k c", p=128)))
    G_PRE = P.now("G")
    s(lambda: NCS.activation(negexpA[:, :], alog_c[:, :], AF.Exp), waits=[("G", G_PRE), ("GC", 2)])
    SN = P.now("S")
    v(lambda: NCV.tensor_scalar(negexpA[:, :], negexpA[:, :], -1.0, None, ALU.mult),
      waits=[("S", SN)])
    V_PRE = P.now("V")

    NT0 = NQ // 128
    p0_tr_done = 0
    p0_st_done = 0
    for it in range(NT0):
        ldn = P.now("LD")
        prevV = P.now("V")
        prevS = P.now("S")
        prevST = P.now("ST")
        yd(lambda it=it: NCY.dma_start(p0_in[:, :], h_own[128 * it:128 * (it + 1), :]),
           waits=[("V", prevV), ("S", prevS), ("ST", prevST)])
        LD_T = ldn + 16
        v(lambda: NCV.tensor_reduce(p0_st[:, 0:1], p0_in[:, :], AX.X, ALU.add),
          waits=[("LD", LD_T)])
        s(lambda: NCS.activation(p0_sq[:, :], p0_in[:, :], AF.Square),
          waits=[("LD", LD_T)])
        SN = P.now("S")
        v(lambda: NCV.tensor_reduce(p0_st[:, 1:2], p0_sq[:, :], AX.X, ALU.add),
          waits=[("S", SN)])
        v(lambda: NCV.tensor_scalar(p0_m[:, :], p0_st[:, 0:1], 1.0 / 2048, None, ALU.mult))
        v(lambda: NCV.scalar_tensor_tensor(p0_st[:, 2:3], p0_m[:, :], -1.0, p0_m[:, :],
                                           ALU.mult, ALU.mult))
        v(lambda: NCV.scalar_tensor_tensor(p0_st[:, 3:4], p0_st[:, 1:2], 1.0 / 2048,
                                           p0_st[:, 2:3], ALU.mult, ALU.add))
        v(lambda: NCV.tensor_scalar(p0_st[:, 3:4], p0_st[:, 3:4], LNEPS, None, ALU.add))
        VN = P.now("V")
        s(lambda: NCS.activation(p0_st[:, 4:5], p0_st[:, 3:4], AF.Ln),
          waits=[("V", VN)])
        s(lambda: NCS.activation(p0_r[:, :], p0_st[:, 4:5], AF.Exp, scale=-0.5),
          drain=True)
        SN = P.now("S")
        v(lambda: NCV.scalar_tensor_tensor(p0_m[:, :], p0_m[:, :], -1.0, p0_r[:, :],
                                           ALU.mult, ALU.mult), waits=[("S", SN)])
        v(lambda: NCV.tensor_scalar(p0_bf[:, :], p0_in[:, :], p0_r[:, :], p0_m[:, :],
                                    ALU.mult, ALU.add), waits=[("T", p0_tr_done)])
        v(lambda: NCV.tensor_copy(p0_hb[:, :], p0_in[:, :]))
        VN = P.now("V")
        ys(lambda it=it: NCY.dma_start(hbf[128 * it:128 * (it + 1), :], p0_hb[:, :]),
           waits=[("V", VN)])
        if DEBUG and it == 0:
            dbg_p0bf = nc.dram_tensor("dbg_p0bf", [128, 2048], BF16,
                                      kind="ExternalOutput").ap()
            ys(lambda: NCY.dma_start(dbg_p0bf, p0_bf[:, :]), waits=[("V", VN)])
            dbg_st = nc.dram_tensor("dbg_st", [128, 8], F32,
                                    kind="ExternalOutput").ap()
            ys(lambda: NCY.dma_start(dbg_st[:, 0:6], p0_st[:, :]), waits=[("V", VN)])
            ys(lambda: NCY.dma_start(dbg_st[:, 6:7], p0_m[:, :], allow_slow_non_contiguous=True), waits=[("V", VN)])
            ys(lambda: NCY.dma_start(dbg_st[:, 7:8], p0_r[:, :], allow_slow_non_contiguous=True), waits=[("V", VN)])
        for dd in range(16):
            t(lambda dd=dd: NCT.transpose(pTR[:, 128 * (dd % 4):128 * (dd % 4) + 128],
                                          p0_bf[:, 128 * dd:128 * (dd + 1)],
                                          ident_bf[:, :]),
              waits=[("V", VN), ("G", G_PRE), ("GC", 2)] if dd == 0 else
                    ([("V", P.now("V"))] if dd % 4 == 0 else []))
            if dd % 4 == 3:
                TN = P.now("T")
                v(lambda dd=dd: NCV.tensor_copy(p0_tr[:, 128 * (dd - 3):128 * (dd + 1)],
                                                pTR[:, 0:512]),
                  waits=[("T", TN), ("ST", p0_st_done)] if dd == 3 else [("T", TN)])
        VN = P.now("V")
        ys(lambda it=it: NCY.dma_start(
            ag1_in.rearrange("(k p) t -> p k t", p=128)[:, :, 128 * it:128 * (it + 1)],
            p0_tr[:, :].rearrange("p (k t) -> p k t", k=16)),
           waits=[("V", VN)])
        if DEBUG and it == 0:
            dbg_p0tr = nc.dram_tensor("dbg_p0tr", [128, 2048], BF16,
                                      kind="ExternalOutput").ap()
            ys(lambda: NCY.dma_start(dbg_p0tr, p0_tr[:, :]), waits=[("V", VN)])
        p0_tr_done = P.now("T")
        p0_st_done = P.now("ST")

    STN = P.now("ST")
    cc(lambda: NCG.collective_compute(
        "AllGather", ALU.bypass, replica_groups=[[0, 1, 2, 3], [4, 5, 6, 7]],
        ins=[ag1_in.opt()], outs=[ag1_out.opt()]),
       waits=[("ST", STN)])
    CC_AG1 = P.now("CC")

    win_mm_done = {}

    def gen_part1(w, ch, ci, hh):
        sl = hh % 2
        MA_sb, T_sb, Qg_sb, Rbuf, KTd_sb = MA_2[sl], T_2[sl], Qg_2[sl], Rbuf_2[sl], KTd_2[sl]
        Tp_sb, TTp_sb, ImM_sb = Tp_2[sl], TTp_2[sl], ImM_2[sl]
        koff = hh * 1024 + ci * 128
        qoff = hh * 1024 + 512 + ci * 128
        SN0 = P.now("S")
        VN0 = P.now("V")
        t(lambda koff=koff: NCT.matmul(
            pG[:, 0:256], qk_buf[:, koff:koff + 128],
            bass.AP(slot2, koff, [[16384, 128], [512, 2], [1, 128]]),
            start=True, stop=True), waits=[("S", SN0), ("V", VN0)])
        t(lambda hh=hh: NCT.matmul(
            pBC[:, 0:256], onesr_f[:, :], r4v[:, 0:2, hh, :],
            start=True, stop=True))
        TN = P.now("T")
        yield
        v(lambda: NCV.tensor_tensor(mexp_sb[:, :], pBC[:, 0:256], maskMA_f[:, :],
                                    ALU.add), waits=[("T", TN), ("S", P.now("S"))],
          nodrain=True)
        VN = P.now("V")
        s(lambda hh=hh: NCS.activation(EM_sb[:, :], mexp_sb[:, :], AF.Exp,
                                       bias=npsic[:, hh:hh + 1]), waits=[("V", VN)])
        SN = P.now("S")
        yield
        v(lambda MA_sb=MA_sb: NCV.tensor_tensor(MA_sb[:, :], EM_sb[:, :], pG[:, 0:256],
                                                ALU.mult),
          waits=[("S", SN)], nodrain=True)
        V_MA = P.now("V")
        t(lambda hh=hh: NCT.matmul(
            pBC[:, 256:512], onesr_f[:, :], r4v[:, 2:4, hh, :],
            start=True, stop=True), waits=[("V", V_MA)])
        TN = P.now("T")
        yield
        v(lambda: NCV.tensor_tensor(texp_sb[:, :], pBC[:, 256:384], maskT_f[:, :],
                                    ALU.add), waits=[("T", TN), ("S", P.now("S"))],
          nodrain=True)
        VN = P.now("V")
        s(lambda hh=hh: NCS.activation(Tpre_sb[:, :], texp_sb[:, :], AF.Exp,
                                       bias=phic[:, hh:hh + 1]), waits=[("V", VN)])
        SN = P.now("S")
        yield
        v(lambda T_sb=T_sb: NCV.tensor_tensor(T_sb[:, :], Tpre_sb[:, :], pG[:, 0:128],
                                              ALU.mult), waits=[("S", SN)], nodrain=True)
        v(lambda qoff=qoff, Qg_sb=Qg_sb: NCV.tensor_tensor(
            Qg_sb[:, :], qk_buf[:, qoff:qoff + 128], pBC[:, 384:512], ALU.mult),
          nodrain=True)
        V_TQ = P.now("V")
        t(lambda koff=koff: NCT.transpose(pTR[:, 0:128], qk_buf[:, koff:koff + 128],
                                          ident_bf[:, :]), waits=[("V", V_TQ)])
        t(lambda hh=hh, ci=ci: NCT.transpose(
            pTR[:, 128:256], v_buf[:, hh * 512 + ci * 128:hh * 512 + ci * 128 + 128],
            ident_bf[:, :]))
        TN = P.now("T")
        yield
        v(lambda hh=hh, Rbuf=Rbuf: NCV.tensor_scalar(
            Rbuf[:, 0:128], pTR[:, 128:256], betac[:, hh:hh + 1], None, ALU.mult),
          waits=[("T", TN)], nodrain=True)
        v(lambda hh=hh, Rbuf=Rbuf: NCV.tensor_scalar(
            Rbuf[:, 128:256], pTR[:, 0:128], expphic[:, hh:hh + 1], None, ALU.mult),
          nodrain=True)
        v(lambda hh=hh, KTd_sb=KTd_sb: NCV.tensor_scalar(
            KTd_sb[:, :], pTR[:, 0:128], kdc[:, hh:hh + 1], None, ALU.mult),
          nodrain=True)
        yield
        sq_src = [(T_sb, 0, MA_sb, 0)]
        for lvl in range(5):
            pTt, pTo, pTTt, pTTo = sq_src[-1]
            t(lambda pTt=pTt, pTo=pTo, pTTt=pTTt, pTTo=pTTo: NCT.matmul(
                pSQ[:, 0:128], pTTt[:, pTTo:pTTo + 128], pTt[:, pTo:pTo + 128],
                start=True, stop=True),
              waits=[("V", P.now("V"))])
            t(lambda pTt=pTt, pTo=pTo, pTTt=pTTt, pTTo=pTTo: NCT.matmul(
                pSQ[:, 128:256], pTt[:, pTo:pTo + 128], pTTt[:, pTTo:pTTo + 128],
                start=True, stop=True))
            TN = P.now("T")
            v(lambda lvl=lvl, Tp_sb=Tp_sb: NCV.tensor_copy(
                Tp_sb[:, 256 * lvl:256 * (lvl + 1)], pSQ[:, 0:256]),
              waits=[("T", TN)], nodrain=True)
            sq_src.append((Tp_sb, 256 * lvl, Tp_sb, 256 * lvl + 128))
            yield
        VN = P.now("V")
        t(lambda Tp_sb=Tp_sb: NCT.matmul(pSQ[:, 128:256], Tp_sb[:, 1024:1152],
                                         Tp_sb[:, 1152:1280],
                                         start=True, stop=True), waits=[("V", VN)])
        TN = P.now("T")
        v(lambda TTp_sb=TTp_sb: NCV.tensor_copy(TTp_sb[:, 0:128], pSQ[:, 128:256]),
          waits=[("T", TN)], nodrain=True)
        v(lambda ImM_sb=ImM_sb, MA_sb=MA_sb: NCV.tensor_tensor(
            ImM_sb[:, :], ident_bf[:, :], MA_sb[:, 0:128], ALU.subtract),
          nodrain=True)

    def gen_part2(w, ch, ci, hh):
        sl = hh % 2
        MA_sb, Qg_sb, Rbuf, KTd_sb = MA_2[sl], Qg_2[sl], Rbuf_2[sl], KTd_2[sl]
        Tp_sb, TTp_sb, ImM_sb = Tp_2[sl], TTp_2[sl], ImM_2[sl]
        ycur = Rbuf
        for li, lvl in enumerate((5, 4, 3, 2, 1, 0)):
            stap = (TTp_sb[:, 0:128] if lvl == 5
                    else Tp_sb[:, 256 * lvl + 128:256 * lvl + 256])
            t(lambda stap=stap, ycur=ycur, li=li: NCT.matmul(
                pY[:, :], stap, ycur[:, 0:256],
                start=(li == 0), stop=(li > 0)),
              waits=[("V", P.now("V"))])
            if li == 0:
                t(lambda ycur=ycur: NCT.matmul(pY[:, :], ident_bf[:, :],
                                               ycur[:, 0:256],
                                               start=False, stop=True))
            TN = P.now("T")
            ynext = Ybuf[li % 2]
            v(lambda ynext=ynext: NCV.tensor_copy(ynext[:, :], pY[:, :]),
              waits=[("T", TN)], nodrain=True)
            ycur = ynext
            yield
        VN = P.now("V")
        t(lambda ycur=ycur, ImM_sb=ImM_sb: NCT.matmul(
            pY[:, :], ImM_sb[:, :], ycur[:, 0:256],
            start=True, stop=True), waits=[("V", VN)])
        TN = P.now("T")
        v(lambda: NCV.tensor_copy(U_sb[:, :], pY[:, :]), waits=[("T", TN)],
          nodrain=True)
        V_U = P.now("V")
        yield
        t(lambda KTd_sb=KTd_sb: NCT.matmul(pY[:, 0:128], U_sb[:, 128:256], KTd_sb[:, :],
                                           start=True, stop=True), waits=[("V", V_U)])
        t(lambda MA_sb=MA_sb: NCT.matmul(pY[:, 128:256], U_sb[:, 128:256],
                                         MA_sb[:, 128:256], start=True, stop=True))
        TN = P.now("T")
        v(lambda hh=hh: NCV.tensor_scalar(Isc_sb[:, :], ident_bf[:, :],
                                          expgend[:, hh:hh + 1], None, ALU.mult),
          nodrain=True)
        v(lambda: NCV.tensor_tensor(PhiT_sb[:, :], Isc_sb[:, :], pY[:, 0:128],
                                    ALU.subtract), waits=[("T", TN)])
        v(lambda Qg_sb=Qg_sb: NCV.tensor_tensor(QtT_sb[:, :], Qg_sb[:, :],
                                                pY[:, 128:256], ALU.subtract),
          nodrain=True)
        V_C = P.now("V")
        yield
        t(lambda MA_sb=MA_sb: NCT.matmul(pOS[:, 0:128], U_sb[:, 0:128],
                                         MA_sb[:, 128:256], start=True, stop=False),
          waits=[("V", V_C)])
        t(lambda hh=hh: NCT.matmul(pOS[:, 0:128], S0_sb[:, 128 * hh:128 * (hh + 1)],
                                   QtT_sb[:, :], start=False, stop=True))
        t(lambda hh=hh: NCT.matmul(pOS[:, 128:256], PhiT_sb[:, :],
                                   S0_sb[:, 128 * hh:128 * (hh + 1)],
                                   start=True, stop=False))
        t(lambda KTd_sb=KTd_sb: NCT.matmul(pOS[:, 128:256], KTd_sb[:, :],
                                           U_sb[:, 0:128], start=False, stop=True))
        TN = P.now("T")
        yield
        v(lambda: NCV.tensor_copy(Ot_sb[:, :], pOS[:, 0:128]), waits=[("T", TN)],
          nodrain=True)
        v(lambda hh=hh: NCV.tensor_copy(S0_sb[:, 128 * hh:128 * (hh + 1)],
                                        pOS[:, 128:256]), nodrain=True)
        v(lambda: NCV.tensor_tensor(Osq_sb[:, :], Ot_sb[:, :], Ot_sb[:, :], ALU.mult))
        VN = P.now("V")
        yield
        t(lambda: NCT.matmul(pMISC[0:1, 256:384], onesc_bf[:, :], Osq_sb[:, :],
                             start=True, stop=True), waits=[("V", VN)])
        TN = P.now("T")
        v(lambda: NCV.tensor_scalar(msr_sb[:, :], pMISC[0:1, 256:384], 1.0 / 128,
                                    EPS, ALU.mult, ALU.add),
          waits=[("T", TN), ("S", P.now("S"))], nodrain=True)
        VNx = P.now("V")
        s(lambda: NCS.activation(msr_sb[:, :], msr_sb[:, :], AF.Ln),
          waits=[("V", VNx)])
        s(lambda: NCS.activation(scr_row[:, 0:128], msr_sb[:, :], AF.Exp, scale=-0.5),
          waits=[("V", P.now("V"))], drain=True)
        SNx = P.now("S")
        yield
        v(lambda: NCV.tensor_copy(rstd_sb[:, :], scr_row[:, 0:128]), waits=[("S", SNx)])
        VNr = P.now("V")
        t(lambda: NCT.matmul(pMISC[:, 384:512], onesr_f[:, :], rstd_sb[:, :],
                             start=True, stop=True), waits=[("V", VNr)])
        TN = P.now("T")
        yield
        v(lambda: NCV.tensor_tensor(gat_sb[:, :], Ot_sb[:, :], pMISC[:, 384:512],
                                    ALU.mult), waits=[("T", TN)], nodrain=True)
        v(lambda hh=hh, ci=ci: NCV.scalar_tensor_tensor(
            coreo[:, hh * 512 + ci * 128:hh * 512 + ci * 128 + 128],
            gat_sb[:, :], normw_c[:, :],
            z_buf[:, hh * 512 + ci * 128:hh * 512 + ci * 128 + 128],
            ALU.mult, ALU.mult))

    def _drive(gens):
        gens = [g for g in gens if g is not None]
        while gens:
            alive = []
            for g in gens:
                try:
                    next(g)
                    alive.append(g)
                except StopIteration:
                    pass
            gens = alive

    def emit_chunk(w, ch, ci):
        csl = slice(128 * ci, 128 * (ci + 1))
        gsl = slice(128 * ci, 128 * (ci + 1))
        VN = P.now("V")
        SN = P.now("S")
        for idx, rsrc in enumerate((g_r, c1_r, c2_r, beta_r, c3_r)):
            t(lambda idx=idx, rsrc=rsrc, gsl=gsl: NCT.transpose(
                pMISC[:, 4 * idx:4 * idx + 4], rsrc[0:4, gsl],
                ident_f[0:4, 0:4]),
              waits=[("V", VN), ("S", SN)] if idx == 0 else [])
        TN = P.now("T")
        v(lambda: NCV.tensor_copy(cols_sb[:, :], pMISC[:, 0:20]), waits=[("T", TN)], nodrain=True)
        VN = P.now("V")
        t(lambda: NCT.matmul(pMISC[:, 32:36], triu_f[:, :], cols_sb[:, 0:4],
                             start=True, stop=True), waits=[("V", VN)])
        t(lambda: NCT.matmul(pMISC[0:1, 36:40], onesc_f[:, :], cols_sb[:, 0:4],
                             start=True, stop=True))
        TN = P.now("T")
        v(lambda: NCV.tensor_copy(gcols_sb[:, :], pMISC[:, 32:36]), waits=[("T", TN)], nodrain=True)
        v(lambda: NCV.tensor_copy(gendr_sb[:, :], pMISC[0:1, 36:40]), nodrain=True)
        VN = P.now("V")
        for hh in range(4):
            t(lambda hh=hh: NCT.matmul(pBC[0:1, 128 * hh:128 * (hh + 1)],
                                       cols_sb[:, hh:hh + 1], triu_f[:, :],
                                       start=True, stop=True),
              waits=[("V", VN)] if hh == 0 else [])
        TN = P.now("T")
        v(lambda: NCV.tensor_copy(growb[:, :], pBC[0:1, 0:512]), waits=[("T", TN)], nodrain=True)
        for k_i, coff in ((0, 4), (1, 8), (2, 16)):
            VN = P.now("V")
            for hh in range(4):
                t(lambda coff=coff, hh=hh: NCT.matmul(
                    pBC[0:1, 128 * hh:128 * (hh + 1)],
                    cols_sb[:, coff + hh:coff + hh + 1], ident_f[:, :],
                    start=True, stop=True),
                  waits=[("V", VN)] if hh == 0 else [])
            TN = P.now("T")
            v(lambda k_i=k_i: NCV.tensor_copy(crowb[:, 512 * k_i:512 * (k_i + 1)],
                                              pBC[0:1, 0:512]), waits=[("T", TN)], nodrain=True)
        v(lambda: NCV.tensor_tensor(rows4[:, 0:512], growb[:, :], crowb[:, 0:512],
                                    ALU.add), waits=[("T", P.now("T")), ("S", P.now("S"))])
        v(lambda: NCV.tensor_tensor(rows4[:, 512:1024], growb[:, :],
                                    crowb[:, 1024:1536], ALU.add), nodrain=True)
        v(lambda: NCV.tensor_tensor(rows4[:, 1024:1536], crowb[:, 512:1024],
                                    growb[:, :], ALU.subtract), nodrain=True)
        VN = P.now("V")
        s(lambda: NCS.activation(scr_row[:, :], rows4[:, 512:1024], AF.Exp),
          waits=[("V", VN)])
        SNx = P.now("S")
        v(lambda: NCV.tensor_copy(rows4[:, 1536:2048], scr_row[:, :]),
          waits=[("S", SNx)])
        v(lambda: NCV.tensor_tensor(npsic[:, :], cols_sb[:, 8:12], gcols_sb[:, :],
                                    ALU.subtract), waits=[("S", P.now("S"))], nodrain=True)
        v(lambda: NCV.tensor_tensor(phic[:, :], gcols_sb[:, :], cols_sb[:, 4:8],
                                    ALU.add), nodrain=True)
        v(lambda: NCV.tensor_copy(betac[:, :], cols_sb[:, 12:16]), nodrain=True)
        VN = P.now("V")
        s(lambda: NCS.activation(expphic[:, :], phic[:, :], AF.Exp), waits=[("V", VN)])
        t(lambda: NCT.matmul(pMISC[:, 40:44], onesr_f[:, :], gendr_sb[:, :],
                             start=True, stop=True), waits=[("V", VN)])
        TN = P.now("T")
        v(lambda: NCV.tensor_tensor(kdc[:, :], pMISC[:, 40:44], npsic[:, :], ALU.add),
          waits=[("T", TN)])
        VN = P.now("V")
        s(lambda: NCS.activation(kdc[:, :], kdc[:, :], AF.Exp), waits=[("V", VN)])
        s(lambda: NCS.activation(expgend[:, :], pMISC[:, 40:44], AF.Exp))
        p1 = gen_part1(w, ch, ci, 0)
        for _ in p1:
            pass
        for hh in range(4):
            if SEQDRV:
                nxt = gen_part1(w, ch, ci, hh + 1) if hh < 3 else None
                for _ in gen_part2(w, ch, ci, hh):
                    pass
                if nxt is not None:
                    for _ in nxt:
                        pass
            else:
                nxt = gen_part1(w, ch, ci, hh + 1) if hh < 3 else None
                _drive([gen_part2(w, ch, ci, hh), nxt])

    def emit_window(w):
        r = w // NWQ
        tloc = 512 * (w % NWQ)
        buf = lw[w % 2]
        ldn = P.now("LD")
        lwait = [("CC", CC_AG1)]
        if w >= 2:
            lwait.append(("T", win_mm_done[w - 2]))
        yd(lambda r=r, tloc=tloc, buf=buf: NCY.dma_start(
            buf[:, :].rearrange("p (k t) -> p k t", k=16),
            bass.AP(ag1_out.tensor, 2048 * r * NQ + tloc, [[NQ, 128], [128 * NQ, 16], [1, 512]])),
           waits=lwait)
        LD_T = ldn + 16

        pA_free = 0
        for ci16 in range(16):
            for kk in range(16):
                wl = []
                if kk == 0:
                    wl = [("LD", LD_T), ("V", pA_free), ("G", G_PRE), ("GC", 2)]
                t(lambda ci16=ci16, kk=kk, buf=buf: NCT.matmul(
                    pA[:, :], Wqkv_sb[:, kk * 2048 + ci16 * 128:kk * 2048 + ci16 * 128 + 128],
                    buf[:, kk * 512:(kk + 1) * 512],
                    start=(kk == 0), stop=(kk == 15)), waits=wl)
            TN = P.now("T")
            hh = ci16 % 4
            if ci16 < 12:
                v(lambda: NCV.tensor_copy(mixed_sb[:, 3:515], pA[:, :]),
                  waits=[("T", TN)], nodrain=True)
                v(lambda ci16=ci16: NCV.tensor_copy(mixed_sb[:, 0:3],
                                                    tail_sb[:, 3 * ci16:3 * ci16 + 3]), nodrain=True)
                v(lambda ci16=ci16: NCV.tensor_copy(tail_sb[:, 3 * ci16:3 * ci16 + 3],
                                                    mixed_sb[:, 512:515]))
                v(lambda ci16=ci16: NCV.tensor_scalar(
                    acc_f[:, :], mixed_sb[:, 0:512],
                    convc_sb[:, 4 * ci16:4 * ci16 + 1], None, ALU.mult), nodrain=True)
                v(lambda ci16=ci16: NCV.tensor_scalar(
                    sq_tmp[:, :], mixed_sb[:, 1:513],
                    convc_sb[:, 4 * ci16 + 1:4 * ci16 + 2], None, ALU.mult),
                  nodrain=True)
                v(lambda ci16=ci16: NCV.scalar_tensor_tensor(
                    acc_f[:, :], mixed_sb[:, 2:514],
                    convc_sb[:, 4 * ci16 + 2:4 * ci16 + 3],
                    acc_f[:, :], ALU.mult, ALU.add))
                v(lambda ci16=ci16: NCV.scalar_tensor_tensor(
                    sq_tmp[:, :], mixed_sb[:, 3:515],
                    convc_sb[:, 4 * ci16 + 3:4 * ci16 + 4],
                    sq_tmp[:, :], ALU.mult, ALU.add), nodrain=True)
                v(lambda: NCV.tensor_tensor(acc_f[:, :], acc_f[:, :], sq_tmp[:, :],
                                            ALU.add))
                VN = P.now("V")
                s(lambda: NCS.activation(tanh_sb[:, :], acc_f[:, :], AF.Tanh,
                                         scale=0.5), waits=[("V", VN)])
                SN = P.now("S")
                v(lambda: NCV.tensor_scalar(tanh_sb[:, :], tanh_sb[:, :], 0.5, 0.5,
                                            ALU.mult, ALU.add), waits=[("S", SN)])
                if ci16 < 4:
                    dst, off = qk_buf, hh * 1024 + 512
                elif ci16 < 8:
                    dst, off = qk_buf, hh * 1024
                else:
                    dst, off = v_buf, hh * 512
                v(lambda dst=dst, off=off: NCV.tensor_tensor(
                    dst[:, off:off + 512], acc_f[:, :], tanh_sb[:, :], ALU.mult))
                pA_free = P.now("V")
                if ci16 < 8:
                    v(lambda dst=dst, off=off: NCV.tensor_tensor(
                        sq_tmp[:, :], dst[:, off:off + 512], dst[:, off:off + 512],
                        ALU.mult), waits=[("T", P.now("T"))])
                    VN = P.now("V")
                    t(lambda hh=hh: NCT.matmul(
                        pMISC[0:4, 0:512], eye4_bf[:, 4 * hh:4 * hh + 4], sq_tmp[:, :],
                        start=(hh == 0), stop=(hh == 3)), waits=[("V", VN)])
                    if hh == 3:
                        TN2 = P.now("T")
                        dstr = ssq_r if ci16 < 4 else ssk_r
                        v(lambda dstr=dstr: NCV.tensor_copy(
                            dstr[0:4, 0:512], pMISC[0:4, 0:512]),
                          waits=[("T", TN2)], nodrain=True)
            else:
                s(lambda: NCS.activation(tanh_sb[:, :], pA[:, :], AF.Tanh, scale=0.5),
                  waits=[("T", TN), ("V", P.now("V"))])
                SN = P.now("S")
                v(lambda: NCV.tensor_scalar(tanh_sb[:, :], tanh_sb[:, :], 0.5, 0.5,
                                            ALU.mult, ALU.add), waits=[("S", SN)])
                v(lambda hh=hh: NCV.tensor_tensor(z_buf[:, hh * 512:(hh + 1) * 512],
                                                  pA[:, :], tanh_sb[:, :], ALU.mult))
                pA_free = P.now("V")

        for Wsb, dstr in ((Wb_sb, bpre_r), (Wa_sb, gpre_r)):
            VN = P.now("V")
            for kk in range(16):
                t(lambda Wsb=Wsb, kk=kk, buf=buf: NCT.matmul(
                    pMISC[0:4, 0:512], Wsb[:, kk * 4:(kk + 1) * 4],
                    buf[:, kk * 512:(kk + 1) * 512],
                    start=(kk == 0), stop=(kk == 15)),
                  waits=[("V", VN)] if kk == 0 else [])
            TN = P.now("T")
            v(lambda dstr=dstr: NCV.tensor_copy(dstr[0:4, 0:512],
                                                     pMISC[0:4, 0:512]),
              waits=[("T", TN)], nodrain=True)
        win_mm_done[w] = P.now("T")

        wsl = slice(0, 512)
        VN = P.now("V")
        s(lambda wsl=wsl: NCS.activation(tmp_r[0:4, :], bpre_r[0:4, wsl], AF.Tanh,
                                         scale=0.5),
          waits=[("V", VN), ("T", P.now("T"))])
        SN = P.now("S")
        v(lambda wsl=wsl: NCV.tensor_scalar(beta_r[0:4, wsl], tmp_r[0:4, :], 0.5, 0.5,
                                            ALU.mult, ALU.add), waits=[("S", SN)])
        VN = P.now("V")
        s(lambda wsl=wsl: NCS.activation(tmp_r[0:4, :], beta_r[0:4, wsl], AF.Ln),
          waits=[("V", VN)])
        v(lambda wsl=wsl: NCV.tensor_scalar(tmp2_r[0:4, :], ssk_r[0:4, wsl], EPS,
                                            None, ALU.add))
        VNx = P.now("V")
        s(lambda: NCS.activation(tmp2_r[0:4, :], tmp2_r[0:4, :], AF.Ln),
          waits=[("V", VNx)])
        SN = P.now("S")
        v(lambda wsl=wsl: NCV.tensor_scalar(c2_r[0:4, wsl], tmp2_r[0:4, :], -0.5, None,
                                            ALU.mult), waits=[("S", SN)])
        v(lambda wsl=wsl: NCV.scalar_tensor_tensor(c1_r[0:4, wsl], tmp2_r[0:4, :],
                                                   -0.5, tmp_r[0:4, :], ALU.mult,
                                                   ALU.add))
        VN = P.now("V")
        v(lambda wsl=wsl: NCV.tensor_scalar(tmp2_r[0:4, :], ssq_r[0:4, wsl], EPS,
                                            None, ALU.add), waits=[("V", VN)])
        VNx = P.now("V")
        s(lambda: NCS.activation(tmp2_r[0:4, :], tmp2_r[0:4, :], AF.Ln),
          waits=[("V", VNx)])
        SN = P.now("S")
        v(lambda wsl=wsl: NCV.tensor_scalar(c3_r[0:4, wsl], tmp2_r[0:4, :], -0.5,
                                            -0.5 * float(np.log(128.0)), ALU.mult,
                                            ALU.add), waits=[("S", SN)])
        v(lambda wsl=wsl: NCV.tensor_scalar(tmp2_r[0:4, :], gpre_r[0:4, wsl],
                                            dtb_c[0:4, :], None, ALU.add))
        VN = P.now("V")
        s(lambda: NCS.activation(tmp_r[0:4, :], tmp2_r[0:4, :], AF.Abs),
          waits=[("V", VN)])
        s(lambda: NCS.activation(tmp_r[0:4, :], tmp_r[0:4, :], AF.Exp, scale=-1.0),
          drain=True)
        SNx = P.now("S")
        v(lambda: NCV.tensor_scalar(tmp_r[0:4, :], tmp_r[0:4, :], 1.0, None, ALU.add),
          waits=[("S", SNx)])
        VNx = P.now("V")
        s(lambda: NCS.activation(tmp_r[0:4, :], tmp_r[0:4, :], AF.Ln),
          waits=[("V", VNx)])
        SN = P.now("S")
        v(lambda: NCV.tensor_scalar(tmp2_r[0:4, :], tmp2_r[0:4, :], 0.0, None,
                                    ALU.max), waits=[("S", SN)])
        v(lambda: NCV.tensor_tensor(tmp_r[0:4, :], tmp_r[0:4, :], tmp2_r[0:4, :],
                                    ALU.add))
        v(lambda wsl=wsl: NCV.tensor_scalar(g_r[0:4, wsl], tmp_r[0:4, :],
                                            negexpA[0:4, :], None, ALU.mult),
          waits=[("V", V_PRE)])

        for ci in range(4):
            emit_chunk(w, 4 * w + ci, ci)

        VN = P.now("V")
        ys(lambda w=w: NCY.dma_start(
            ag2_in.rearrange("(k p) t -> p k t", p=128)[:, :, 512 * w:512 * (w + 1)],
            coreo[:, :].rearrange("p (k t) -> p k t", k=4)), waits=[("V", VN)])
        STN = P.now("ST")
        P.op("v", lambda: NCV.engine_nop(), [("ST", STN)], None)

    for w in range(NW):
        emit_window(w)

    STN = P.now("ST")
    cc(lambda: NCG.collective_compute(
        "AllGather", ALU.bypass, replica_groups=[[0, 1, 2, 3], [4, 5, 6, 7]],
        ins=[ag2_in.opt()], outs=[ag2_out.opt()]), waits=[("ST", STN)])
    CC_AG2 = P.now("CC")
    if RELAX < 2:
        P.CONS = P.CONS_FULL

    T_P2 = P.now("T")
    gd(lambda: NCG.dma_start(Wqkv_sb[:, :].rearrange("p (k c) -> p k c", k=16),
                             Wout_in.rearrange("(k p) c -> p k c", p=128)),
       waits=[("T", T_P2)])
    G_WOUT = P.now("G")
    gd(lambda: NCG.dma_start(w1T_sb[:, :], w1T_in))
    gd(lambda: NCG.dma_start(w2T_sb[:, :], w2T_in))
    gd(lambda: NCG.dma_start(b1T_sb[:, :], b1T_in))
    gd(lambda: NCG.dma_start(b2T_sb[:, :], b2T_in))
    gd(lambda: NCG.dma_start(swb_sb[:, :], swb_in))
    gd(lambda: NCG.dma_start(emask_sb[:, 0:8], emask0_in))
    gd(lambda: NCG.dma_start(emask_sb[:, 8:16], emask1_in))
    gd(lambda: NCG.dma_start(b3_sb[:, :], b3_in))
    G_W3X = P.now("G")
    ys(lambda: NCY.dma_start(offr_sb[:, :], offtok_in))
    OFF_LD = P.now("LD") + P.now("ST")
    ST_OFF = P.now("ST")
    hold = {}

    def _load_off():
        reg = NCY.alloc_register("offreg")
        NCY.reg_load(reg, offr_sb[0:1, 0:1])
        hold["off"] = NCY.snap(reg)
        return NCY.nop()
    P.op("y", _load_off, [("ST", ST_OFF)], None)

    def ln_pair_stats(src_getter, n_cc, dst_ps1, dst_ps2, first_waits):
        pass

    def emit_p3a_window(wi):
        ldn = P.now("LD")
        wl = [("CC", CC_AG2)]
        if wi > 0:
            wl.append(("ST", hold["p3a_st_%d" % (wi - 1)]))

        def ld_core(wi=wi):
            off = hold["off"]
            src = bass.AP(ag2_out, 0, [[S_, 128], [128 * S_, 16], [1, 512]],
                          ) if False else None
            view = ag2_out.rearrange("(k p) t -> p k t", p=128)
            return NCY.dma_start(core_w[:, :].rearrange("p (k t) -> p k t", k=16),
                                 view[:, :, bass.ds(off + 512 * wi, 512)])
        yd(ld_core, waits=wl)
        for dd in range(16):
            yd(lambda dd=dd, wi=wi: NCY.dma_start(
                hbf_w[:, 512 * dd:512 * (dd + 1)],
                hbf[512 * wi:512 * (wi + 1), 128 * dd:128 * (dd + 1)],
                transpose=True))
        LD_T = ldn + 16 * 17

        for ci16 in range(16):
            for kk in range(16):
                wl = []
                if kk == 0:
                    wl = [("LD", LD_T), ("G", G_WOUT),
                          ("V", hold.get("p3a_pa", 0))]
                t(lambda ci16=ci16, kk=kk: NCT.matmul(
                    pA[:, :],
                    Wqkv_sb[:, kk * 2048 + ci16 * 128:kk * 2048 + ci16 * 128 + 128],
                    core_w[:, 512 * kk:512 * (kk + 1)],
                    start=(kk == 0), stop=(kk == 15)), waits=wl)
            TN = P.now("T")
            evw = [("T", TN)]
            if ci16 == 0 and wi > 0:
                evw.append(("ST", hold["p3a_st_%d" % (wi - 1)]))
            v(lambda ci16=ci16: NCV.tensor_copy(attn_sb[:, 512 * ci16:512 * (ci16 + 1)],
                                                pA[:, :]), waits=evw, nodrain=True)
            VN = P.now("V")
            hold["p3a_pa"] = VN
            v(lambda ci16=ci16: NCV.tensor_tensor(
                asq_sb[:, :], attn_sb[:, 512 * ci16:512 * (ci16 + 1)],
                attn_sb[:, 512 * ci16:512 * (ci16 + 1)], ALU.mult),
              waits=[("T", P.now("T"))])
            VN2 = P.now("V")
            t(lambda ci16=ci16: NCT.matmul(
                pMISC[0:1, 0:512], onesc_bf[:, :], attn_sb[:, 512 * ci16:512 * (ci16 + 1)],
                start=(ci16 == 0), stop=(ci16 == 15)), waits=[("V", VN2)])
            t(lambda ci16=ci16: NCT.matmul(
                pBC[0:1, 0:512], onesc_bf[:, :], asq_sb[:, :],
                start=(ci16 == 0), stop=(ci16 == 15)))
        TN = P.now("T")

        def stats_rows(TN):
            v(lambda: NCV.tensor_scalar(murow[:, :], pMISC[0:1, 0:512], 1.0 / 2048,
                                        None, ALU.mult), waits=[("T", TN)])
            v(lambda: NCV.tensor_scalar(tmpsr[:, :], pBC[0:1, 0:512], 1.0 / 2048,
                                        None, ALU.mult))
            v(lambda: NCV.scalar_tensor_tensor(varow[:, :], murow[:, :], -1.0,
                                               murow[:, :], ALU.mult, ALU.mult))
            v(lambda: NCV.tensor_tensor(varow[:, :], tmpsr[:, :], varow[:, :],
                                        ALU.add))
            v(lambda: NCV.tensor_scalar(varow[:, :], varow[:, :], LNEPS, None,
                                        ALU.add))
            VN = P.now("V")
            s(lambda: NCS.activation(varow[:, :], varow[:, :], AF.Ln),
              waits=[("V", VN)])
            s(lambda: NCS.activation(scr_row[:, :], varow[:, :], AF.Exp, scale=-0.5),
              waits=[("V", P.now("V"))], drain=True)
            SNx = P.now("S")
            v(lambda: NCV.tensor_copy(r1row[:, :], scr_row[:, :]), waits=[("S", SNx)])
            SN = P.now("S")
            t(lambda: NCT.matmul(pMISC[:, 0:512], onesr_f[:, :], murow[:, :],
                                 start=True, stop=True), waits=[("S", SN)])
            t(lambda: NCT.matmul(pBC[:, 0:512], onesr_f[:, :], r1row[:, :],
                                 start=True, stop=True))
            return P.now("T")

        TN = stats_rows(TN)
        for ci16 in range(16):
            v(lambda ci16=ci16: NCV.tensor_tensor(
                x1_w[:, 512 * ci16:512 * (ci16 + 1)],
                attn_sb[:, 512 * ci16:512 * (ci16 + 1)], pMISC[:, 0:512],
                ALU.subtract), waits=[("T", TN)] if ci16 == 0 else [])
            v(lambda ci16=ci16: NCV.tensor_tensor(
                x1_w[:, 512 * ci16:512 * (ci16 + 1)],
                x1_w[:, 512 * ci16:512 * (ci16 + 1)], pBC[:, 0:512], ALU.mult))
            v(lambda ci16=ci16: NCV.tensor_tensor(
                x1_w[:, 512 * ci16:512 * (ci16 + 1)],
                x1_w[:, 512 * ci16:512 * (ci16 + 1)],
                hbf_w[:, 512 * ci16:512 * (ci16 + 1)], ALU.add))
        V_X1 = P.now("V")
        for ci16 in range(16):
            v(lambda ci16=ci16: NCV.tensor_copy(stat_bf[:, 0:512],
                                                x1_w[:, 512 * ci16:512 * (ci16 + 1)]),
              waits=[("T", P.now("T"))], nodrain=True)
            VN = P.now("V")
            v(lambda: NCV.tensor_tensor(stat_bf[:, 512:1024], stat_bf[:, 0:512],
                                        stat_bf[:, 0:512], ALU.mult))
            VN2 = P.now("V")
            t(lambda ci16=ci16: NCT.matmul(
                pMISC[0:1, 0:512], onesc_bf[:, :], stat_bf[:, 0:512],
                start=(ci16 == 0), stop=(ci16 == 15)), waits=[("V", VN2)])
            t(lambda ci16=ci16: NCT.matmul(
                pBC[0:1, 0:512], onesc_bf[:, :], stat_bf[:, 512:1024],
                start=(ci16 == 0), stop=(ci16 == 15)))
        TN = P.now("T")
        TN = stats_rows(TN)
        for ci16 in range(16):
            v(lambda ci16=ci16: NCV.tensor_tensor(
                ln2_w[:, 512 * ci16:512 * (ci16 + 1)],
                x1_w[:, 512 * ci16:512 * (ci16 + 1)], pMISC[:, 0:512], ALU.subtract),
              waits=[("T", TN)] if ci16 == 0 else [])
            v(lambda ci16=ci16: NCV.tensor_tensor(
                ln2_w[:, 512 * ci16:512 * (ci16 + 1)],
                ln2_w[:, 512 * ci16:512 * (ci16 + 1)], pBC[:, 0:512], ALU.mult))
        V_LN2 = P.now("V")
        ys(lambda wi=wi: NCY.dma_start(
            bass.AP(x1_sp.tensor, 512 * wi, [[NQ, 128], [128 * NQ, 16], [1, 512]]),
            x1_w[:, :].rearrange("p (k t) -> p k t", k=16)), waits=[("V", V_X1)])
        ys(lambda wi=wi: NCY.dma_start(
            bass.AP(ln2_sp.tensor, 512 * wi, [[NQ, 128], [128 * NQ, 16], [1, 512]]),
            ln2_w[:, :].rearrange("p (k t) -> p k t", k=16)), waits=[("V", V_LN2)])
        hold["p3a_st_%d" % wi] = P.now("ST")

    for wi in range(NW3):
        emit_p3a_window(wi)

    STN = P.now("ST")
    gd(lambda: NCG.dma_start(agr_in, ln2_sp[:, 0:RT]), waits=[("ST", STN)])
    GN = P.now("G")
    cc(lambda: NCG.collective_compute(
        "AllGather", ALU.bypass, replica_groups=[[0, 1, 2, 3], [4, 5, 6, 7]],
        ins=[agr_in.opt()], outs=[agr_out.opt()]), waits=[("G", GN)])
    CC_AGR = P.now("CC")

    for e in range(2):
        for cc16 in range(16):
            ldn = P.now("LD")
            swwl = [("V", P.now("V"))] if (e, cc16) >= (0, 2) else []
            if e == 0 and cc16 < 2:
                swwl = [("CC", CC_AGR)]
            gd(lambda e=e, cc16=cc16: NCG.dma_start(
                sw_sb[cc16 % 2][:, 0:RT],
                switch_in[128 * cc16:128 * (cc16 + 1), RT * e:RT * (e + 1)]),
               waits=swwl)
            GN = P.now("G")
            yd(lambda cc16=cc16: NCY.dma_start(
                rtl_sb[:, 0:RT], agr_out[128 * cc16:128 * (cc16 + 1), 0:RT]),
               waits=[("CC", CC_AGR)])
            LDN = P.now("LD")
            for tt4 in range(RT // 512):
                v(lambda cc16=cc16, tt4=tt4: NCV.tensor_tensor(
                    prod_sb[:, :], rtl_sb[:, 512 * tt4:512 * (tt4 + 1)],
                    sw_sb[cc16 % 2][:, 512 * tt4:512 * (tt4 + 1)], ALU.mult),
                  waits=([("G", GN), ("LD", LDN)] if tt4 == 0 else [])
                  + [("T", P.now("T"))])
                VN = P.now("V")
                t(lambda e=e, cc16=cc16, tt4=tt4: NCT.matmul(
                    pMISC[0:1, 0:512] if e == 0 else pBC[0:1, 0:512],
                    onesc_bf[:, :], prod_sb[:, :],
                    start=(cc16 == 0 and tt4 == 0),
                    stop=(cc16 == 15 and tt4 == RT // 512 - 1)),
                  waits=[("V", VN)])
    TN = P.now("T")
    v(lambda: NCV.tensor_reduce(lgsc[:, 0:1], pMISC[0:1, 0:512], AX.X, ALU.add),
      waits=[("T", TN)])
    v(lambda: NCV.tensor_reduce(lgsc[:, 1:2], pBC[0:1, 0:512], AX.X, ALU.add))
    v(lambda: NCV.tensor_copy(dbgk[:, 0:2], lgsc[:, 0:2]))
    v(lambda: NCV.tensor_scalar(lgrow[:, :], emask_sb[:, 0:8], lgsc[:, 0:1], None,
                                ALU.mult))
    v(lambda: NCV.scalar_tensor_tensor(lgrow[:, :], emask_sb[:, 8:16], lgsc[:, 1:2],
                                       lgrow[:, :], ALU.mult, ALU.add))
    VN = P.now("V")
    gd(lambda: NCG.dma_start(lgt_in, lgrow[:, :]), waits=[("V", VN)])
    GN = P.now("G")
    cc(lambda: NCG.collective_compute(
        "AllReduce", ALU.add, replica_groups=[[0, 1, 2, 3], [4, 5, 6, 7]],
        ins=[lgt_in.opt()], outs=[lgt_out.opt()]), waits=[("G", GN)])
    CC_LGT = P.now("CC")
    gd(lambda: NCG.dma_start(lgrow[:, :], lgt_out), waits=[("CC", CC_LGT)])
    G_LGT = P.now("G")
    v(lambda: NCV.tensor_tensor(lgrow[:, :], lgrow[:, :], swb_sb[:, :], ALU.add),
      waits=[("G", G_LGT)])
    v(lambda: NCV.tensor_copy(dbgk[:, 8:16], lgrow[:, :]))
    v(lambda: NCV.tensor_reduce(lgsc[:, 0:1], lgrow[:, :], AX.X, ALU.max))
    v(lambda: NCV.tensor_scalar(prob_sb[:, :], lgrow[:, :], lgsc[:, 0:1], None,
                                ALU.subtract))
    VN = P.now("V")
    s(lambda: NCS.activation(prob_sb[:, :], prob_sb[:, :], AF.Exp), waits=[("V", VN)])
    SN = P.now("S")
    v(lambda: NCV.tensor_reduce(lgsc[:, 1:2], prob_sb[:, :], AX.X, ALU.add),
      waits=[("S", SN)])
    v(lambda: NCV.reciprocal(lgsc[:, 2:3], lgsc[:, 1:2]))
    v(lambda: NCV.tensor_scalar(prob_sb[:, :], prob_sb[:, :], lgsc[:, 2:3], None,
                                ALU.mult))
    v(lambda: NCV.tensor_copy(lgrow[:, :], prob_sb[:, :]))
    VN = P.now("V")
    t(lambda: NCT.transpose(pMISC[0:8, 16:17], lgrow[:, :], ident_f[0:1, 0:1]),
      waits=[("V", VN)])
    TN = P.now("T")
    v(lambda: NCV.tensor_copy(probc_sb[:, :], pMISC[0:8, 16:17]), waits=[("T", TN)])
    VN = P.now("V")
    t(lambda: NCT.matmul(pMISC[0:8, 20:28], onesr_f[:, 0:8], prob_sb[:, :],
                         start=True, stop=True), waits=[("V", VN)])
    TN = P.now("T")
    v(lambda: NCV.tensor_scalar(cmp_sb[:, :], pMISC[0:8, 20:28], probc_sb[:, :], None,
                                ALU.is_gt), waits=[("T", TN)])
    VN = P.now("V")
    t(lambda: NCT.matmul(pMISC[0:1, 28:36], onesc_f[0:8, :],
                         cmp_sb[:, :], start=True, stop=True), waits=[("V", VN)])
    TN = P.now("T")
    v(lambda: NCV.tensor_scalar(selr_sb[:, 0:8], pMISC[0:1, 28:36], 7.0, None,
                                ALU.is_equal), waits=[("T", TN)])
    v(lambda: NCV.tensor_scalar(selr_sb[:, 8:16], pMISC[0:1, 28:36], 6.0, None,
                                ALU.is_equal))
    v(lambda: NCV.tensor_tensor(lgrow[:, :], prob_sb[:, :], selr_sb[:, 0:8], ALU.mult),
      waits=[("T", P.now("T"))])
    v(lambda: NCV.tensor_reduce(lgsc[:, 0:1], lgrow[:, :], AX.X, ALU.add))
    v(lambda: NCV.tensor_tensor(lgrow[:, :], prob_sb[:, :], selr_sb[:, 8:16], ALU.mult))
    v(lambda: NCV.tensor_reduce(lgsc[:, 1:2], lgrow[:, :], AX.X, ALU.add))
    VN = P.now("V")
    t(lambda: NCT.matmul(pMISC[:, 36:52], onesr_f[:, :], selr_sb[:, :],
                         start=True, stop=True), waits=[("V", VN)])
    t(lambda: NCT.matmul(pMISC[:, 52:54], onesr_f[:, :], lgsc[:, 0:2],
                         start=True, stop=True))
    TN = P.now("T")
    v(lambda: NCV.tensor_copy(selbc[:, :], pMISC[:, 36:52]), waits=[("T", TN)])
    v(lambda: NCV.tensor_copy(gcol_sb[:, :], pMISC[:, 52:54]))
    for name, wsrc, dst in (("w1", w1T_sb, w1sel), ("w2", w2T_sb, w2sel)):
        for e2 in range(2):
            v(lambda wsrc=wsrc, dst=dst, e2=e2: NCV.tensor_scalar(
                dst[:, 128 * e2:128 * (e2 + 1)], wsrc[:, 0:128],
                selbc[:, 8 * e2:8 * e2 + 1], None, ALU.mult),
              waits=[("G", G_W3X)])
            for ee in range(1, 8):
                v(lambda wsrc=wsrc, dst=dst, e2=e2, ee=ee: NCV.scalar_tensor_tensor(
                    dst[:, 128 * e2:128 * (e2 + 1)], wsrc[:, 128 * ee:128 * (ee + 1)],
                    selbc[:, 8 * e2 + ee:8 * e2 + ee + 1],
                    dst[:, 128 * e2:128 * (e2 + 1)], ALU.mult, ALU.add))
    for bsrc, dst in ((b1T_sb, b1sel), (b2T_sb, b2sel)):
        for e2 in range(2):
            v(lambda bsrc=bsrc, dst=dst, e2=e2: NCV.tensor_scalar(
                dst[:, e2:e2 + 1], bsrc[:, 0:1], selbc[:, 8 * e2:8 * e2 + 1], None,
                ALU.mult))
            for ee in range(1, 8):
                v(lambda bsrc=bsrc, dst=dst, e2=e2, ee=ee: NCV.scalar_tensor_tensor(
                    dst[:, e2:e2 + 1], bsrc[:, ee:ee + 1],
                    selbc[:, 8 * e2 + ee:8 * e2 + ee + 1], dst[:, e2:e2 + 1],
                    ALU.mult, ALU.add))
    v(lambda: NCV.tensor_scalar(b2c_sb[:, :], b2sel[:, 0:1], gcol_sb[:, 0:1], None,
                                ALU.mult))
    v(lambda: NCV.scalar_tensor_tensor(b2c_sb[:, :], b2sel[:, 1:2], gcol_sb[:, 1:2],
                                       b2c_sb[:, :], ALU.mult, ALU.add))
    V_SEL = P.now("V")

    T_P3A = P.now("T")
    gd(lambda: NCG.dma_start(Wqkv_sb[:, :].rearrange("p (k c) -> p k c", k=16),
                             w3T_in.rearrange("(k p) c -> p k c", p=128)),
       waits=[("T", T_P3A)])
    G_W3 = P.now("G")

    def emit_p3b_window(wi):
        ldn = P.now("LD")
        wl = [("V", V_SEL)]
        if wi > 0:
            wl.append(("ST", hold["p3b_st_%d" % (wi - 1)]))
        yd(lambda wi=wi: NCY.dma_start(
            ln2_w[:, :].rearrange("p (k t) -> p k t", k=16),
            bass.AP(ln2_sp.tensor, 512 * wi, [[NQ, 128], [128 * NQ, 16], [1, 512]])),
           waits=wl)
        yd(lambda wi=wi: NCY.dma_start(
            x1_w[:, :].rearrange("p (k t) -> p k t", k=16),
            bass.AP(x1_sp.tensor, 512 * wi, [[NQ, 128], [128 * NQ, 16], [1, 512]])))
        LD_T = ldn + 32
        for ci16 in range(16):
            for e2 in range(2):
                wl = [("LD", LD_T)] if (ci16 == 0 and e2 == 0) else []
                t(lambda ci16=ci16, e2=e2: NCT.matmul(
                    pA[:, :], w1sel[:, 128 * e2:128 * (e2 + 1)],
                    ln2_w[:, 512 * ci16:512 * (ci16 + 1)], start=True, stop=True),
                  waits=wl + [("V", P.now("V")), ("S", P.now("S"))])
                TN = P.now("T")
                s(lambda e2=e2: NCS.activation(h1_sb[:, :], pA[:, :], AF.Gelu,
                                               bias=b1sel[:, e2:e2 + 1]),
                  waits=[("T", TN), ("V", P.now("V"))])
                SN = P.now("S")
                v(lambda: NCV.tensor_copy(h1b_sb[:, :], h1_sb[:, :]),
                  waits=[("S", SN), ("T", P.now("T"))], nodrain=True)
                VN3 = P.now("V")
                t(lambda e2=e2: NCT.matmul(pBC[:, 0:512], w2sel[:, 128 * e2:128 * (e2 + 1)],
                                           h1b_sb[:, :], start=True, stop=True),
                  waits=[("V", VN3)])
                TN = P.now("T")
                if e2 == 0:
                    v(lambda: NCV.tensor_scalar(oacc[:, :], pBC[:, 0:512],
                                                gcol_sb[:, 0:1], None, ALU.mult),
                      waits=[("T", TN)])
                else:
                    v(lambda: NCV.scalar_tensor_tensor(oacc[:, :], pBC[:, 0:512],
                                                       gcol_sb[:, 1:2], oacc[:, :],
                                                       ALU.mult, ALU.add),
                      waits=[("T", TN)])
            v(lambda ci16=ci16: NCV.tensor_scalar(oacc[:, :], oacc[:, :],
                                                  b2c_sb[:, :], None, ALU.add))
            v(lambda ci16=ci16: NCV.tensor_tensor(
                w3in_w[:, 512 * ci16:512 * (ci16 + 1)], oacc[:, :],
                ln2_w[:, 512 * ci16:512 * (ci16 + 1)], ALU.add))
        V_W3IN = P.now("V")
        if DEBUG and wi == 0:
            dbg_w3in = nc.dram_tensor("dbg_w3in", [128, 8192], BF16,
                                      kind="ExternalOutput").ap()
            ys(lambda: NCY.dma_start(dbg_w3in, w3in_w[:, :]), waits=[("V", V_W3IN)])
        for ci16 in range(16):
            for kk in range(16):
                wl = [("V", max(V_W3IN, P.now("V"))), ("G", G_W3),
                      ("S", P.now("S"))] if kk == 0 else []
                t(lambda ci16=ci16, kk=kk: NCT.matmul(
                    pA[:, :],
                    Wqkv_sb[:, kk * 2048 + ci16 * 128:kk * 2048 + ci16 * 128 + 128],
                    w3in_w[:, 512 * kk:512 * (kk + 1)],
                    start=(kk == 0), stop=(kk == 15)), waits=wl)
            TN = P.now("T")
            v(lambda ci16=ci16: NCV.tensor_scalar(
                attn_sb[:, 512 * ci16:512 * (ci16 + 1)], pA[:, :],
                b3_sb[:, ci16:ci16 + 1], None, ALU.add), waits=[("T", TN)])
            VN = P.now("V")
            v(lambda ci16=ci16: NCV.tensor_tensor(
                asq_sb[:, :], attn_sb[:, 512 * ci16:512 * (ci16 + 1)],
                attn_sb[:, 512 * ci16:512 * (ci16 + 1)], ALU.mult),
              waits=[("T", P.now("T"))])
            VN2 = P.now("V")
            t(lambda ci16=ci16: NCT.matmul(
                pMISC[0:1, 0:512], onesc_bf[:, :],
                attn_sb[:, 512 * ci16:512 * (ci16 + 1)],
                start=(ci16 == 0), stop=(ci16 == 15)), waits=[("V", VN2)])
            t(lambda ci16=ci16: NCT.matmul(
                pBC[0:1, 0:512], onesc_bf[:, :], asq_sb[:, :],
                start=(ci16 == 0), stop=(ci16 == 15)))
        TN = P.now("T")
        v(lambda: NCV.tensor_scalar(murow[:, :], pMISC[0:1, 0:512], 1.0 / 2048, None,
                                    ALU.mult), waits=[("T", TN)])
        v(lambda: NCV.tensor_scalar(tmpsr[:, :], pBC[0:1, 0:512], 1.0 / 2048, None,
                                    ALU.mult))
        v(lambda: NCV.scalar_tensor_tensor(varow[:, :], murow[:, :], -1.0,
                                           murow[:, :], ALU.mult, ALU.mult))
        v(lambda: NCV.tensor_tensor(varow[:, :], tmpsr[:, :], varow[:, :], ALU.add))
        v(lambda: NCV.tensor_scalar(varow[:, :], varow[:, :], LNEPS, None, ALU.add))
        if DEBUG and wi == 0:
            dbg_lnr = nc.dram_tensor("dbg_lnr", [1, 1536], F32,
                                     kind="ExternalOutput").ap()
            VNR = P.now("V")
            ys(lambda: NCY.dma_start(dbg_lnr[:, 0:512], murow[:, :]), waits=[("V", VNR)])
            ys(lambda: NCY.dma_start(dbg_lnr[:, 512:1024], tmpsr[:, :]), waits=[("V", VNR)])
            ys(lambda: NCY.dma_start(dbg_lnr[:, 1024:1536], varow[:, :]), waits=[("V", VNR)])
        VN = P.now("V")
        s(lambda: NCS.activation(varow[:, :], varow[:, :], AF.Ln),
          waits=[("V", VN)])
        s(lambda: NCS.activation(scr_row[:, :], varow[:, :], AF.Exp, scale=-0.5),
          waits=[("V", P.now("V"))], drain=True)
        SNx = P.now("S")
        v(lambda: NCV.tensor_copy(r1row[:, :], scr_row[:, :]), waits=[("S", SNx)])
        SN = P.now("S")
        t(lambda: NCT.matmul(pMISC[:, 0:512], onesr_f[:, :], murow[:, :],
                             start=True, stop=True), waits=[("S", SN)])
        t(lambda: NCT.matmul(pBC[:, 0:512], onesr_f[:, :], r1row[:, :],
                             start=True, stop=True))
        TN = P.now("T")
        if DEBUG and wi == 0:
            dbg_ffn = nc.dram_tensor("dbg_ffn", [128, 8192], BF16,
                                     kind="ExternalOutput").ap()
            VNF = P.now("V")
            ys(lambda: NCY.dma_start(dbg_ffn, attn_sb[:, :]), waits=[("V", VNF)])
        for ci16 in range(16):
            v(lambda ci16=ci16: NCV.tensor_tensor(
                x2_sb[:, :], attn_sb[:, 512 * ci16:512 * (ci16 + 1)], pMISC[:, 0:512],
                ALU.subtract), waits=[("T", TN)] if ci16 == 0 else [])
            v(lambda ci16=ci16: NCV.tensor_tensor(x2_sb[:, :], x2_sb[:, :],
                                                  pBC[:, 0:512], ALU.mult))
            v(lambda ci16=ci16: NCV.tensor_tensor(
                x1_w[:, 512 * ci16:512 * (ci16 + 1)], x2_sb[:, :],
                x1_w[:, 512 * ci16:512 * (ci16 + 1)], ALU.add))
        V_X2 = P.now("V")
        for tt4 in range(4):
            STW = hold.get("p3b_ost_%d_%d" % (wi, tt4 - 1), hold.get("p3b_ost_%d_3" % (wi - 1), 0))
            for ci16 in range(16):
                t(lambda ci16=ci16, tt4=tt4: NCT.transpose(
                    pY[:, 0:128], x1_w[:, 512 * ci16 + 128 * tt4:512 * ci16 + 128 * (tt4 + 1)],
                    ident_f[:, :]),
                  waits=[("V", V_X2 if tt4 == 0 else P.now("V"))] if ci16 == 0
                  else [("V", P.now("V"))])
                TN2 = P.now("T")
                v(lambda ci16=ci16: NCV.tensor_copy(
                    outb[:, 128 * ci16:128 * (ci16 + 1)], pY[:, 0:128]),
                  waits=[("T", TN2), ("ST", STW)] if ci16 == 0 else [("T", TN2)], nodrain=True)
            VN2 = P.now("V")
            ys(lambda wi=wi, tt4=tt4: NCY.dma_start(
                out_t[512 * wi + 128 * tt4:512 * wi + 128 * (tt4 + 1), :],
                outb[:, :]), waits=[("V", VN2)])
            hold["p3b_ost_%d_%d" % (wi, tt4)] = P.now("ST")
        hold["p3b_st_%d" % wi] = P.now("ST")

    for wi in range(NW3):
        emit_p3b_window(wi)

    if DEBUG:
        dbg_ag1 = nc.dram_tensor("dbg_ag1", [2048, NQ], BF16, kind="ExternalOutput").ap()
        dbg_ag2 = nc.dram_tensor("dbg_ag2", [512, S_], BF16, kind="ExternalOutput").ap()
        dbg_x1 = nc.dram_tensor("dbg_x1", [2048, NQ], F32, kind="ExternalOutput").ap()
        dbg_ln2 = nc.dram_tensor("dbg_ln2", [2048, NQ], BF16, kind="ExternalOutput").ap()
        dbg_sc = nc.dram_tensor("dbg_sc", [1, 64], F32, kind="ExternalOutput").ap()
        STN0 = P.now("ST")
        gd(lambda: NCG.dma_start(dbg_ag1, ag1_in), waits=[("ST", STN0), ("CC", P.now("CC"))])
        gd(lambda: NCG.dma_start(dbg_ag2, ag2_in))
        gd(lambda: NCG.dma_start(dbg_x1, x1_sp))
        gd(lambda: NCG.dma_start(dbg_ln2, ln2_sp))
        dbg_agr = nc.dram_tensor("dbg_agr", [2048, RT], BF16, kind="ExternalOutput").ap()
        gd(lambda: NCG.dma_start(dbg_agr, agr_out[0:2048, :]))
        VND = P.now("V")
        v(lambda: NCV.tensor_copy(murow[:, 0:8], prob_sb[:, :]))
        v(lambda: NCV.tensor_copy(murow[:, 8:24], selr_sb[:, :]))
        v(lambda: NCV.tensor_copy(murow[:, 24:28], lgsc[:, :]))
        v(lambda: NCV.tensor_copy(murow[:, 32:64], dbgk[:, :]))
        VND = P.now("V")
        gd(lambda: NCG.dma_start(dbg_sc, murow[:, 0:64]), waits=[("V", VND)])
        GDN = P.now("G")
        P.op("g", lambda: NCG.engine_nop(), [("G", GDN)], None)

    STN = P.now("ST")
    P.op("y", lambda: NCY.nop(), [("ST", STN)], None)

    P.replay()
    es.close()
    return nc


_BUILD_CACHE = {}


def kernel(**inputs):
    S_ = int(inputs["L0_attn_hidden_states"].shape[1])
    NQ = S_ // 4
    maps = make_in_maps(inputs, S_)
    if S_ not in _BUILD_CACHE:
        _BUILD_CACHE[S_] = build(S_)
    nc = _BUILD_CACHE[S_]
    res = run_bass_kernel_spmd(nc, maps, core_ids=list(range(8)))
    out = np.empty((B, S_, D), np.float32)
    for core in range(8):
        b = core // 4
        j = core % 4
        out[b, j * NQ:(j + 1) * NQ, :] = res.results[core]["out"]
    return out


# version 14
# speedup vs baseline: 1.0168x; 1.0038x over previous
import contextlib
import numpy as np
import ml_dtypes
import concourse.bass as bass
import concourse.mybir as mybir
from concourse.bass_utils import run_bass_kernel_spmd

F32 = mybir.dt.float32
BF16 = mybir.dt.bfloat16
AF = mybir.ActivationFunctionType
ALU = mybir.AluOpType
AX = mybir.AxisListType

B = 2
D = 2048
NEG = -1e30
EPS = 1e-6
LNEPS = 1e-5


class Prog:
    ENGINES = ("t", "v", "s", "g", "y")

    def __init__(self, nc):
        self.nc = nc
        self.ops = {e: [] for e in self.ENGINES}
        self.sems = {}
        self.cnt = {}
        self.hist = {}

    def sem(self, name):
        h = self.nc.semaphore(name).__enter__()
        self.sems[name] = h
        self.cnt[name] = 0

    SEQCST = True
    CONS_FULL = ("T", "V", "S", "LD", "ST", "G", "CC")
    CONS_LITE = ("LD", "ST", "G", "CC")
    CONS = CONS_FULL

    def __init2__(self):
        pass

    def op(self, eng, fn, waits=(), inc=None, inc_n=1, nodrain=False,
           drain_before=False):
        w = dict()
        for sname, val in waits:
            w[sname] = max(w.get(sname, 0), val)
        if self.SEQCST:
            for sname in self.CONS:
                w[sname] = max(w.get(sname, 0), self.cnt[sname])
        hist = self.hist.setdefault(eng, {})
        waits = []
        for k, v in w.items():
            if v > hist.get(k, 0):
                waits.append((k, v))
                hist[k] = v
        if inc is not None:
            self.cnt[inc] += inc_n
            if (eng, inc) in (("t", "T"), ("v", "V"), ("s", "S")):
                h2 = self.hist.setdefault(eng, {})
                h2[inc] = max(h2.get(inc, 0), self.cnt[inc])
        self.ops[eng].append((fn, tuple(waits), inc, inc_n, nodrain,
                              drain_before))

    def now(self, sem):
        return self.cnt[sem]

    def replay(self):
        nc = self.nc
        attr = {"t": "tensor", "v": "vector", "s": "scalar", "g": "gpsimd", "y": "sync"}
        with nc.Block() as blk:
            for ename in self.ENGINES:
                ops_local = self.ops[ename]

                def mk(ops_local, ename=ename):
                    def body(eng):
                        first = True
                        for fn, waits, inc, inc_n, nodrain, drain_before in ops_local:
                            if drain_before and not first:
                                eng.drain()
                            if ename == "v" and not first and not nodrain:
                                eng.drain()
                            first = False
                            for (sname, val) in waits:
                                if val > 0:
                                    eng.wait_ge(self.sems[sname], val)
                            inst = fn()
                            if inc is not None:
                                inst.then_inc(self.sems[inc], inc_n)
                    return body

                getattr(blk, attr[ename])(mk(ops_local))


def _consts():
    ident = np.eye(128, dtype=np.float32)
    ii = np.arange(128)[:, None]
    tt = np.arange(128)[None, :]
    triu = (ii <= tt).astype(np.float32)
    maskMA = np.concatenate([
        np.where(ii < tt, 0.0, NEG), np.where(ii <= tt, 0.0, NEG)],
        axis=1).astype(np.float32)
    maskT = np.where(tt < ii, 0.0, NEG).astype(np.float32)
    eye4 = np.zeros((128, 16), np.float32)
    for h in range(4):
        eye4[:, 4 * h + h] = 1.0
    return dict(c_ident=ident,
                c_ones_row=np.ones((1, 128), np.float32),
                c_ones_col=np.ones((128, 1), np.float32),
                c_triu=triu, c_maskMA=maskMA, c_maskT=maskT, c_eye4=eye4)


def make_in_maps(inputs, S_):
    NQ = S_ // 4
    RT = min(2048, NQ)
    hs = np.asarray(inputs["L0_attn_hidden_states"])[:, :S_, :]
    Wqkv = np.asarray(inputs["L0_attn_W_qkv"])
    Wz = np.asarray(inputs["L0_attn_W_z"])
    Wb = np.asarray(inputs["L0_attn_W_b"])
    Wa = np.asarray(inputs["L0_attn_W_a"])
    convw = np.asarray(inputs["L0_attn_conv_w"])[:, 0, :]
    A_log = np.asarray(inputs["L0_attn_A_log"])
    dtb = np.asarray(inputs["L0_attn_dt_bias"])
    norm_w = np.asarray(inputs["L0_attn_norm_w"])
    W_out = np.asarray(inputs["L0_attn_W_out"])
    sw_w = np.asarray(inputs["L0_ffn_switch_w"])
    sw_b = np.asarray(inputs["L0_ffn_switch_b"])
    w1 = np.asarray(inputs["L0_ffn_w1"]); b1 = np.asarray(inputs["L0_ffn_b1"])
    w2 = np.asarray(inputs["L0_ffn_w2"]); b2 = np.asarray(inputs["L0_ffn_b2"])
    w3 = np.asarray(inputs["L0_ffn_w3"]); b3 = np.asarray(inputs["L0_ffn_b3"])
    consts = _consts()

    maps = []
    for core in range(8):
        b = core // 4
        j = core % 4
        heads = list(range(4 * j, 4 * j + 4))
        wqkv_sl = np.concatenate(
            [Wqkv[:, 128 * h:128 * (h + 1)] for h in heads]
            + [Wqkv[:, 2048 + 128 * h:2048 + 128 * (h + 1)] for h in heads]
            + [Wqkv[:, 4096 + 128 * h:4096 + 128 * (h + 1)] for h in heads]
            + [Wz[:, 128 * h:128 * (h + 1)] for h in heads], axis=1)
        convc = np.concatenate(
            [convw[128 * h:128 * (h + 1), :] for h in heads]
            + [convw[2048 + 128 * h:2048 + 128 * (h + 1), :] for h in heads]
            + [convw[4096 + 128 * h:4096 + 128 * (h + 1), :] for h in heads], axis=1)
        e0, e1 = 2 * j, 2 * j + 1
        sw0 = sw_w[e0, :2048 * RT].reshape(RT, 2048).T
        sw1 = sw_w[e1, :2048 * RT].reshape(RT, 2048).T
        switch_fm = np.concatenate([sw0, sw1], axis=1)
        m0 = np.zeros((1, 8), np.float32); m0[0, e0] = 1.0
        m1 = np.zeros((1, 8), np.float32); m1[0, e1] = 1.0
        m = dict(
            h_own=hs[b, j * NQ:(j + 1) * NQ, :],
            wqkv_sl=wqkv_sl, wb_sl=Wb[:, heads], wa_sl=Wa[:, heads], convc=convc,
            alog_c=A_log[heads].reshape(4, 1), dtb_c=dtb[heads].reshape(4, 1),
            normw_c=norm_w.reshape(128, 1),
            W_out=W_out, w3T=w3.T, b3_c=b3.reshape(16, 128).T,
            w1T=np.transpose(w1, (2, 0, 1)).reshape(128, 1024),
            w2T=np.transpose(w2, (2, 0, 1)).reshape(128, 1024),
            b1T=b1.T, b2T=b2.T,
            swb_row=sw_b.reshape(1, 8), switch_fm=switch_fm,
            emask0=m0, emask1=m1,
            offtok=np.array([[j * NQ]], np.int32),
            **consts,
        )
        mm = {}
        for k, vv in m.items():
            vv = np.ascontiguousarray(vv)
            if k != "offtok":
                vv = vv.astype(np.float32)
            mm[k] = vv
        maps.append(mm)
    return maps


DEBUG = False
RELAX = 1
SEQDRV = False


def build(S_):
    NQ = S_ // 4
    NW = S_ // 512
    NWQ = NW // 4
    NW3 = NQ // 512
    RT = min(2048, NQ)
    RTW = RT // 512

    nc = bass.Bass("TRN2", target_bir_lowering=False)

    def din(name, shape, dt=F32):
        return nc.dram_tensor(name, list(shape), dt, kind="ExternalInput").ap()

    h_own = din("h_own", (NQ, 2048))
    wqkv_in = din("wqkv_sl", (2048, 2048))
    wb_in = din("wb_sl", (2048, 4))
    wa_in = din("wa_sl", (2048, 4))
    convc_in = din("convc", (128, 48))
    alog_in = din("alog_c", (4, 1))
    dtb_in = din("dtb_c", (4, 1))
    normw_in = din("normw_c", (128, 1))
    Wout_in = din("W_out", (2048, 2048))
    w3T_in = din("w3T", (2048, 2048))
    b3_in = din("b3_c", (128, 16))
    w1T_in = din("w1T", (128, 1024))
    w2T_in = din("w2T", (128, 1024))
    b1T_in = din("b1T", (128, 8))
    b2T_in = din("b2T", (128, 8))
    swb_in = din("swb_row", (1, 8))
    switch_in = din("switch_fm", (2048, 2 * RT))
    emask0_in = din("emask0", (1, 8))
    emask1_in = din("emask1", (1, 8))
    offtok_in = nc.dram_tensor("offtok", [1, 1], mybir.dt.int32, kind="ExternalInput").ap()
    ident_in = din("c_ident", (128, 128))
    onesr_in = din("c_ones_row", (1, 128))
    onesc_in = din("c_ones_col", (128, 1))
    triu_in = din("c_triu", (128, 128))
    maskMA_in = din("c_maskMA", (128, 256))
    maskT_in = din("c_maskT", (128, 128))
    eye4_in = din("c_eye4", (128, 16))

    out_t = nc.dram_tensor("out", [NQ, 2048], F32, kind="ExternalOutput").ap()

    ag1_in = nc.dram_tensor("ag1_in", [2048, NQ], BF16, kind="Internal").ap()
    ag1_out = nc.dram_tensor("ag1_out", [4 * 2048, NQ], BF16, kind="Internal").ap()
    ag2_in = nc.dram_tensor("ag2_in", [512, S_], BF16, kind="Internal").ap()
    ag2_out = nc.dram_tensor("ag2_out", [4 * 512, S_], BF16, kind="Internal").ap()
    agr_in = nc.dram_tensor("agr_in", [2048, RT], BF16, kind="Internal").ap()
    agr_out = nc.dram_tensor("agr_out", [4 * 2048, RT], BF16, kind="Internal").ap()
    lgt_in = nc.dram_tensor("lgt_in", [1, 8], F32, kind="Internal").ap()
    lgt_out = nc.dram_tensor("lgt_out", [1, 8], F32, kind="Internal").ap()
    hbf = nc.dram_tensor("hbf", [NQ, 2048], BF16, kind="Internal").ap()
    x1_sp = nc.dram_tensor("x1_sp", [2048, NQ], F32, kind="Internal").ap()
    ln2_sp = nc.dram_tensor("ln2_sp", [2048, NQ], BF16, kind="Internal").ap()

    es = contextlib.ExitStack()

    def sb(name, shape, dt=BF16):
        return es.enter_context(nc.sbuf_tensor("sb_" + name, list(shape), dt))

    ident_bf = sb("ident_bf", (128, 128))
    ident_f = sb("ident_f", (128, 128), F32)
    onesr_f = sb("onesr_f", (1, 128), F32)
    onesc_bf = sb("onesc_bf", (128, 1))
    onesc_f = sb("onesc_f", (128, 1), F32)
    triu_f = sb("triu_f", (128, 128), F32)
    maskMA_f = sb("maskMA_f", (128, 256), F32)
    maskT_f = sb("maskT_f", (128, 128), F32)
    eye4_bf = sb("eye4_bf", (128, 16))
    normw_c = sb("normw_c", (128, 1), F32)
    alog_c = sb("alog_c", (4, 1), F32)
    negexpA = sb("negexpA", (4, 1), F32)
    dtb_c = sb("dtb_c", (4, 1), F32)
    Wqkv_sb = sb("WBIG", (128, 16 * 2048))
    Wb_sb = sb("Wb_sb", (128, 16 * 4))
    Wa_sb = sb("Wa_sb", (128, 16 * 4))
    convc_sb = sb("convc_sb", (128, 48), F32)
    slot1 = sb("slot1", (128, 16384))
    slot2 = sb("slot2", (128, 16384))
    slot3 = sb("slot3", (128, 14848))
    slot4 = sb("slot4", (128, 8192))
    s1f = slot1[:, :].bitcast(F32)
    s2f = slot2[:, :].bitcast(F32)
    s3f = slot3[:, :].bitcast(F32)
    p0_in = s1f[:, 0:2048]
    p0_sq = s1f[:, 2048:4096]
    p0_bf = slot2[:, 0:2048]
    p0_hb = slot2[:, 2048:4096]
    p0_tr = slot2[:, 4096:6144]
    p0_st = sb("p0_st", (128, 6), F32)
    p0_m = sb("p0_m", (128, 1), F32)
    p0_r = sb("p0_r", (128, 1), F32)
    lw = [slot1[:, 0:8192], slot1[:, 8192:16384]]
    qk_buf = slot2[:, 0:4096]
    v_buf = slot2[:, 4096:6144]
    z_buf = slot2[:, 6144:8192]
    coreo = slot2[:, 8192:10240]
    mixed_sb = slot2[:, 10240:10756]
    tail_sb = slot2[:, 10756:10792]
    tanh_sb = slot2[:, 10792:11304]
    sq_tmp = slot2[:, 11304:11816]
    acc_f = s2f[:, 5908:6420]
    s4f = slot4[:, :].bitcast(F32)
    gpre_r = s4f[0:4, 0:512]
    bpre_r = s4f[0:4, 512:1024]
    ssq_r = s4f[0:4, 1024:1536]
    ssk_r = s4f[0:4, 1536:2048]
    g_r = s4f[0:4, 2048:2560]
    c1_r = s4f[0:4, 2560:3072]
    c2_r = s4f[0:4, 3072:3584]
    c3_r = s4f[0:4, 3584:4096]
    beta_r = s2f[0:4, 6420:6932]
    tmp_r = s2f[0:4, 6932:7444]
    tmp2_r = s2f[0:4, 7444:7956]
    mexp_sb = s3f[:, 0:256]
    EM_sb = slot3[:, 512:768]
    texp_sb = s3f[:, 512:640]
    Tpre_sb = slot3[:, 1280:1408]
    MA_2 = [slot3[:, 768:1024], slot2[:, 15400:15656]]
    T_2 = [slot3[:, 1408:1536], slot2[:, 15656:15784]]
    Qg_2 = [slot3[:, 1536:1664], slot2[:, 15784:15912]]
    Rbuf_2 = [slot3[:, 1664:1920], slot2[:, 10240:10496]]
    KTd_2 = [slot3[:, 1920:2048], slot2[:, 10496:10624]]
    Tp_2 = [slot3[:, 2048:3328], slot2[:, 13864:15144]]
    TTp_2 = [slot3[:, 3328:3456], slot2[:, 15144:15272]]
    ImM_2 = [slot3[:, 3456:3584], slot2[:, 15272:15400]]
    Ybuf = [slot3[:, 3584:3840], slot3[:, 3840:4096]]
    U_sb = slot3[:, 4096:4352]
    Isc_sb = slot3[:, 4352:4480]
    PhiT_sb = slot3[:, 4480:4608]
    QtT_sb = slot3[:, 4608:4736]
    S0_sb = slot3[:, 4736:5248]
    Ot_sb = slot3[:, 5248:5376]
    Osq_sb = slot3[:, 5376:5504]
    gat_sb = slot3[:, 5504:5632]
    cols_sb = s3f[:, 2816:2836]
    gcols_sb = s3f[:, 2836:2840]
    gendr_sb = s3f[0:1, 2868:2872]
    npsic = s3f[:, 2844:2848]
    phic = s3f[:, 2848:2852]
    expphic = s3f[:, 2852:2856]
    kdc = s3f[:, 2856:2860]
    expgend = s3f[:, 2860:2864]
    betac = s3f[:, 2864:2868]
    rows4 = s3f[0:1, 2880:4928]
    r4v = rows4.rearrange("p (a b t) -> p a b t", a=4, b=4)
    growb = s3f[0:1, 4928:5440]
    crowb = s3f[0:1, 5440:6976]
    msr_sb = s3f[0:1, 6976:7104]
    rstd_sb = s3f[0:1, 7104:7232]
    x1_w = s1f[:, :]
    core_w = slot2[:, 0:8192]
    w3in_w = slot2[:, 0:8192]
    hbf_w = slot2[:, 8192:16384]
    outb = s2f[:, 4096:6144]
    sw_sb = [slot2[:, 8192:10240], slot2[:, 10240:12288]]
    rtl_sb = slot2[:, 12288:14336]
    prod_sb = slot2[:, 14336:14848]
    attn_sb = slot3[:, 0:8192]
    asq_sb = slot3[:, 8192:8704]
    stat_bf = slot3[:, 8704:9728]
    ln2_w = slot4[:, 0:8192]
    murow = sb("murow", (1, 512), F32)
    varow = sb("varow", (1, 512), F32)
    r1row = sb("r1row", (1, 512), F32)
    tmpsr = sb("tmpsr", (1, 512), F32)
    w1T_sb = sb("w1T_sb", (128, 1024))
    w2T_sb = sb("w2T_sb", (128, 1024))
    b1T_sb = sb("b1T_sb", (128, 8), F32)
    b2T_sb = sb("b2T_sb", (128, 8), F32)
    b3_sb = sb("b3_sb", (128, 16), F32)
    swb_sb = sb("swb_sb", (1, 8), F32)
    emask_sb = sb("emask_sb", (1, 16), F32)
    lgrow = sb("lgrow", (1, 8), F32)
    lgsc = sb("lgsc", (1, 4), F32)
    prob_sb = sb("prob_sb", (1, 8), F32)
    probc_sb = sb("probc_sb", (8, 1), F32)
    cmp_sb = sb("cmp_sb", (8, 8), F32)
    selr_sb = sb("selr_sb", (1, 16), F32)
    selbc = sb("selbc", (128, 16), F32)
    gcol_sb = sb("gcol_sb", (128, 2), F32)
    w1sel = sb("w1sel", (128, 256))
    w2sel = sb("w2sel", (128, 256))
    b1sel = sb("b1sel", (128, 2), F32)
    b2sel = sb("b2sel", (128, 2), F32)
    b2c_sb = sb("b2c_sb", (128, 1), F32)
    h1_sb = sb("h1_sb", (128, 512))
    oacc = sb("oacc", (128, 512), F32)
    x2_sb = sb("x2_sb", (128, 512), F32)
    offr_sb = es.enter_context(nc.sbuf_tensor("sb_offr", [1, 1], mybir.dt.int32))
    dbgk = sb("dbgk", (1, 32), F32)
    scr_row = sb("scr_row", (1, 512), F32)
    h1b_sb = sb("h1b", (128, 512))

    pA = es.enter_context(nc.psum_tensor("pA", [128, 512], F32))
    pG_full = es.enter_context(nc.psum_tensor("pG", [128, 512], F32))
    pG = pG_full[:, 0:256]
    pBC = es.enter_context(nc.psum_tensor("pBC", [128, 512], F32))
    pSQ_full = es.enter_context(nc.psum_tensor("pSQ", [128, 512], F32))
    pSQ = pSQ_full[:, 0:256]
    pY_full = es.enter_context(nc.psum_tensor("pY", [128, 512], F32))
    pY = pY_full[:, 0:256]
    pOS_full = es.enter_context(nc.psum_tensor("pOS", [128, 512], F32))
    pOS = pOS_full[:, 0:256]
    pMISC = es.enter_context(nc.psum_tensor("pMISC", [128, 512], F32))
    pTR_full = es.enter_context(nc.psum_tensor("pTR", [128, 1024], BF16))
    pTR = pTR_full[:, 0:512]

    P = Prog(nc)
    for sname in ("LD", "ST", "T", "V", "S", "G", "GC", "CC"):
        P.sem(sname)

    def t(fn, waits=()):
        P.op("t", fn, waits, "T")

    def v(fn, waits=(), nodrain=False):
        P.op("v", fn, waits, "V", nodrain=nodrain)

    def s(fn, waits=(), drain=False):
        P.op("s", fn, waits, "S", drain_before=drain)

    def gd(fn, waits=()):
        P.op("g", fn, waits, "G", 16)

    def gc(fn, waits=()):
        P.op("g", fn, waits, "GC", 1)

    def cc(fn, waits=()):
        P.op("g", fn, waits, "CC", 1)

    def yd(fn, waits=()):
        P.op("y", fn, waits, "LD", 16)

    def ys(fn, waits=()):
        P.op("y", fn, waits, "ST", 16)

    def yw(fn, waits=()):
        P.op("y", fn, waits, None)

    NCV, NCS, NCT, NCG, NCY = nc.vector, nc.scalar, nc.tensor, nc.gpsimd, nc.sync

    gc(lambda: NCG.memset(S0_sb[:, :], 0.0))
    gc(lambda: NCG.memset(tail_sb[:, :], 0.0))
    gd(lambda: NCG.dma_start(ident_bf[:, :], ident_in))
    gd(lambda: NCG.dma_start(ident_f[:, :], ident_in))
    gd(lambda: NCG.dma_start(onesr_f[:, :], onesr_in))
    gd(lambda: NCG.dma_start(onesc_bf[:, :], onesc_in))
    gd(lambda: NCG.dma_start(onesc_f[:, :], onesc_in))
    gd(lambda: NCG.dma_start(triu_f[:, :], triu_in))
    gd(lambda: NCG.dma_start(maskMA_f[:, :], maskMA_in))
    gd(lambda: NCG.dma_start(maskT_f[:, :], maskT_in))
    gd(lambda: NCG.dma_start(eye4_bf[:, :], eye4_in))
    gd(lambda: NCG.dma_start(normw_c[:, :], normw_in))
    gd(lambda: NCG.dma_start(alog_c[:, :], alog_in))
    gd(lambda: NCG.dma_start(dtb_c[:, :], dtb_in))
    gd(lambda: NCG.dma_start(convc_sb[:, :], convc_in))
    gd(lambda: NCG.dma_start(Wqkv_sb[:, :].rearrange("p (k c) -> p k c", k=16),
                             wqkv_in.rearrange("(k p) c -> p k c", p=128)))
    gd(lambda: NCG.dma_start(Wb_sb[:, :].rearrange("p (k c) -> p k c", k=16),
                             wb_in.rearrange("(k p) c -> p k c", p=128)))
    gd(lambda: NCG.dma_start(Wa_sb[:, :].rearrange("p (k c) -> p k c", k=16),
                             wa_in.rearrange("(k p) c -> p k c", p=128)))
    G_PRE = P.now("G")
    s(lambda: NCS.activation(negexpA[:, :], alog_c[:, :], AF.Exp), waits=[("G", G_PRE), ("GC", 2)])
    SN = P.now("S")
    v(lambda: NCV.tensor_scalar(negexpA[:, :], negexpA[:, :], -1.0, None, ALU.mult),
      waits=[("S", SN)])
    V_PRE = P.now("V")

    NT0 = NQ // 128
    p0_tr_done = 0
    p0_st_done = 0
    for it in range(NT0):
        ldn = P.now("LD")
        prevV = P.now("V")
        prevS = P.now("S")
        prevST = P.now("ST")
        yd(lambda it=it: NCY.dma_start(p0_in[:, :], h_own[128 * it:128 * (it + 1), :]),
           waits=[("V", prevV), ("S", prevS), ("ST", prevST)])
        LD_T = ldn + 16
        v(lambda: NCV.tensor_reduce(p0_st[:, 0:1], p0_in[:, :], AX.X, ALU.add),
          waits=[("LD", LD_T)])
        s(lambda: NCS.activation(p0_sq[:, :], p0_in[:, :], AF.Square),
          waits=[("LD", LD_T)])
        SN = P.now("S")
        v(lambda: NCV.tensor_reduce(p0_st[:, 1:2], p0_sq[:, :], AX.X, ALU.add),
          waits=[("S", SN)])
        v(lambda: NCV.tensor_scalar(p0_m[:, :], p0_st[:, 0:1], 1.0 / 2048, None, ALU.mult))
        v(lambda: NCV.scalar_tensor_tensor(p0_st[:, 2:3], p0_m[:, :], -1.0, p0_m[:, :],
                                           ALU.mult, ALU.mult))
        v(lambda: NCV.scalar_tensor_tensor(p0_st[:, 3:4], p0_st[:, 1:2], 1.0 / 2048,
                                           p0_st[:, 2:3], ALU.mult, ALU.add))
        v(lambda: NCV.tensor_scalar(p0_st[:, 3:4], p0_st[:, 3:4], LNEPS, None, ALU.add))
        VN = P.now("V")
        s(lambda: NCS.activation(p0_st[:, 4:5], p0_st[:, 3:4], AF.Ln),
          waits=[("V", VN)])
        s(lambda: NCS.activation(p0_r[:, :], p0_st[:, 4:5], AF.Exp, scale=-0.5),
          drain=True)
        SN = P.now("S")
        v(lambda: NCV.scalar_tensor_tensor(p0_m[:, :], p0_m[:, :], -1.0, p0_r[:, :],
                                           ALU.mult, ALU.mult), waits=[("S", SN)])
        v(lambda: NCV.tensor_scalar(p0_bf[:, :], p0_in[:, :], p0_r[:, :], p0_m[:, :],
                                    ALU.mult, ALU.add), waits=[("T", p0_tr_done)])
        v(lambda: NCV.tensor_copy(p0_hb[:, :], p0_in[:, :]))
        VN = P.now("V")
        ys(lambda it=it: NCY.dma_start(hbf[128 * it:128 * (it + 1), :], p0_hb[:, :]),
           waits=[("V", VN)])
        if DEBUG and it == 0:
            dbg_p0bf = nc.dram_tensor("dbg_p0bf", [128, 2048], BF16,
                                      kind="ExternalOutput").ap()
            ys(lambda: NCY.dma_start(dbg_p0bf, p0_bf[:, :]), waits=[("V", VN)])
            dbg_st = nc.dram_tensor("dbg_st", [128, 8], F32,
                                    kind="ExternalOutput").ap()
            ys(lambda: NCY.dma_start(dbg_st[:, 0:6], p0_st[:, :]), waits=[("V", VN)])
            ys(lambda: NCY.dma_start(dbg_st[:, 6:7], p0_m[:, :], allow_slow_non_contiguous=True), waits=[("V", VN)])
            ys(lambda: NCY.dma_start(dbg_st[:, 7:8], p0_r[:, :], allow_slow_non_contiguous=True), waits=[("V", VN)])
        for dd in range(16):
            t(lambda dd=dd: NCT.transpose(pTR[:, 128 * (dd % 4):128 * (dd % 4) + 128],
                                          p0_bf[:, 128 * dd:128 * (dd + 1)],
                                          ident_bf[:, :]),
              waits=[("V", VN), ("G", G_PRE), ("GC", 2)] if dd == 0 else
                    ([("V", P.now("V"))] if dd % 4 == 0 else []))
            if dd % 4 == 3:
                TN = P.now("T")
                v(lambda dd=dd: NCV.tensor_copy(p0_tr[:, 128 * (dd - 3):128 * (dd + 1)],
                                                pTR[:, 0:512]),
                  waits=[("T", TN), ("ST", p0_st_done)] if dd == 3 else [("T", TN)])
        VN = P.now("V")
        ys(lambda it=it: NCY.dma_start(
            ag1_in.rearrange("(k p) t -> p k t", p=128)[:, :, 128 * it:128 * (it + 1)],
            p0_tr[:, :].rearrange("p (k t) -> p k t", k=16)),
           waits=[("V", VN)])
        if DEBUG and it == 0:
            dbg_p0tr = nc.dram_tensor("dbg_p0tr", [128, 2048], BF16,
                                      kind="ExternalOutput").ap()
            ys(lambda: NCY.dma_start(dbg_p0tr, p0_tr[:, :]), waits=[("V", VN)])
        p0_tr_done = P.now("T")
        p0_st_done = P.now("ST")

    STN = P.now("ST")
    cc(lambda: NCG.collective_compute(
        "AllGather", ALU.bypass, replica_groups=[[0, 1, 2, 3], [4, 5, 6, 7]],
        ins=[ag1_in.opt()], outs=[ag1_out.opt()]),
       waits=[("ST", STN)])
    CC_AG1 = P.now("CC")

    win_mm_done = {}

    def gen_part1(w, ch, ci, hh):
        sl = hh % 2
        MA_sb, T_sb, Qg_sb, Rbuf, KTd_sb = MA_2[sl], T_2[sl], Qg_2[sl], Rbuf_2[sl], KTd_2[sl]
        Tp_sb, TTp_sb, ImM_sb = Tp_2[sl], TTp_2[sl], ImM_2[sl]
        koff = hh * 1024 + ci * 128
        qoff = hh * 1024 + 512 + ci * 128
        SN0 = P.now("S")
        VN0 = P.now("V")
        t(lambda koff=koff: NCT.matmul(
            pG[:, 0:256], qk_buf[:, koff:koff + 128],
            bass.AP(slot2, koff, [[16384, 128], [512, 2], [1, 128]]),
            start=True, stop=True), waits=[("S", SN0), ("V", VN0)])
        t(lambda hh=hh: NCT.matmul(
            pBC[:, 0:256], onesr_f[:, :], r4v[:, 0:2, hh, :],
            start=True, stop=True))
        TN = P.now("T")
        yield
        v(lambda: NCV.tensor_tensor(mexp_sb[:, :], pBC[:, 0:256], maskMA_f[:, :],
                                    ALU.add), waits=[("T", TN), ("S", P.now("S"))],
          nodrain=True)
        VN = P.now("V")
        s(lambda hh=hh: NCS.activation(EM_sb[:, :], mexp_sb[:, :], AF.Exp,
                                       bias=npsic[:, hh:hh + 1]), waits=[("V", VN)])
        SN = P.now("S")
        yield
        v(lambda MA_sb=MA_sb: NCV.tensor_tensor(MA_sb[:, :], EM_sb[:, :], pG[:, 0:256],
                                                ALU.mult),
          waits=[("S", SN)], nodrain=True)
        V_MA = P.now("V")
        t(lambda hh=hh: NCT.matmul(
            pBC[:, 256:512], onesr_f[:, :], r4v[:, 2:4, hh, :],
            start=True, stop=True), waits=[("V", V_MA)])
        TN = P.now("T")
        yield
        v(lambda: NCV.tensor_tensor(texp_sb[:, :], pBC[:, 256:384], maskT_f[:, :],
                                    ALU.add), waits=[("T", TN), ("S", P.now("S"))],
          nodrain=True)
        VN = P.now("V")
        s(lambda hh=hh: NCS.activation(Tpre_sb[:, :], texp_sb[:, :], AF.Exp,
                                       bias=phic[:, hh:hh + 1]), waits=[("V", VN)])
        SN = P.now("S")
        yield
        v(lambda T_sb=T_sb: NCV.tensor_tensor(T_sb[:, :], Tpre_sb[:, :], pG[:, 0:128],
                                              ALU.mult), waits=[("S", SN)], nodrain=True)
        v(lambda qoff=qoff, Qg_sb=Qg_sb: NCV.tensor_tensor(
            Qg_sb[:, :], qk_buf[:, qoff:qoff + 128], pBC[:, 384:512], ALU.mult),
          nodrain=True)
        V_TQ = P.now("V")
        t(lambda koff=koff: NCT.transpose(pTR[:, 0:128], qk_buf[:, koff:koff + 128],
                                          ident_bf[:, :]), waits=[("V", V_TQ)])
        t(lambda hh=hh, ci=ci: NCT.transpose(
            pTR[:, 128:256], v_buf[:, hh * 512 + ci * 128:hh * 512 + ci * 128 + 128],
            ident_bf[:, :]))
        TN = P.now("T")
        yield
        v(lambda hh=hh, Rbuf=Rbuf: NCV.tensor_scalar(
            Rbuf[:, 0:128], pTR[:, 128:256], betac[:, hh:hh + 1], None, ALU.mult),
          waits=[("T", TN)], nodrain=True)
        v(lambda hh=hh, Rbuf=Rbuf: NCV.tensor_scalar(
            Rbuf[:, 128:256], pTR[:, 0:128], expphic[:, hh:hh + 1], None, ALU.mult),
          nodrain=True)
        v(lambda hh=hh, KTd_sb=KTd_sb: NCV.tensor_scalar(
            KTd_sb[:, :], pTR[:, 0:128], kdc[:, hh:hh + 1], None, ALU.mult),
          nodrain=True)
        yield
        sq_src = [(T_sb, 0, MA_sb, 0)]
        for lvl in range(5):
            pTt, pTo, pTTt, pTTo = sq_src[-1]
            t(lambda pTt=pTt, pTo=pTo, pTTt=pTTt, pTTo=pTTo: NCT.matmul(
                pSQ[:, 0:128], pTTt[:, pTTo:pTTo + 128], pTt[:, pTo:pTo + 128],
                start=True, stop=True),
              waits=[("V", P.now("V"))])
            t(lambda pTt=pTt, pTo=pTo, pTTt=pTTt, pTTo=pTTo: NCT.matmul(
                pSQ[:, 128:256], pTt[:, pTo:pTo + 128], pTTt[:, pTTo:pTTo + 128],
                start=True, stop=True))
            TN = P.now("T")
            v(lambda lvl=lvl, Tp_sb=Tp_sb: NCV.tensor_copy(
                Tp_sb[:, 256 * lvl:256 * (lvl + 1)], pSQ[:, 0:256]),
              waits=[("T", TN)], nodrain=True)
            sq_src.append((Tp_sb, 256 * lvl, Tp_sb, 256 * lvl + 128))
            yield
        VN = P.now("V")
        t(lambda Tp_sb=Tp_sb: NCT.matmul(pSQ[:, 128:256], Tp_sb[:, 1024:1152],
                                         Tp_sb[:, 1152:1280],
                                         start=True, stop=True), waits=[("V", VN)])
        TN = P.now("T")
        v(lambda TTp_sb=TTp_sb: NCV.tensor_copy(TTp_sb[:, 0:128], pSQ[:, 128:256]),
          waits=[("T", TN)], nodrain=True)
        v(lambda ImM_sb=ImM_sb, MA_sb=MA_sb: NCV.tensor_tensor(
            ImM_sb[:, :], ident_bf[:, :], MA_sb[:, 0:128], ALU.subtract),
          nodrain=True)

    def gen_part2(w, ch, ci, hh):
        sl = hh % 2
        MA_sb, Qg_sb, Rbuf, KTd_sb = MA_2[sl], Qg_2[sl], Rbuf_2[sl], KTd_2[sl]
        Tp_sb, TTp_sb, ImM_sb = Tp_2[sl], TTp_2[sl], ImM_2[sl]
        ycur = Rbuf
        for li, lvl in enumerate((5, 4, 3, 2, 1, 0)):
            stap = (TTp_sb[:, 0:128] if lvl == 5
                    else Tp_sb[:, 256 * lvl + 128:256 * lvl + 256])
            t(lambda stap=stap, ycur=ycur, li=li: NCT.matmul(
                pY[:, :], stap, ycur[:, 0:256],
                start=(li == 0), stop=(li == 5)),
              waits=[("V", P.now("V"))])
            if li == 0:
                t(lambda ycur=ycur: NCT.matmul(pY[:, :], ident_bf[:, :],
                                               ycur[:, 0:256],
                                               start=False, stop=False))
            TN = P.now("T")
            ynext = Ybuf[li % 2]
            v(lambda ynext=ynext: NCV.tensor_copy(ynext[:, :], pY[:, :]),
              waits=[("T", TN)], nodrain=True)
            ycur = ynext
            yield
        VN = P.now("V")
        t(lambda ycur=ycur, ImM_sb=ImM_sb: NCT.matmul(
            pY[:, :], ImM_sb[:, :], ycur[:, 0:256],
            start=True, stop=True), waits=[("V", VN)])
        TN = P.now("T")
        v(lambda: NCV.tensor_copy(U_sb[:, :], pY[:, :]), waits=[("T", TN)],
          nodrain=True)
        V_U = P.now("V")
        yield
        t(lambda KTd_sb=KTd_sb: NCT.matmul(pY[:, 0:128], U_sb[:, 128:256], KTd_sb[:, :],
                                           start=True, stop=True), waits=[("V", V_U)])
        t(lambda MA_sb=MA_sb: NCT.matmul(pY[:, 128:256], U_sb[:, 128:256],
                                         MA_sb[:, 128:256], start=True, stop=True))
        TN = P.now("T")
        v(lambda hh=hh: NCV.tensor_scalar(Isc_sb[:, :], ident_bf[:, :],
                                          expgend[:, hh:hh + 1], None, ALU.mult),
          nodrain=True)
        v(lambda: NCV.tensor_tensor(PhiT_sb[:, :], Isc_sb[:, :], pY[:, 0:128],
                                    ALU.subtract), waits=[("T", TN)])
        v(lambda Qg_sb=Qg_sb: NCV.tensor_tensor(QtT_sb[:, :], Qg_sb[:, :],
                                                pY[:, 128:256], ALU.subtract),
          nodrain=True)
        V_C = P.now("V")
        yield
        t(lambda MA_sb=MA_sb: NCT.matmul(pOS[:, 0:128], U_sb[:, 0:128],
                                         MA_sb[:, 128:256], start=True, stop=False),
          waits=[("V", V_C)])
        t(lambda hh=hh: NCT.matmul(pOS[:, 0:128], S0_sb[:, 128 * hh:128 * (hh + 1)],
                                   QtT_sb[:, :], start=False, stop=True))
        t(lambda hh=hh: NCT.matmul(pOS[:, 128:256], PhiT_sb[:, :],
                                   S0_sb[:, 128 * hh:128 * (hh + 1)],
                                   start=True, stop=False))
        t(lambda KTd_sb=KTd_sb: NCT.matmul(pOS[:, 128:256], KTd_sb[:, :],
                                           U_sb[:, 0:128], start=False, stop=True))
        TN = P.now("T")
        yield
        v(lambda: NCV.tensor_copy(Ot_sb[:, :], pOS[:, 0:128]), waits=[("T", TN)],
          nodrain=True)
        v(lambda hh=hh: NCV.tensor_copy(S0_sb[:, 128 * hh:128 * (hh + 1)],
                                        pOS[:, 128:256]), nodrain=True)
        v(lambda: NCV.tensor_tensor(Osq_sb[:, :], Ot_sb[:, :], Ot_sb[:, :], ALU.mult))
        VN = P.now("V")
        yield
        t(lambda: NCT.matmul(pMISC[0:1, 256:384], onesc_bf[:, :], Osq_sb[:, :],
                             start=True, stop=True), waits=[("V", VN)])
        TN = P.now("T")
        v(lambda: NCV.tensor_scalar(msr_sb[:, :], pMISC[0:1, 256:384], 1.0 / 128,
                                    EPS, ALU.mult, ALU.add),
          waits=[("T", TN), ("S", P.now("S"))], nodrain=True)
        VNx = P.now("V")
        s(lambda: NCS.activation(msr_sb[:, :], msr_sb[:, :], AF.Ln),
          waits=[("V", VNx)])
        s(lambda: NCS.activation(scr_row[:, 0:128], msr_sb[:, :], AF.Exp, scale=-0.5),
          waits=[("V", P.now("V"))], drain=True)
        SNx = P.now("S")
        yield
        v(lambda: NCV.tensor_copy(rstd_sb[:, :], scr_row[:, 0:128]), waits=[("S", SNx)])
        VNr = P.now("V")
        t(lambda: NCT.matmul(pMISC[:, 384:512], onesr_f[:, :], rstd_sb[:, :],
                             start=True, stop=True), waits=[("V", VNr)])
        TN = P.now("T")
        yield
        v(lambda: NCV.tensor_tensor(gat_sb[:, :], Ot_sb[:, :], pMISC[:, 384:512],
                                    ALU.mult), waits=[("T", TN)], nodrain=True)
        v(lambda hh=hh, ci=ci: NCV.scalar_tensor_tensor(
            coreo[:, hh * 512 + ci * 128:hh * 512 + ci * 128 + 128],
            gat_sb[:, :], normw_c[:, :],
            z_buf[:, hh * 512 + ci * 128:hh * 512 + ci * 128 + 128],
            ALU.mult, ALU.mult))

    def _drive(gens):
        gens = [g for g in gens if g is not None]
        while gens:
            alive = []
            for g in gens:
                try:
                    next(g)
                    alive.append(g)
                except StopIteration:
                    pass
            gens = alive

    def emit_chunk(w, ch, ci):
        csl = slice(128 * ci, 128 * (ci + 1))
        gsl = slice(128 * ci, 128 * (ci + 1))
        VN = P.now("V")
        SN = P.now("S")
        for idx, rsrc in enumerate((g_r, c1_r, c2_r, beta_r, c3_r)):
            t(lambda idx=idx, rsrc=rsrc, gsl=gsl: NCT.transpose(
                pMISC[:, 4 * idx:4 * idx + 4], rsrc[0:4, gsl],
                ident_f[0:4, 0:4]),
              waits=[("V", VN), ("S", SN)] if idx == 0 else [])
        TN = P.now("T")
        v(lambda: NCV.tensor_copy(cols_sb[:, :], pMISC[:, 0:20]), waits=[("T", TN)], nodrain=True)
        VN = P.now("V")
        t(lambda: NCT.matmul(pMISC[:, 32:36], triu_f[:, :], cols_sb[:, 0:4],
                             start=True, stop=True), waits=[("V", VN)])
        t(lambda: NCT.matmul(pMISC[0:1, 36:40], onesc_f[:, :], cols_sb[:, 0:4],
                             start=True, stop=True))
        TN = P.now("T")
        v(lambda: NCV.tensor_copy(gcols_sb[:, :], pMISC[:, 32:36]), waits=[("T", TN)], nodrain=True)
        v(lambda: NCV.tensor_copy(gendr_sb[:, :], pMISC[0:1, 36:40]), nodrain=True)
        VN = P.now("V")
        for hh in range(4):
            t(lambda hh=hh: NCT.matmul(pBC[0:1, 128 * hh:128 * (hh + 1)],
                                       cols_sb[:, hh:hh + 1], triu_f[:, :],
                                       start=True, stop=True),
              waits=[("V", VN)] if hh == 0 else [])
        TN = P.now("T")
        v(lambda: NCV.tensor_copy(growb[:, :], pBC[0:1, 0:512]), waits=[("T", TN)], nodrain=True)
        for k_i, coff in ((0, 4), (1, 8), (2, 16)):
            VN = P.now("V")
            for hh in range(4):
                t(lambda coff=coff, hh=hh: NCT.matmul(
                    pBC[0:1, 128 * hh:128 * (hh + 1)],
                    cols_sb[:, coff + hh:coff + hh + 1], ident_f[:, :],
                    start=True, stop=True),
                  waits=[("V", VN)] if hh == 0 else [])
            TN = P.now("T")
            v(lambda k_i=k_i: NCV.tensor_copy(crowb[:, 512 * k_i:512 * (k_i + 1)],
                                              pBC[0:1, 0:512]), waits=[("T", TN)], nodrain=True)
        v(lambda: NCV.tensor_tensor(rows4[:, 0:512], growb[:, :], crowb[:, 0:512],
                                    ALU.add), waits=[("T", P.now("T")), ("S", P.now("S"))])
        v(lambda: NCV.tensor_tensor(rows4[:, 512:1024], growb[:, :],
                                    crowb[:, 1024:1536], ALU.add), nodrain=True)
        v(lambda: NCV.tensor_tensor(rows4[:, 1024:1536], crowb[:, 512:1024],
                                    growb[:, :], ALU.subtract), nodrain=True)
        VN = P.now("V")
        s(lambda: NCS.activation(scr_row[:, :], rows4[:, 512:1024], AF.Exp),
          waits=[("V", VN)])
        SNx = P.now("S")
        v(lambda: NCV.tensor_copy(rows4[:, 1536:2048], scr_row[:, :]),
          waits=[("S", SNx)])
        v(lambda: NCV.tensor_tensor(npsic[:, :], cols_sb[:, 8:12], gcols_sb[:, :],
                                    ALU.subtract), waits=[("S", P.now("S"))], nodrain=True)
        v(lambda: NCV.tensor_tensor(phic[:, :], gcols_sb[:, :], cols_sb[:, 4:8],
                                    ALU.add), nodrain=True)
        v(lambda: NCV.tensor_copy(betac[:, :], cols_sb[:, 12:16]), nodrain=True)
        VN = P.now("V")
        s(lambda: NCS.activation(expphic[:, :], phic[:, :], AF.Exp), waits=[("V", VN)])
        t(lambda: NCT.matmul(pMISC[:, 40:44], onesr_f[:, :], gendr_sb[:, :],
                             start=True, stop=True), waits=[("V", VN)])
        TN = P.now("T")
        v(lambda: NCV.tensor_tensor(kdc[:, :], pMISC[:, 40:44], npsic[:, :], ALU.add),
          waits=[("T", TN)])
        VN = P.now("V")
        s(lambda: NCS.activation(kdc[:, :], kdc[:, :], AF.Exp), waits=[("V", VN)])
        s(lambda: NCS.activation(expgend[:, :], pMISC[:, 40:44], AF.Exp))
        p1 = gen_part1(w, ch, ci, 0)
        for _ in p1:
            pass
        for hh in range(4):
            if SEQDRV:
                nxt = gen_part1(w, ch, ci, hh + 1) if hh < 3 else None
                for _ in gen_part2(w, ch, ci, hh):
                    pass
                if nxt is not None:
                    for _ in nxt:
                        pass
            else:
                nxt = gen_part1(w, ch, ci, hh + 1) if hh < 3 else None
                _drive([gen_part2(w, ch, ci, hh), nxt])

    def emit_window(w):
        r = w // NWQ
        tloc = 512 * (w % NWQ)
        buf = lw[w % 2]
        ldn = P.now("LD")
        lwait = [("CC", CC_AG1)]
        if w >= 2:
            lwait.append(("T", win_mm_done[w - 2]))
        yd(lambda r=r, tloc=tloc, buf=buf: NCY.dma_start(
            buf[:, :].rearrange("p (k t) -> p k t", k=16),
            bass.AP(ag1_out.tensor, 2048 * r * NQ + tloc, [[NQ, 128], [128 * NQ, 16], [1, 512]])),
           waits=lwait)
        LD_T = ldn + 16

        pA_free = 0
        for ci16 in range(16):
            for kk in range(16):
                wl = []
                if kk == 0:
                    wl = [("LD", LD_T), ("V", pA_free), ("G", G_PRE), ("GC", 2)]
                t(lambda ci16=ci16, kk=kk, buf=buf: NCT.matmul(
                    pA[:, :], Wqkv_sb[:, kk * 2048 + ci16 * 128:kk * 2048 + ci16 * 128 + 128],
                    buf[:, kk * 512:(kk + 1) * 512],
                    start=(kk == 0), stop=(kk == 15)), waits=wl)
            TN = P.now("T")
            hh = ci16 % 4
            if ci16 < 12:
                v(lambda: NCV.tensor_copy(mixed_sb[:, 3:515], pA[:, :]),
                  waits=[("T", TN)], nodrain=True)
                v(lambda ci16=ci16: NCV.tensor_copy(mixed_sb[:, 0:3],
                                                    tail_sb[:, 3 * ci16:3 * ci16 + 3]), nodrain=True)
                v(lambda ci16=ci16: NCV.tensor_copy(tail_sb[:, 3 * ci16:3 * ci16 + 3],
                                                    mixed_sb[:, 512:515]))
                v(lambda ci16=ci16: NCV.tensor_scalar(
                    acc_f[:, :], mixed_sb[:, 0:512],
                    convc_sb[:, 4 * ci16:4 * ci16 + 1], None, ALU.mult), nodrain=True)
                v(lambda ci16=ci16: NCV.tensor_scalar(
                    sq_tmp[:, :], mixed_sb[:, 1:513],
                    convc_sb[:, 4 * ci16 + 1:4 * ci16 + 2], None, ALU.mult),
                  nodrain=True)
                v(lambda ci16=ci16: NCV.scalar_tensor_tensor(
                    acc_f[:, :], mixed_sb[:, 2:514],
                    convc_sb[:, 4 * ci16 + 2:4 * ci16 + 3],
                    acc_f[:, :], ALU.mult, ALU.add))
                v(lambda ci16=ci16: NCV.scalar_tensor_tensor(
                    sq_tmp[:, :], mixed_sb[:, 3:515],
                    convc_sb[:, 4 * ci16 + 3:4 * ci16 + 4],
                    sq_tmp[:, :], ALU.mult, ALU.add), nodrain=True)
                v(lambda: NCV.tensor_tensor(acc_f[:, :], acc_f[:, :], sq_tmp[:, :],
                                            ALU.add))
                VN = P.now("V")
                s(lambda: NCS.activation(tanh_sb[:, :], acc_f[:, :], AF.Tanh,
                                         scale=0.5), waits=[("V", VN)])
                SN = P.now("S")
                v(lambda: NCV.tensor_scalar(tanh_sb[:, :], tanh_sb[:, :], 0.5, 0.5,
                                            ALU.mult, ALU.add), waits=[("S", SN)])
                if ci16 < 4:
                    dst, off = qk_buf, hh * 1024 + 512
                elif ci16 < 8:
                    dst, off = qk_buf, hh * 1024
                else:
                    dst, off = v_buf, hh * 512
                v(lambda dst=dst, off=off: NCV.tensor_tensor(
                    dst[:, off:off + 512], acc_f[:, :], tanh_sb[:, :], ALU.mult))
                pA_free = P.now("V")
                if ci16 < 8:
                    v(lambda dst=dst, off=off: NCV.tensor_tensor(
                        sq_tmp[:, :], dst[:, off:off + 512], dst[:, off:off + 512],
                        ALU.mult), waits=[("T", P.now("T"))])
                    VN = P.now("V")
                    t(lambda hh=hh: NCT.matmul(
                        pMISC[0:4, 0:512], eye4_bf[:, 4 * hh:4 * hh + 4], sq_tmp[:, :],
                        start=(hh == 0), stop=(hh == 3)), waits=[("V", VN)])
                    if hh == 3:
                        TN2 = P.now("T")
                        dstr = ssq_r if ci16 < 4 else ssk_r
                        v(lambda dstr=dstr: NCV.tensor_copy(
                            dstr[0:4, 0:512], pMISC[0:4, 0:512]),
                          waits=[("T", TN2)], nodrain=True)
            else:
                s(lambda: NCS.activation(tanh_sb[:, :], pA[:, :], AF.Tanh, scale=0.5),
                  waits=[("T", TN), ("V", P.now("V"))])
                SN = P.now("S")
                v(lambda: NCV.tensor_scalar(tanh_sb[:, :], tanh_sb[:, :], 0.5, 0.5,
                                            ALU.mult, ALU.add), waits=[("S", SN)])
                v(lambda hh=hh: NCV.tensor_tensor(z_buf[:, hh * 512:(hh + 1) * 512],
                                                  pA[:, :], tanh_sb[:, :], ALU.mult))
                pA_free = P.now("V")

        for Wsb, dstr in ((Wb_sb, bpre_r), (Wa_sb, gpre_r)):
            VN = P.now("V")
            for kk in range(16):
                t(lambda Wsb=Wsb, kk=kk, buf=buf: NCT.matmul(
                    pMISC[0:4, 0:512], Wsb[:, kk * 4:(kk + 1) * 4],
                    buf[:, kk * 512:(kk + 1) * 512],
                    start=(kk == 0), stop=(kk == 15)),
                  waits=[("V", VN)] if kk == 0 else [])
            TN = P.now("T")
            v(lambda dstr=dstr: NCV.tensor_copy(dstr[0:4, 0:512],
                                                     pMISC[0:4, 0:512]),
              waits=[("T", TN)], nodrain=True)
        win_mm_done[w] = P.now("T")

        wsl = slice(0, 512)
        VN = P.now("V")
        s(lambda wsl=wsl: NCS.activation(tmp_r[0:4, :], bpre_r[0:4, wsl], AF.Tanh,
                                         scale=0.5),
          waits=[("V", VN), ("T", P.now("T"))])
        SN = P.now("S")
        v(lambda wsl=wsl: NCV.tensor_scalar(beta_r[0:4, wsl], tmp_r[0:4, :], 0.5, 0.5,
                                            ALU.mult, ALU.add), waits=[("S", SN)])
        VN = P.now("V")
        s(lambda wsl=wsl: NCS.activation(tmp_r[0:4, :], beta_r[0:4, wsl], AF.Ln),
          waits=[("V", VN)])
        v(lambda wsl=wsl: NCV.tensor_scalar(tmp2_r[0:4, :], ssk_r[0:4, wsl], EPS,
                                            None, ALU.add))
        VNx = P.now("V")
        s(lambda: NCS.activation(tmp2_r[0:4, :], tmp2_r[0:4, :], AF.Ln),
          waits=[("V", VNx)])
        SN = P.now("S")
        v(lambda wsl=wsl: NCV.tensor_scalar(c2_r[0:4, wsl], tmp2_r[0:4, :], -0.5, None,
                                            ALU.mult), waits=[("S", SN)])
        v(lambda wsl=wsl: NCV.scalar_tensor_tensor(c1_r[0:4, wsl], tmp2_r[0:4, :],
                                                   -0.5, tmp_r[0:4, :], ALU.mult,
                                                   ALU.add))
        VN = P.now("V")
        v(lambda wsl=wsl: NCV.tensor_scalar(tmp2_r[0:4, :], ssq_r[0:4, wsl], EPS,
                                            None, ALU.add), waits=[("V", VN)])
        VNx = P.now("V")
        s(lambda: NCS.activation(tmp2_r[0:4, :], tmp2_r[0:4, :], AF.Ln),
          waits=[("V", VNx)])
        SN = P.now("S")
        v(lambda wsl=wsl: NCV.tensor_scalar(c3_r[0:4, wsl], tmp2_r[0:4, :], -0.5,
                                            -0.5 * float(np.log(128.0)), ALU.mult,
                                            ALU.add), waits=[("S", SN)])
        v(lambda wsl=wsl: NCV.tensor_scalar(tmp2_r[0:4, :], gpre_r[0:4, wsl],
                                            dtb_c[0:4, :], None, ALU.add))
        VN = P.now("V")
        s(lambda: NCS.activation(tmp_r[0:4, :], tmp2_r[0:4, :], AF.Abs),
          waits=[("V", VN)])
        s(lambda: NCS.activation(tmp_r[0:4, :], tmp_r[0:4, :], AF.Exp, scale=-1.0),
          drain=True)
        SNx = P.now("S")
        v(lambda: NCV.tensor_scalar(tmp_r[0:4, :], tmp_r[0:4, :], 1.0, None, ALU.add),
          waits=[("S", SNx)])
        VNx = P.now("V")
        s(lambda: NCS.activation(tmp_r[0:4, :], tmp_r[0:4, :], AF.Ln),
          waits=[("V", VNx)])
        SN = P.now("S")
        v(lambda: NCV.tensor_scalar(tmp2_r[0:4, :], tmp2_r[0:4, :], 0.0, None,
                                    ALU.max), waits=[("S", SN)])
        v(lambda: NCV.tensor_tensor(tmp_r[0:4, :], tmp_r[0:4, :], tmp2_r[0:4, :],
                                    ALU.add))
        v(lambda wsl=wsl: NCV.tensor_scalar(g_r[0:4, wsl], tmp_r[0:4, :],
                                            negexpA[0:4, :], None, ALU.mult),
          waits=[("V", V_PRE)])

        for ci in range(4):
            emit_chunk(w, 4 * w + ci, ci)

        VN = P.now("V")
        ys(lambda w=w: NCY.dma_start(
            ag2_in.rearrange("(k p) t -> p k t", p=128)[:, :, 512 * w:512 * (w + 1)],
            coreo[:, :].rearrange("p (k t) -> p k t", k=4)), waits=[("V", VN)])
        STN = P.now("ST")
        P.op("v", lambda: NCV.engine_nop(), [("ST", STN)], None)

    for w in range(NW):
        emit_window(w)

    STN = P.now("ST")
    cc(lambda: NCG.collective_compute(
        "AllGather", ALU.bypass, replica_groups=[[0, 1, 2, 3], [4, 5, 6, 7]],
        ins=[ag2_in.opt()], outs=[ag2_out.opt()]), waits=[("ST", STN)])
    CC_AG2 = P.now("CC")
    if RELAX < 2:
        P.CONS = P.CONS_FULL

    T_P2 = P.now("T")
    gd(lambda: NCG.dma_start(Wqkv_sb[:, :].rearrange("p (k c) -> p k c", k=16),
                             Wout_in.rearrange("(k p) c -> p k c", p=128)),
       waits=[("T", T_P2)])
    G_WOUT = P.now("G")
    gd(lambda: NCG.dma_start(w1T_sb[:, :], w1T_in))
    gd(lambda: NCG.dma_start(w2T_sb[:, :], w2T_in))
    gd(lambda: NCG.dma_start(b1T_sb[:, :], b1T_in))
    gd(lambda: NCG.dma_start(b2T_sb[:, :], b2T_in))
    gd(lambda: NCG.dma_start(swb_sb[:, :], swb_in))
    gd(lambda: NCG.dma_start(emask_sb[:, 0:8], emask0_in))
    gd(lambda: NCG.dma_start(emask_sb[:, 8:16], emask1_in))
    gd(lambda: NCG.dma_start(b3_sb[:, :], b3_in))
    G_W3X = P.now("G")
    ys(lambda: NCY.dma_start(offr_sb[:, :], offtok_in))
    OFF_LD = P.now("LD") + P.now("ST")
    ST_OFF = P.now("ST")
    hold = {}

    def _load_off():
        reg = NCY.alloc_register("offreg")
        NCY.reg_load(reg, offr_sb[0:1, 0:1])
        hold["off"] = NCY.snap(reg)
        return NCY.nop()
    P.op("y", _load_off, [("ST", ST_OFF)], None)

    def ln_pair_stats(src_getter, n_cc, dst_ps1, dst_ps2, first_waits):
        pass

    def emit_p3a_window(wi):
        ldn = P.now("LD")
        wl = [("CC", CC_AG2)]
        if wi > 0:
            wl.append(("ST", hold["p3a_st_%d" % (wi - 1)]))

        def ld_core(wi=wi):
            off = hold["off"]
            src = bass.AP(ag2_out, 0, [[S_, 128], [128 * S_, 16], [1, 512]],
                          ) if False else None
            view = ag2_out.rearrange("(k p) t -> p k t", p=128)
            return NCY.dma_start(core_w[:, :].rearrange("p (k t) -> p k t", k=16),
                                 view[:, :, bass.ds(off + 512 * wi, 512)])
        yd(ld_core, waits=wl)
        for dd in range(16):
            yd(lambda dd=dd, wi=wi: NCY.dma_start(
                hbf_w[:, 512 * dd:512 * (dd + 1)],
                hbf[512 * wi:512 * (wi + 1), 128 * dd:128 * (dd + 1)],
                transpose=True))
        LD_T = ldn + 16 * 17

        for ci16 in range(16):
            for kk in range(16):
                wl = []
                if kk == 0:
                    wl = [("LD", LD_T), ("G", G_WOUT),
                          ("V", hold.get("p3a_pa", 0))]
                t(lambda ci16=ci16, kk=kk: NCT.matmul(
                    pA[:, :],
                    Wqkv_sb[:, kk * 2048 + ci16 * 128:kk * 2048 + ci16 * 128 + 128],
                    core_w[:, 512 * kk:512 * (kk + 1)],
                    start=(kk == 0), stop=(kk == 15)), waits=wl)
            TN = P.now("T")
            evw = [("T", TN)]
            if ci16 == 0 and wi > 0:
                evw.append(("ST", hold["p3a_st_%d" % (wi - 1)]))
            v(lambda ci16=ci16: NCV.tensor_copy(attn_sb[:, 512 * ci16:512 * (ci16 + 1)],
                                                pA[:, :]), waits=evw, nodrain=True)
            VN = P.now("V")
            hold["p3a_pa"] = VN
            v(lambda ci16=ci16: NCV.tensor_tensor(
                asq_sb[:, :], attn_sb[:, 512 * ci16:512 * (ci16 + 1)],
                attn_sb[:, 512 * ci16:512 * (ci16 + 1)], ALU.mult),
              waits=[("T", P.now("T"))])
            VN2 = P.now("V")
            t(lambda ci16=ci16: NCT.matmul(
                pMISC[0:1, 0:512], onesc_bf[:, :], attn_sb[:, 512 * ci16:512 * (ci16 + 1)],
                start=(ci16 == 0), stop=(ci16 == 15)), waits=[("V", VN2)])
            t(lambda ci16=ci16: NCT.matmul(
                pBC[0:1, 0:512], onesc_bf[:, :], asq_sb[:, :],
                start=(ci16 == 0), stop=(ci16 == 15)))
        TN = P.now("T")

        def stats_rows(TN):
            v(lambda: NCV.tensor_scalar(murow[:, :], pMISC[0:1, 0:512], 1.0 / 2048,
                                        None, ALU.mult), waits=[("T", TN)])
            v(lambda: NCV.tensor_scalar(tmpsr[:, :], pBC[0:1, 0:512], 1.0 / 2048,
                                        None, ALU.mult))
            v(lambda: NCV.scalar_tensor_tensor(varow[:, :], murow[:, :], -1.0,
                                               murow[:, :], ALU.mult, ALU.mult))
            v(lambda: NCV.tensor_tensor(varow[:, :], tmpsr[:, :], varow[:, :],
                                        ALU.add))
            v(lambda: NCV.tensor_scalar(varow[:, :], varow[:, :], LNEPS, None,
                                        ALU.add))
            VN = P.now("V")
            s(lambda: NCS.activation(varow[:, :], varow[:, :], AF.Ln),
              waits=[("V", VN)])
            s(lambda: NCS.activation(scr_row[:, :], varow[:, :], AF.Exp, scale=-0.5),
              waits=[("V", P.now("V"))], drain=True)
            SNx = P.now("S")
            v(lambda: NCV.tensor_copy(r1row[:, :], scr_row[:, :]), waits=[("S", SNx)])
            SN = P.now("S")
            t(lambda: NCT.matmul(pMISC[:, 0:512], onesr_f[:, :], murow[:, :],
                                 start=True, stop=True), waits=[("S", SN)])
            t(lambda: NCT.matmul(pBC[:, 0:512], onesr_f[:, :], r1row[:, :],
                                 start=True, stop=True))
            return P.now("T")

        TN = stats_rows(TN)
        for ci16 in range(16):
            v(lambda ci16=ci16: NCV.tensor_tensor(
                x1_w[:, 512 * ci16:512 * (ci16 + 1)],
                attn_sb[:, 512 * ci16:512 * (ci16 + 1)], pMISC[:, 0:512],
                ALU.subtract), waits=[("T", TN)] if ci16 == 0 else [])
            v(lambda ci16=ci16: NCV.tensor_tensor(
                x1_w[:, 512 * ci16:512 * (ci16 + 1)],
                x1_w[:, 512 * ci16:512 * (ci16 + 1)], pBC[:, 0:512], ALU.mult))
            v(lambda ci16=ci16: NCV.tensor_tensor(
                x1_w[:, 512 * ci16:512 * (ci16 + 1)],
                x1_w[:, 512 * ci16:512 * (ci16 + 1)],
                hbf_w[:, 512 * ci16:512 * (ci16 + 1)], ALU.add))
        V_X1 = P.now("V")
        for ci16 in range(16):
            v(lambda ci16=ci16: NCV.tensor_copy(stat_bf[:, 0:512],
                                                x1_w[:, 512 * ci16:512 * (ci16 + 1)]),
              waits=[("T", P.now("T"))], nodrain=True)
            VN = P.now("V")
            v(lambda: NCV.tensor_tensor(stat_bf[:, 512:1024], stat_bf[:, 0:512],
                                        stat_bf[:, 0:512], ALU.mult))
            VN2 = P.now("V")
            t(lambda ci16=ci16: NCT.matmul(
                pMISC[0:1, 0:512], onesc_bf[:, :], stat_bf[:, 0:512],
                start=(ci16 == 0), stop=(ci16 == 15)), waits=[("V", VN2)])
            t(lambda ci16=ci16: NCT.matmul(
                pBC[0:1, 0:512], onesc_bf[:, :], stat_bf[:, 512:1024],
                start=(ci16 == 0), stop=(ci16 == 15)))
        TN = P.now("T")
        TN = stats_rows(TN)
        for ci16 in range(16):
            v(lambda ci16=ci16: NCV.tensor_tensor(
                ln2_w[:, 512 * ci16:512 * (ci16 + 1)],
                x1_w[:, 512 * ci16:512 * (ci16 + 1)], pMISC[:, 0:512], ALU.subtract),
              waits=[("T", TN)] if ci16 == 0 else [])
            v(lambda ci16=ci16: NCV.tensor_tensor(
                ln2_w[:, 512 * ci16:512 * (ci16 + 1)],
                ln2_w[:, 512 * ci16:512 * (ci16 + 1)], pBC[:, 0:512], ALU.mult))
        V_LN2 = P.now("V")
        ys(lambda wi=wi: NCY.dma_start(
            bass.AP(x1_sp.tensor, 512 * wi, [[NQ, 128], [128 * NQ, 16], [1, 512]]),
            x1_w[:, :].rearrange("p (k t) -> p k t", k=16)), waits=[("V", V_X1)])
        ys(lambda wi=wi: NCY.dma_start(
            bass.AP(ln2_sp.tensor, 512 * wi, [[NQ, 128], [128 * NQ, 16], [1, 512]]),
            ln2_w[:, :].rearrange("p (k t) -> p k t", k=16)), waits=[("V", V_LN2)])
        hold["p3a_st_%d" % wi] = P.now("ST")

    for wi in range(NW3):
        emit_p3a_window(wi)

    STN = P.now("ST")
    gd(lambda: NCG.dma_start(agr_in, ln2_sp[:, 0:RT]), waits=[("ST", STN)])
    GN = P.now("G")
    cc(lambda: NCG.collective_compute(
        "AllGather", ALU.bypass, replica_groups=[[0, 1, 2, 3], [4, 5, 6, 7]],
        ins=[agr_in.opt()], outs=[agr_out.opt()]), waits=[("G", GN)])
    CC_AGR = P.now("CC")

    for e in range(2):
        for cc16 in range(16):
            ldn = P.now("LD")
            swwl = [("V", P.now("V"))] if (e, cc16) >= (0, 2) else []
            if e == 0 and cc16 < 2:
                swwl = [("CC", CC_AGR)]
            gd(lambda e=e, cc16=cc16: NCG.dma_start(
                sw_sb[cc16 % 2][:, 0:RT],
                switch_in[128 * cc16:128 * (cc16 + 1), RT * e:RT * (e + 1)]),
               waits=swwl)
            GN = P.now("G")
            yd(lambda cc16=cc16: NCY.dma_start(
                rtl_sb[:, 0:RT], agr_out[128 * cc16:128 * (cc16 + 1), 0:RT]),
               waits=[("CC", CC_AGR)])
            LDN = P.now("LD")
            for tt4 in range(RT // 512):
                v(lambda cc16=cc16, tt4=tt4: NCV.tensor_tensor(
                    prod_sb[:, :], rtl_sb[:, 512 * tt4:512 * (tt4 + 1)],
                    sw_sb[cc16 % 2][:, 512 * tt4:512 * (tt4 + 1)], ALU.mult),
                  waits=([("G", GN), ("LD", LDN)] if tt4 == 0 else [])
                  + [("T", P.now("T"))])
                VN = P.now("V")
                t(lambda e=e, cc16=cc16, tt4=tt4: NCT.matmul(
                    pMISC[0:1, 0:512] if e == 0 else pBC[0:1, 0:512],
                    onesc_bf[:, :], prod_sb[:, :],
                    start=(cc16 == 0 and tt4 == 0),
                    stop=(cc16 == 15 and tt4 == RT // 512 - 1)),
                  waits=[("V", VN)])
    TN = P.now("T")
    v(lambda: NCV.tensor_reduce(lgsc[:, 0:1], pMISC[0:1, 0:512], AX.X, ALU.add),
      waits=[("T", TN)])
    v(lambda: NCV.tensor_reduce(lgsc[:, 1:2], pBC[0:1, 0:512], AX.X, ALU.add))
    v(lambda: NCV.tensor_copy(dbgk[:, 0:2], lgsc[:, 0:2]))
    v(lambda: NCV.tensor_scalar(lgrow[:, :], emask_sb[:, 0:8], lgsc[:, 0:1], None,
                                ALU.mult))
    v(lambda: NCV.scalar_tensor_tensor(lgrow[:, :], emask_sb[:, 8:16], lgsc[:, 1:2],
                                       lgrow[:, :], ALU.mult, ALU.add))
    VN = P.now("V")
    gd(lambda: NCG.dma_start(lgt_in, lgrow[:, :]), waits=[("V", VN)])
    GN = P.now("G")
    cc(lambda: NCG.collective_compute(
        "AllReduce", ALU.add, replica_groups=[[0, 1, 2, 3], [4, 5, 6, 7]],
        ins=[lgt_in.opt()], outs=[lgt_out.opt()]), waits=[("G", GN)])
    CC_LGT = P.now("CC")
    gd(lambda: NCG.dma_start(lgrow[:, :], lgt_out), waits=[("CC", CC_LGT)])
    G_LGT = P.now("G")
    v(lambda: NCV.tensor_tensor(lgrow[:, :], lgrow[:, :], swb_sb[:, :], ALU.add),
      waits=[("G", G_LGT)])
    v(lambda: NCV.tensor_copy(dbgk[:, 8:16], lgrow[:, :]))
    v(lambda: NCV.tensor_reduce(lgsc[:, 0:1], lgrow[:, :], AX.X, ALU.max))
    v(lambda: NCV.tensor_scalar(prob_sb[:, :], lgrow[:, :], lgsc[:, 0:1], None,
                                ALU.subtract))
    VN = P.now("V")
    s(lambda: NCS.activation(prob_sb[:, :], prob_sb[:, :], AF.Exp), waits=[("V", VN)])
    SN = P.now("S")
    v(lambda: NCV.tensor_reduce(lgsc[:, 1:2], prob_sb[:, :], AX.X, ALU.add),
      waits=[("S", SN)])
    v(lambda: NCV.reciprocal(lgsc[:, 2:3], lgsc[:, 1:2]))
    v(lambda: NCV.tensor_scalar(prob_sb[:, :], prob_sb[:, :], lgsc[:, 2:3], None,
                                ALU.mult))
    v(lambda: NCV.tensor_copy(lgrow[:, :], prob_sb[:, :]))
    VN = P.now("V")
    t(lambda: NCT.transpose(pMISC[0:8, 16:17], lgrow[:, :], ident_f[0:1, 0:1]),
      waits=[("V", VN)])
    TN = P.now("T")
    v(lambda: NCV.tensor_copy(probc_sb[:, :], pMISC[0:8, 16:17]), waits=[("T", TN)])
    VN = P.now("V")
    t(lambda: NCT.matmul(pMISC[0:8, 20:28], onesr_f[:, 0:8], prob_sb[:, :],
                         start=True, stop=True), waits=[("V", VN)])
    TN = P.now("T")
    v(lambda: NCV.tensor_scalar(cmp_sb[:, :], pMISC[0:8, 20:28], probc_sb[:, :], None,
                                ALU.is_gt), waits=[("T", TN)])
    VN = P.now("V")
    t(lambda: NCT.matmul(pMISC[0:1, 28:36], onesc_f[0:8, :],
                         cmp_sb[:, :], start=True, stop=True), waits=[("V", VN)])
    TN = P.now("T")
    v(lambda: NCV.tensor_scalar(selr_sb[:, 0:8], pMISC[0:1, 28:36], 7.0, None,
                                ALU.is_equal), waits=[("T", TN)])
    v(lambda: NCV.tensor_scalar(selr_sb[:, 8:16], pMISC[0:1, 28:36], 6.0, None,
                                ALU.is_equal))
    v(lambda: NCV.tensor_tensor(lgrow[:, :], prob_sb[:, :], selr_sb[:, 0:8], ALU.mult),
      waits=[("T", P.now("T"))])
    v(lambda: NCV.tensor_reduce(lgsc[:, 0:1], lgrow[:, :], AX.X, ALU.add))
    v(lambda: NCV.tensor_tensor(lgrow[:, :], prob_sb[:, :], selr_sb[:, 8:16], ALU.mult))
    v(lambda: NCV.tensor_reduce(lgsc[:, 1:2], lgrow[:, :], AX.X, ALU.add))
    VN = P.now("V")
    t(lambda: NCT.matmul(pMISC[:, 36:52], onesr_f[:, :], selr_sb[:, :],
                         start=True, stop=True), waits=[("V", VN)])
    t(lambda: NCT.matmul(pMISC[:, 52:54], onesr_f[:, :], lgsc[:, 0:2],
                         start=True, stop=True))
    TN = P.now("T")
    v(lambda: NCV.tensor_copy(selbc[:, :], pMISC[:, 36:52]), waits=[("T", TN)])
    v(lambda: NCV.tensor_copy(gcol_sb[:, :], pMISC[:, 52:54]))
    for name, wsrc, dst in (("w1", w1T_sb, w1sel), ("w2", w2T_sb, w2sel)):
        for e2 in range(2):
            v(lambda wsrc=wsrc, dst=dst, e2=e2: NCV.tensor_scalar(
                dst[:, 128 * e2:128 * (e2 + 1)], wsrc[:, 0:128],
                selbc[:, 8 * e2:8 * e2 + 1], None, ALU.mult),
              waits=[("G", G_W3X)])
            for ee in range(1, 8):
                v(lambda wsrc=wsrc, dst=dst, e2=e2, ee=ee: NCV.scalar_tensor_tensor(
                    dst[:, 128 * e2:128 * (e2 + 1)], wsrc[:, 128 * ee:128 * (ee + 1)],
                    selbc[:, 8 * e2 + ee:8 * e2 + ee + 1],
                    dst[:, 128 * e2:128 * (e2 + 1)], ALU.mult, ALU.add))
    for bsrc, dst in ((b1T_sb, b1sel), (b2T_sb, b2sel)):
        for e2 in range(2):
            v(lambda bsrc=bsrc, dst=dst, e2=e2: NCV.tensor_scalar(
                dst[:, e2:e2 + 1], bsrc[:, 0:1], selbc[:, 8 * e2:8 * e2 + 1], None,
                ALU.mult))
            for ee in range(1, 8):
                v(lambda bsrc=bsrc, dst=dst, e2=e2, ee=ee: NCV.scalar_tensor_tensor(
                    dst[:, e2:e2 + 1], bsrc[:, ee:ee + 1],
                    selbc[:, 8 * e2 + ee:8 * e2 + ee + 1], dst[:, e2:e2 + 1],
                    ALU.mult, ALU.add))
    v(lambda: NCV.tensor_scalar(b2c_sb[:, :], b2sel[:, 0:1], gcol_sb[:, 0:1], None,
                                ALU.mult))
    v(lambda: NCV.scalar_tensor_tensor(b2c_sb[:, :], b2sel[:, 1:2], gcol_sb[:, 1:2],
                                       b2c_sb[:, :], ALU.mult, ALU.add))
    V_SEL = P.now("V")

    T_P3A = P.now("T")
    gd(lambda: NCG.dma_start(Wqkv_sb[:, :].rearrange("p (k c) -> p k c", k=16),
                             w3T_in.rearrange("(k p) c -> p k c", p=128)),
       waits=[("T", T_P3A)])
    G_W3 = P.now("G")

    def emit_p3b_window(wi):
        ldn = P.now("LD")
        wl = [("V", V_SEL)]
        if wi > 0:
            wl.append(("ST", hold["p3b_st_%d" % (wi - 1)]))
        yd(lambda wi=wi: NCY.dma_start(
            ln2_w[:, :].rearrange("p (k t) -> p k t", k=16),
            bass.AP(ln2_sp.tensor, 512 * wi, [[NQ, 128], [128 * NQ, 16], [1, 512]])),
           waits=wl)
        yd(lambda wi=wi: NCY.dma_start(
            x1_w[:, :].rearrange("p (k t) -> p k t", k=16),
            bass.AP(x1_sp.tensor, 512 * wi, [[NQ, 128], [128 * NQ, 16], [1, 512]])))
        LD_T = ldn + 32
        for ci16 in range(16):
            for e2 in range(2):
                wl = [("LD", LD_T)] if (ci16 == 0 and e2 == 0) else []
                t(lambda ci16=ci16, e2=e2: NCT.matmul(
                    pA[:, :], w1sel[:, 128 * e2:128 * (e2 + 1)],
                    ln2_w[:, 512 * ci16:512 * (ci16 + 1)], start=True, stop=True),
                  waits=wl + [("V", P.now("V")), ("S", P.now("S"))])
                TN = P.now("T")
                s(lambda e2=e2: NCS.activation(h1_sb[:, :], pA[:, :], AF.Gelu,
                                               bias=b1sel[:, e2:e2 + 1]),
                  waits=[("T", TN), ("V", P.now("V"))])
                SN = P.now("S")
                v(lambda: NCV.tensor_copy(h1b_sb[:, :], h1_sb[:, :]),
                  waits=[("S", SN), ("T", P.now("T"))], nodrain=True)
                VN3 = P.now("V")
                t(lambda e2=e2: NCT.matmul(pBC[:, 0:512], w2sel[:, 128 * e2:128 * (e2 + 1)],
                                           h1b_sb[:, :], start=True, stop=True),
                  waits=[("V", VN3)])
                TN = P.now("T")
                if e2 == 0:
                    v(lambda: NCV.tensor_scalar(oacc[:, :], pBC[:, 0:512],
                                                gcol_sb[:, 0:1], None, ALU.mult),
                      waits=[("T", TN)])
                else:
                    v(lambda: NCV.scalar_tensor_tensor(oacc[:, :], pBC[:, 0:512],
                                                       gcol_sb[:, 1:2], oacc[:, :],
                                                       ALU.mult, ALU.add),
                      waits=[("T", TN)])
            v(lambda ci16=ci16: NCV.tensor_scalar(oacc[:, :], oacc[:, :],
                                                  b2c_sb[:, :], None, ALU.add))
            v(lambda ci16=ci16: NCV.tensor_tensor(
                w3in_w[:, 512 * ci16:512 * (ci16 + 1)], oacc[:, :],
                ln2_w[:, 512 * ci16:512 * (ci16 + 1)], ALU.add))
        V_W3IN = P.now("V")
        if DEBUG and wi == 0:
            dbg_w3in = nc.dram_tensor("dbg_w3in", [128, 8192], BF16,
                                      kind="ExternalOutput").ap()
            ys(lambda: NCY.dma_start(dbg_w3in, w3in_w[:, :]), waits=[("V", V_W3IN)])
        for ci16 in range(16):
            for kk in range(16):
                wl = [("V", max(V_W3IN, P.now("V"))), ("G", G_W3),
                      ("S", P.now("S"))] if kk == 0 else []
                t(lambda ci16=ci16, kk=kk: NCT.matmul(
                    pA[:, :],
                    Wqkv_sb[:, kk * 2048 + ci16 * 128:kk * 2048 + ci16 * 128 + 128],
                    w3in_w[:, 512 * kk:512 * (kk + 1)],
                    start=(kk == 0), stop=(kk == 15)), waits=wl)
            TN = P.now("T")
            v(lambda ci16=ci16: NCV.tensor_scalar(
                attn_sb[:, 512 * ci16:512 * (ci16 + 1)], pA[:, :],
                b3_sb[:, ci16:ci16 + 1], None, ALU.add), waits=[("T", TN)])
            VN = P.now("V")
            v(lambda ci16=ci16: NCV.tensor_tensor(
                asq_sb[:, :], attn_sb[:, 512 * ci16:512 * (ci16 + 1)],
                attn_sb[:, 512 * ci16:512 * (ci16 + 1)], ALU.mult),
              waits=[("T", P.now("T"))])
            VN2 = P.now("V")
            t(lambda ci16=ci16: NCT.matmul(
                pMISC[0:1, 0:512], onesc_bf[:, :],
                attn_sb[:, 512 * ci16:512 * (ci16 + 1)],
                start=(ci16 == 0), stop=(ci16 == 15)), waits=[("V", VN2)])
            t(lambda ci16=ci16: NCT.matmul(
                pBC[0:1, 0:512], onesc_bf[:, :], asq_sb[:, :],
                start=(ci16 == 0), stop=(ci16 == 15)))
        TN = P.now("T")
        v(lambda: NCV.tensor_scalar(murow[:, :], pMISC[0:1, 0:512], 1.0 / 2048, None,
                                    ALU.mult), waits=[("T", TN)])
        v(lambda: NCV.tensor_scalar(tmpsr[:, :], pBC[0:1, 0:512], 1.0 / 2048, None,
                                    ALU.mult))
        v(lambda: NCV.scalar_tensor_tensor(varow[:, :], murow[:, :], -1.0,
                                           murow[:, :], ALU.mult, ALU.mult))
        v(lambda: NCV.tensor_tensor(varow[:, :], tmpsr[:, :], varow[:, :], ALU.add))
        v(lambda: NCV.tensor_scalar(varow[:, :], varow[:, :], LNEPS, None, ALU.add))
        if DEBUG and wi == 0:
            dbg_lnr = nc.dram_tensor("dbg_lnr", [1, 1536], F32,
                                     kind="ExternalOutput").ap()
            VNR = P.now("V")
            ys(lambda: NCY.dma_start(dbg_lnr[:, 0:512], murow[:, :]), waits=[("V", VNR)])
            ys(lambda: NCY.dma_start(dbg_lnr[:, 512:1024], tmpsr[:, :]), waits=[("V", VNR)])
            ys(lambda: NCY.dma_start(dbg_lnr[:, 1024:1536], varow[:, :]), waits=[("V", VNR)])
        VN = P.now("V")
        s(lambda: NCS.activation(varow[:, :], varow[:, :], AF.Ln),
          waits=[("V", VN)])
        s(lambda: NCS.activation(scr_row[:, :], varow[:, :], AF.Exp, scale=-0.5),
          waits=[("V", P.now("V"))], drain=True)
        SNx = P.now("S")
        v(lambda: NCV.tensor_copy(r1row[:, :], scr_row[:, :]), waits=[("S", SNx)])
        SN = P.now("S")
        t(lambda: NCT.matmul(pMISC[:, 0:512], onesr_f[:, :], murow[:, :],
                             start=True, stop=True), waits=[("S", SN)])
        t(lambda: NCT.matmul(pBC[:, 0:512], onesr_f[:, :], r1row[:, :],
                             start=True, stop=True))
        TN = P.now("T")
        if DEBUG and wi == 0:
            dbg_ffn = nc.dram_tensor("dbg_ffn", [128, 8192], BF16,
                                     kind="ExternalOutput").ap()
            VNF = P.now("V")
            ys(lambda: NCY.dma_start(dbg_ffn, attn_sb[:, :]), waits=[("V", VNF)])
        for ci16 in range(16):
            v(lambda ci16=ci16: NCV.tensor_tensor(
                x2_sb[:, :], attn_sb[:, 512 * ci16:512 * (ci16 + 1)], pMISC[:, 0:512],
                ALU.subtract), waits=[("T", TN)] if ci16 == 0 else [])
            v(lambda ci16=ci16: NCV.tensor_tensor(x2_sb[:, :], x2_sb[:, :],
                                                  pBC[:, 0:512], ALU.mult))
            v(lambda ci16=ci16: NCV.tensor_tensor(
                x1_w[:, 512 * ci16:512 * (ci16 + 1)], x2_sb[:, :],
                x1_w[:, 512 * ci16:512 * (ci16 + 1)], ALU.add))
        V_X2 = P.now("V")
        for tt4 in range(4):
            STW = hold.get("p3b_ost_%d_%d" % (wi, tt4 - 1), hold.get("p3b_ost_%d_3" % (wi - 1), 0))
            for ci16 in range(16):
                t(lambda ci16=ci16, tt4=tt4: NCT.transpose(
                    pY[:, 0:128], x1_w[:, 512 * ci16 + 128 * tt4:512 * ci16 + 128 * (tt4 + 1)],
                    ident_f[:, :]),
                  waits=[("V", V_X2 if tt4 == 0 else P.now("V"))] if ci16 == 0
                  else [("V", P.now("V"))])
                TN2 = P.now("T")
                v(lambda ci16=ci16: NCV.tensor_copy(
                    outb[:, 128 * ci16:128 * (ci16 + 1)], pY[:, 0:128]),
                  waits=[("T", TN2), ("ST", STW)] if ci16 == 0 else [("T", TN2)], nodrain=True)
            VN2 = P.now("V")
            ys(lambda wi=wi, tt4=tt4: NCY.dma_start(
                out_t[512 * wi + 128 * tt4:512 * wi + 128 * (tt4 + 1), :],
                outb[:, :]), waits=[("V", VN2)])
            hold["p3b_ost_%d_%d" % (wi, tt4)] = P.now("ST")
        hold["p3b_st_%d" % wi] = P.now("ST")

    for wi in range(NW3):
        emit_p3b_window(wi)

    if DEBUG:
        dbg_ag1 = nc.dram_tensor("dbg_ag1", [2048, NQ], BF16, kind="ExternalOutput").ap()
        dbg_ag2 = nc.dram_tensor("dbg_ag2", [512, S_], BF16, kind="ExternalOutput").ap()
        dbg_x1 = nc.dram_tensor("dbg_x1", [2048, NQ], F32, kind="ExternalOutput").ap()
        dbg_ln2 = nc.dram_tensor("dbg_ln2", [2048, NQ], BF16, kind="ExternalOutput").ap()
        dbg_sc = nc.dram_tensor("dbg_sc", [1, 64], F32, kind="ExternalOutput").ap()
        STN0 = P.now("ST")
        gd(lambda: NCG.dma_start(dbg_ag1, ag1_in), waits=[("ST", STN0), ("CC", P.now("CC"))])
        gd(lambda: NCG.dma_start(dbg_ag2, ag2_in))
        gd(lambda: NCG.dma_start(dbg_x1, x1_sp))
        gd(lambda: NCG.dma_start(dbg_ln2, ln2_sp))
        dbg_agr = nc.dram_tensor("dbg_agr", [2048, RT], BF16, kind="ExternalOutput").ap()
        gd(lambda: NCG.dma_start(dbg_agr, agr_out[0:2048, :]))
        VND = P.now("V")
        v(lambda: NCV.tensor_copy(murow[:, 0:8], prob_sb[:, :]))
        v(lambda: NCV.tensor_copy(murow[:, 8:24], selr_sb[:, :]))
        v(lambda: NCV.tensor_copy(murow[:, 24:28], lgsc[:, :]))
        v(lambda: NCV.tensor_copy(murow[:, 32:64], dbgk[:, :]))
        VND = P.now("V")
        gd(lambda: NCG.dma_start(dbg_sc, murow[:, 0:64]), waits=[("V", VND)])
        GDN = P.now("G")
        P.op("g", lambda: NCG.engine_nop(), [("G", GDN)], None)

    STN = P.now("ST")
    P.op("y", lambda: NCY.nop(), [("ST", STN)], None)

    P.replay()
    es.close()
    return nc


_BUILD_CACHE = {}


def kernel(**inputs):
    S_ = int(inputs["L0_attn_hidden_states"].shape[1])
    NQ = S_ // 4
    maps = make_in_maps(inputs, S_)
    if S_ not in _BUILD_CACHE:
        _BUILD_CACHE[S_] = build(S_)
    nc = _BUILD_CACHE[S_]
    res = run_bass_kernel_spmd(nc, maps, core_ids=list(range(8)))
    out = np.empty((B, S_, D), np.float32)
    for core in range(8):
        b = core // 4
        j = core % 4
        out[b, j * NQ:(j + 1) * NQ, :] = res.results[core]["out"]
    return out
